# version 21
# baseline (speedup 1.0000x reference)
import contextlib
import numpy as np
import ml_dtypes
import concourse.bass as bass
import concourse.mybir as mybir
from concourse.bass_utils import run_bass_kernel_spmd

F32 = mybir.dt.float32
BF16 = mybir.dt.bfloat16
I32 = mybir.dt.int32
AF = mybir.ActivationFunctionType
ALU = mybir.AluOpType
AX = mybir.AxisListType

B, S, HID = 4, 2048, 2048
H, HKV, D = 16, 4, 128
IFF = 8192
NCO = 16
NIC = 64
TOK = 1024
EPS = 1e-5
INV_SQRT_D = 1.0 / float(np.float32(np.sqrt(np.float32(128.0))))
FWHT_NORM = float(np.float32(1.0 / np.sqrt(8192.0)))
MAGIC = 12582912.0
MAGIC64 = 6755399441055744.0
INV_N_GU = 1.0 / float(IFF * HID)

MASKED = [(0, kt) for kt in range(8)] + [(1, kt) for kt in range(8, 16)]
MASK_IDX = {qk: i for i, qk in enumerate(MASKED)}
KT_OF_QB = {0: 8, 1: 16}
_DEBUG = False
_PHASES = 99
_P6SUB = 7


class _StopBuild(Exception):
    pass


def _rope_tables():
    import jax
    import jax.numpy as jnp
    cpu = jax.devices("cpu")[0]
    with jax.default_device(cpu):
        inv = 1.0 / (1000000.0 ** (jnp.arange(0, D, 2, dtype=jnp.float32) / D))
        t = jnp.arange(S, dtype=jnp.float32)
        fr = jnp.outer(t, inv)
        cos = np.asarray(jnp.concatenate([jnp.cos(fr), jnp.cos(fr)], axis=-1))
        sin = np.asarray(jnp.concatenate([jnp.sin(fr), jnp.sin(fr)], axis=-1))
    return cos.T.copy(), sin.T.copy()


def _stage_stripes(w, n_oc):
    out_dim, in_dim = w.shape
    nhc = in_dim // 128
    st = np.empty((n_oc, 128, in_dim), dtype=np.float32)
    for oc in range(n_oc):
        s = np.ascontiguousarray(w[oc * 128:(oc + 1) * 128, :].T)
        st[oc] = s.reshape(nhc, 128, 128).transpose(1, 0, 2).reshape(128,
                                                                     in_dim)
    return st


class Pr:
    ENG = {"PE": "tensor", "ACT": "scalar", "DVE": "vector", "POOL": "gpsimd",
           "SP": "sync"}
    QUEUES = {"SP": "QSP", "POOL": "QPOOL"}

    def __init__(self, nc, es):
        self.nc = nc
        self.es = es
        self.block = es.enter_context(nc.Block())
        self.sem = {}
        self.cnt = {}
        for name in ["PE", "ACT", "DVE", "POOL"]:
            self.sem[name] = es.enter_context(nc.semaphore(name.lower()))
            self.cnt[name] = 0
        self.waited = {}
        self.lastw = {}
        self.readers = {}
        self.cur = None

    def qsem(self, name):
        if name not in self.sem:
            self.sem[name] = self.es.enter_context(
                self.nc.semaphore(name.lower()))
            self.cnt[name] = 0
        return name

    def section(self, eng_name, fn):
        def body(e):
            self.cur = (eng_name, e)
            fn(e)
            self.cur = None
        getattr(self.block, self.ENG[eng_name])(body)

    def _wait(self, sem_name, val):
        if val <= 0:
            return
        key = (self.cur[0], sem_name)
        if self.waited.get(key, 0) >= val:
            return
        self.cur[1].wait_ge(self.sem[sem_name], val)
        self.waited[key] = val

    def _need(self, reads, writes):
        for k in reads:
            w = self.lastw.get(k)
            if w:
                self._wait(*w)
        for k in writes:
            w = self.lastw.get(k)
            if w:
                self._wait(*w)
            for r in self.readers.get(k, ()):
                self._wait(*r)

    def _did(self, sem_name, reads, writes):
        c = self.cnt[sem_name]
        for k in writes:
            self.lastw[k] = (sem_name, c)
            self.readers[k] = []
        for k in reads:
            self.readers.setdefault(k, []).append((sem_name, c))

    def do(self, build, reads=(), writes=()):
        eng_name, e = self.cur
        self._need(reads, writes)
        inst = build(e)
        self.cnt[eng_name] += 1
        inst.then_inc(self.sem[eng_name], 1)
        self._did(eng_name, reads, writes)
        return inst

    def do_group(self, builds, reads=(), writes=()):
        eng_name, e = self.cur
        self._need(reads, writes)
        inst = None
        for b in builds:
            inst = b(e)
        self.cnt[eng_name] += 1
        inst.then_inc(self.sem[eng_name], 1)
        self._did(eng_name, reads, writes)
        return inst

    def dma(self, out, in_, reads=(), writes=(), q="q_dflt"):
        eng_name, e = self.cur
        self.qsem(q)
        self._need(reads, writes)
        inst = e.dma_start(out=out, in_=in_)
        self.cnt[q] += 16
        inst.then_inc(self.sem[q], 16)
        c = self.cnt[q]
        for k in writes:
            self.lastw[k] = (q, c)
            self.readers[k] = []
        for k in reads:
            self.readers.setdefault(k, []).append((q, c))
        return inst

    def barrier(self):
        snap = dict(self.cnt)
        for en in ["PE", "ACT", "DVE", "POOL", "SP"]:
            def fn(e, en=en):
                for sn, v in snap.items():
                    if sn != en and v > 0:
                        self._wait(sn, v)
            self.section(en, fn)


def build_program():
    nc = bass.Bass("TRN2", target_bir_lowering=False)

    def din(name, shape, dtype=F32):
        return nc.dram_tensor(name, shape, dtype, kind="ExternalInput").ap()

    xq_h = din("xq", [HID, TOK])
    xk_h = din("xk", [HID, S])
    cosq_h = din("cosq", [D, TOK]); sinq_h = din("sinq", [D, TOK])
    cosk_h = din("cosk", [D, S]); sink_h = din("sink", [D, S])
    mask_h = din("mask01", [16, 128, 512], BF16)
    qng_h = din("qn_g", [D, 1]); qnb_h = din("qn_b", [D, 1])
    kng_h = din("kn_g", [D, 1]); knb_h = din("kn_b", [D, 1])
    wqkv_h = din("wqkv", [24, 128, HID])
    wo_h = din("wo", [16, 128, HID])
    wg_h = din("wg", [NIC, 128, HID])
    wu_h = din("wu", [NIC, 128, HID])
    wd_h = din("wd", [NIC, 128, HID])
    y_h = nc.dram_tensor("y", [HID, TOK], F32, kind="ExternalOutput").ap()

    z_dram = nc.dram_tensor("z_scr", [NIC, 128, TOK], BF16, kind="Internal").ap()
    yv_dram = nc.dram_tensor("yv_scr", [NIC, 128, TOK], BF16,
                             kind="Internal").ap()
    aq2_dram = nc.dram_tensor("aq2_scr", [NIC, 128, TOK], BF16,
                              kind="Internal").ap()
    trd_dram = nc.dram_tensor("trd_scr", [NIC, 128, HID], BF16,
                              kind="Internal").ap()
    x1_dram = nc.dram_tensor("x1_scr", [NCO, 128, TOK], F32,
                             kind="Internal").ap()

    ident_bf = nc.inline_tensor(np.eye(128, dtype=ml_dtypes.bfloat16), "id_bf")
    ident_f = nc.inline_tensor(np.eye(128, dtype=np.float32), "id_f")
    ones_col_t = nc.inline_tensor(np.ones((128, 1), dtype=np.float32), "ones_c")
    ones_row_t = nc.inline_tensor(np.ones((1, 128), dtype=np.float32), "ones_r")
    ones_all_t = nc.inline_tensor(np.ones((128, 128), dtype=np.float32),
                                  "ones_a")
    eps_col_t = nc.inline_tensor(np.full((128, 1), EPS, dtype=np.float32),
                                 "eps_c")
    srt_t = nc.inline_tensor(np.full((1, 1), np.sqrt(np.float32(128.0)),
                                     dtype=np.float32), "srt_c")
    hmat = np.ones((1, 1), dtype=np.float64)
    while hmat.shape[0] < 128:
        hmat = np.block([[hmat, hmat], [hmat, -hmat]])
    h64 = hmat[:64, :64]
    h64blk = np.zeros((128, 128), dtype=np.float64)
    h64blk[:64, :64] = h64; h64blk[64:, 64:] = h64
    h128_t = nc.inline_tensor(hmat.astype(ml_dtypes.bfloat16), "h128")
    h64b_t = nc.inline_tensor(h64blk.astype(ml_dtypes.bfloat16), "h64b")

    es = contextlib.ExitStack()
    try:
      with es:
        pr = Pr(nc, es)
        scopes = {}
        tapn = [0]

        def tap(name, ap, key):
            if not _DEBUG:
                return
            t = nc.dram_tensor("tap_" + name, list(ap.shape), ap.dtype,
                               kind="Internal").ap()
            tapn[0] += 1
            pr.section("POOL", lambda e: pr.dma(t, ap, reads=[key],
                                                q="qtap%d" % tapn[0]))

        def sbin(scope, name, shape, dt=F32):
            st = scopes.setdefault(scope, contextlib.ExitStack())
            return st.enter_context(nc.sbuf_tensor(name, shape, dt))

        def close_scope(scope):
            if scope in scopes:
                scopes.pop(scope).close()

        sb = lambda n, s, d=F32: sbin("all", n, s, d)

        qng = sb("qng", [128, 1]); qnb = sb("qnb", [128, 1])
        kng = sb("kng", [128, 1]); knb = sb("knb", [128, 1])
        s_id_bf = sb("s_id_bf", [128, 128], BF16)
        s_id_f = sb("s_id_f", [128, 128], F32)
        s_ones_c = sb("s_ones_c", [128, 1], F32)
        s_ones_r = sb("s_ones_r", [1, 128], F32)
        s_eps = sb("s_eps", [128, 1], F32)
        s_h128 = sb("s_h128", [128, 128], BF16)
        s_h64b = sb("s_h64b", [128, 128], BF16)
        onescol_bf = sb("onescol_bf", [128, 1], BF16)
        s_ones_all = sb("s_ones_all", [128, 128], F32)
        rows = sb("rows", [128, 4096], F32)
        _RP = {0: 32, 1: 0, 2: 0, 3: 32, 4: 64, 5: 64, 6: 0, 7: 32,
               8: 0, 9: 0, 10: 32, 11: 0, 12: 32, 13: 64, 14: 32, 15: 64}
        _RC = {0: 0, 1: 0, 2: 512, 3: 512, 4: 0, 5: 1024, 6: 1024,
               7: 1024, 8: 2048, 9: 2560, 10: 2048, 11: 3072, 12: 3072,
               13: 3072, 14: 512, 15: 512}
        cols = sb("cols", [128, 32], F32)
        mean_acc = sb("mean_acc", [128, 4], F32)
        scrA = sb("scrA", [128, 1024], F32)
        scrB = sb("scrB", [128, 1024], F32)
        scrC = sb("scrC", [128, 1024], F32)
        scrD = sb("scrD", [128, 1024], F32)
        b1b = sb("b1b", [128, 1024], F32)
        b2b = sb("b2b", [128, 1024], F32)
        wstr = [sb(f"wstr{i}", [128, HID], F32) for i in range(2)]
        wstr_bf = [sb(f"wstrbf{i}", [128, HID], BF16) for i in range(2)]
        bigA = sb("bigA", [128, NCO * TOK], BF16)
        bigB = sb("bigB", [128, NCO * TOK], BF16)

        def rowv(i, n=1024):
            p = _RP[i]
            return rows[p:p + 1, _RC[i]:_RC[i] + n]

        def colv(i):
            return cols[:, i:i + 1]

        def p0_sp(e):
            pr.dma(qng[:, :], qng_h, writes=["qng"], q="qc_qng")
            pr.dma(qnb[:, :], qnb_h, writes=["qnb"], q="qc_qnb")
            pr.dma(kng[:, :], kng_h, writes=["kng"], q="qc_kng")
            pr.dma(knb[:, :], knb_h, writes=["knb"], q="qc_knb")
            pr.dma(s_id_bf[:, :], ident_bf.ap(), writes=["idbf"], q="qc_idbf")
            pr.dma(s_id_f[:, :], ident_f.ap(), writes=["idf"], q="qc_idf")
            pr.dma(s_ones_c[:, :], ones_col_t.ap(), writes=["onesc"], q="qc_onesc")
            pr.dma(s_ones_r[:, :], ones_row_t.ap(), writes=["onesr"], q="qc_onesr")
            pr.dma(s_ones_all[:, :], ones_all_t.ap(), writes=["onesall"],
                   q="qc_onesall")
            pr.dma(s_eps[:, :], eps_col_t.ap(), writes=["s_eps"], q="qc_s_eps")
            pr.dma(s_h128[:, :], h128_t.ap(), writes=["h128"], q="qc_h128")
            pr.dma(s_h64b[:, :], h64b_t.ap(), writes=["h64b"], q="qc_h64b")
        pr.section("SP", p0_sp)

        def p0_dve(e):
            pr.do(lambda e2: e2.memset(mean_acc[:, :], 0.0),
                  writes=["mean_acc"])
            pr.do(lambda e2: e2.memset(cols[:, :], 0.0), writes=["cols"])
            pr.do(lambda e2: e2.tensor_copy(onescol_bf[:, :], s_ones_c[:, :]),
                  reads=["onesc"], writes=["onescbf"])
        pr.section("DVE", p0_dve)

        mslot = [None, None]
        mean_jobs = []
        for mi, wh in enumerate((wg_h, wu_h, wd_h)):
            for oc in range(NIC):
                for hf in range(2):
                    mean_jobs.append((wh[oc][:, hf * 1024:hf * 1024 + 1024],
                                      mi))
        mean_state = {"dma": 0, "cmp": 0}

        def mean_emit(n):
            j0 = mean_state["cmp"]
            j1 = min(j0 + n, len(mean_jobs))
            if j0 >= j1:
                return
            def emit_dmas(bound):
                while mean_state["dma"] <= min(bound, len(mean_jobs) - 1):
                    jd = mean_state["dma"]
                    srcd, _m = mean_jobs[jd]
                    sld = jd % 2
                    pr.dma(mslot[sld][:, :], srcd, writes=[f"mslot{sld}"],
                           q=f"qm{sld}")
                    mean_state["dma"] += 1
            for j in range(j0, j1):
                src_, mi = mean_jobs[j]
                sl = j % 2
                ck = f"mcol{sl}"
                def act_fn(e, j=j, sl=sl, ck=ck):
                    emit_dmas(j + 1)
                    pr.do(lambda e2: e2.activation(
                        mslot[sl][:, :], mslot[sl][:, :], AF.Abs,
                        accum_out=colv(8 + sl)),
                        reads=[f"mslot{sl}"], writes=[f"mslot{sl}", ck])
                    emit_dmas(j + 2)
                pr.section("ACT", act_fn)
                pr.section("DVE", lambda e, mi=mi, sl=sl, ck=ck: pr.do(
                    lambda e2: e2.tensor_tensor(
                        mean_acc[:, mi:mi + 1], mean_acc[:, mi:mi + 1],
                        colv(8 + sl), ALU.add),
                    reads=[ck, "mean_acc"], writes=["mean_acc"]))
            mean_state["cmp"] = j1

        psum_es = [None, 0]

        def psum_phase(spec):
            if psum_es[0] is not None:
                psum_es[0].close()
            psum_es[0] = contextlib.ExitStack()
            psum_es[1] += 1
            out = {}
            for name, shape, dt in spec:
                out[name] = psum_es[0].enter_context(
                    nc.psum_tensor(f"{name}_{psum_es[1]}", shape, dt))
            return out

        ppd = {}

        def stats2048(tag, chunk_ap, T0, T, src_key, bf_src=False,
                      chunk_load=None):
            kS, kS2 = f"{tag}S", f"{tag}S2"
            ones_ap = onescol_bf[:, :] if bf_src else s_ones_c[:, :]
            ones_key = "onescbf" if bf_src else "onesc"
            for c in range(NCO):
                if chunk_load is not None:
                    chunk_load(c)
                skey = src_key(c) if callable(src_key) else src_key
                pr.section("PE", lambda e, c=c, skey=skey: pr.do_group(
                    [lambda e2: e2.matmul(
                        ppd["pp_stat"][0:1, 0:T], ones_ap, chunk_ap(c),
                        start=(c == 0), stop=(c == NCO - 1))],
                    reads=[skey, ones_key], writes=[kS, "ppstat"]))
                sl = c % 2
                scr = scrA[:, sl * 512:sl * 512 + 512][:, 0:T]
                pr.section("DVE", lambda e, c=c, scr=scr, skey=skey: pr.do(
                    lambda e2: e2.tensor_tensor(scr, chunk_ap(c), chunk_ap(c),
                                                ALU.mult),
                    reads=[skey], writes=["scrA"]))
                pr.section("PE", lambda e, c=c, scr=scr: pr.do_group(
                    [lambda e2: e2.matmul(ppd["pp_stat"][64:65, 0:T],
                                          s_ones_c[:, :], scr,
                                          start=(c == 0),
                                          stop=(c == NCO - 1))],
                    reads=["scrA", "onesc"], writes=[kS2, "ppstat"]))
            def dve_rows(e):
                pr.do(lambda e2: e2.tensor_scalar(
                    rowv(0, T), ppd["pp_stat"][0:1, 0:T], 1.0 / 2048.0, None,
                    ALU.mult), reads=[kS], writes=["hr0"])
                pr.do(lambda e2: e2.tensor_scalar(
                    rowv(1, T), ppd["pp_stat"][64:65, 0:T], 1.0 / 2048.0,
                    None, ALU.mult), reads=[kS2], writes=["hr0"])
                pr.do(lambda e2: e2.tensor_tensor(
                    rowv(2, T), rowv(0, T), rowv(0, T), ALU.mult),
                    reads=["hr0"], writes=["hr0"])
                pr.do(lambda e2: e2.tensor_tensor(
                    rowv(1, T), rowv(1, T), rowv(2, T), ALU.subtract),
                    reads=["hr0"], writes=["hr0"])
            pr.section("DVE", dve_rows)
            pr.section("ACT", lambda e: pr.do(
                lambda e2: e2.activation(rowv(2, T), rowv(1, T), AF.Sqrt,
                                         bias=s_eps[0:1, 0:1]),
                reads=["hr0", "s_eps"], writes=["hr0"]))
            def dve_rows2(e):
                pr.do(lambda e2: e2.reciprocal(rowv(3, T), rowv(2, T)),
                      reads=["hr0"], writes=["hr0"])
                pr.do(lambda e2: e2.tensor_tensor(
                    rowv(4, T), rowv(0, T), rowv(3, T), ALU.mult),
                    reads=["hr0"], writes=["hr0"])
            pr.section("DVE", dve_rows2)
            bcast_row(3, 0, b1b[:, T0:T0 + T], "b1b", T, row_key="hr0")
            bcast_row(4, 0, b2b[:, T0:T0 + T], "b2b", T, row_key="hr0")

        def bcast_row(ri, col0, dst_ap, dst_key, T, row_key="rows"):
            p = _RP[ri]
            src = rows[p:p + 1, _RC[ri] + col0:_RC[ri] + col0 + T]
            pr.section("PE", lambda e: pr.do(
                lambda e2: e2.matmul(ppd["pp_b"][:, 0:T],
                                     s_ones_all[p:p + 1, :], src,
                                     start=True, stop=True),
                reads=[row_key, "onesall"], writes=["ppb"]))
            pr.section("DVE", lambda e: pr.do(
                lambda e2: e2.tensor_copy(dst_ap, ppd["pp_b"][:, 0:T]),
                reads=["ppb"], writes=[dst_key]))

        def round_exact(dst_ap, dst_key, v_ap, v_key, T, dbg=None):
            kf = scrKI[:, 0:T].bitcast(F32)
            def dve1(e):
                pr.do(lambda e2: e2.tensor_scalar(
                    scrC[:, 0:T], v_ap, MAGIC, MAGIC, ALU.add, ALU.subtract),
                    reads=[v_key], writes=["scrC"])
                pr.do(lambda e2: e2.tensor_copy(kf, scrC[:, 0:T]),
                      reads=["scrC"], writes=["scrKI"])
                pr.do(lambda e2: e2.scalar_tensor_tensor(
                    scrC[:, 0:T], kf, -1.0, v_ap, ALU.mult, ALU.add),
                    reads=["scrKI", v_key], writes=["scrC"])
            pr.section("DVE", dve1)
            if dbg:
                tap(dbg + "k0", kf, "scrKI")
                tap(dbg + "d", scrC[:, 0:T], "scrC")
            def dve3(e):
                pr.do(lambda e2: e2.tensor_scalar(
                    scrC[:, 0:T], scrC[:, 0:T], MAGIC64, MAGIC64,
                    ALU.add, ALU.subtract),
                    reads=["scrC"], writes=["scrC"])
                pr.do(lambda e2: e2.scalar_tensor_tensor(
                    scrC[:, 0:T], scrC[:, 0:T], 1.0, kf, ALU.mult, ALU.add),
                    reads=["scrC", "scrKI"], writes=["scrC"])
                pr.do(lambda e2: e2.tensor_scalar(
                    dst_ap, scrC[:, 0:T], -128.0, 127.0, ALU.max, ALU.min),
                    reads=["scrC"], writes=[dst_key])
            pr.section("DVE", dve3)

        def load_stripe(sl, src_ap, n=HID):
            pr.section("SP", lambda e: pr.dma(wstr[sl][:, 0:n], src_ap,
                                              writes=[f"wsf{sl}"],
                                              q=f"qw{sl}"))
            pr.section("DVE", lambda e: pr.do(
                lambda e2: e2.tensor_copy(wstr_bf[sl][:, 0:n],
                                          wstr[sl][:, 0:n]),
                reads=[f"wsf{sl}"], writes=[f"wsb{sl}"]))

        def ln_from_dram(tag, dram_ap, col0, dst, xg):
            for g in range(4):
                T0g = g * 256
                kx = "xgrp"
                def sp(e, g=g, kx=kx):
                    for c in range(NCO):
                        pr.dma(xg[c][:, :],
                               dram_ap[c * 128:(c + 1) * 128,
                                       col0 + g * 256: col0 + g * 256 + 256],
                               writes=[kx], q="qxg")
                pr.section("SP", sp)
                stats2048(tag + str(g), lambda c: xg[c][:, :],
                          0, 256, kx)
                def dve_apply(e, g=g, kx=kx):
                    for c in range(NCO):
                        pr.do(lambda e2, c=c: e2.tensor_tensor(
                            xg[c][:, :], xg[c][:, :], b1b[:, 0:256],
                            ALU.mult),
                            reads=["b1b", kx], writes=[kx])
                        pr.do(lambda e2, c=c, g=g: e2.tensor_tensor(
                            dst[:, c * TOK + g * 256: c * TOK + g * 256 + 256],
                            xg[c][:, :], b2b[:, 0:256], ALU.subtract),
                            reads=["b2b", kx], writes=["lnb"])
                pr.section("DVE", dve_apply)

        HLR = ((0, 1, 2, 3, 4), (10, 11, 6, 12, 13))

        def head_ln_rope(tag, src, src_key, T, gcol, gbkeys, bcol, cosT, sinT,
                         ckey, dst_ap, dst_key, par=0):
            R = HLR[par]
            rk = f"hr{par}"
            pst = "ppstat" if par == 0 else "ppstat1"
            psn = "pp_stat" if par == 0 else "pp_st2"
            pb = 0
            for g in range(T // 512):
                sl512 = g * 512
                sq = sqbuf[:, par * 512:par * 512 + 512]
                sqk = f"sq{par}"
                pr.section("DVE", lambda e, sl512=sl512, sq=sq, sqk=sqk:
                           pr.do(lambda e2: e2.tensor_tensor(
                               sq, src[:, sl512:sl512 + 512],
                               src[:, sl512:sl512 + 512], ALU.mult),
                               reads=[src_key], writes=[sqk]))
                pr.section("PE", lambda e, sl512=sl512, sq=sq, sqk=sqk: (
                    pr.do(lambda e2: e2.matmul(
                        ppd[psn][pb:pb + 1, 0:512], s_ones_c[:, :],
                        src[:, sl512:sl512 + 512], start=True, stop=True),
                        reads=[src_key, "onesc"], writes=[pst]),
                    pr.do(lambda e2: e2.matmul(
                        ppd[psn][pb + 64:pb + 65, 0:512],
                        s_ones_c[:, :], sq, start=True, stop=True),
                        reads=[sqk, "onesc"], writes=[pst])))
                def rows_fn(e):
                    pr.do(lambda e2: e2.tensor_scalar(
                        rowv(R[0], 512), ppd[psn][pb:pb + 1, 0:512],
                        1.0 / 128.0, None, ALU.mult),
                        reads=[pst], writes=[rk])
                    pr.do(lambda e2: e2.tensor_scalar(
                        rowv(R[1], 512), ppd[psn][pb + 64:pb + 65, 0:512],
                        1.0 / 128.0, None, ALU.mult),
                        reads=[pst], writes=[rk])
                    pr.do(lambda e2: e2.tensor_tensor(
                        rowv(R[2], 512), rowv(R[0], 512), rowv(R[0], 512),
                        ALU.mult), reads=[rk], writes=[rk])
                    pr.do(lambda e2: e2.tensor_tensor(
                        rowv(R[1], 512), rowv(R[1], 512), rowv(R[2], 512),
                        ALU.subtract), reads=[rk], writes=[rk])
                pr.section("DVE", rows_fn)
                pr.section("ACT", lambda e: pr.do(
                    lambda e2: e2.activation(rowv(R[2], 512), rowv(R[1], 512),
                                             AF.Sqrt, bias=s_eps[0:1, 0:1]),
                    reads=[rk, "s_eps"], writes=[rk]))
                def rows2_fn(e):
                    pr.do(lambda e2: e2.reciprocal(rowv(R[3], 512),
                                                   rowv(R[2], 512)),
                          reads=[rk], writes=[rk])
                    pr.do(lambda e2: e2.tensor_tensor(
                        rowv(R[4], 512), rowv(R[0], 512), rowv(R[3], 512),
                        ALU.mult), reads=[rk], writes=[rk])
                pr.section("DVE", rows2_fn)
                bcast_row(R[3], 0, b1b[:, sl512:sl512 + 512], "b1b", 512,
                          row_key=rk)
                bcast_row(R[4], 0, b2b[:, sl512:sl512 + 512], "b2b", 512,
                          row_key=rk)
            def apply_fn(e):
                pr.do(lambda e2: e2.tensor_tensor(
                    scrB[:, 0:T], src[:, 0:T], b1b[:, 0:T], ALU.mult),
                    reads=[src_key, "b1b"], writes=["scrB"])
                pr.do(lambda e2: e2.tensor_tensor(
                    scrB[:, 0:T], scrB[:, 0:T], b2b[:, 0:T], ALU.subtract),
                    reads=["scrB", "b2b"], writes=["scrB"])
            pr.section("DVE", apply_fn)
            pr.section("DVE", lambda e: pr.do(
                lambda e2: e2.tensor_scalar(scrB[:, 0:T], scrB[:, 0:T],
                                            gcol[:, 0:1], bcol[:, 0:1],
                                            ALU.mult, ALU.add),
                reads=["scrB"] + list(gbkeys), writes=["scrB"]))
            def rope_fn(e):
                pr.do(lambda e2: e2.tensor_tensor(
                    scrC[:, 0:T], scrB[:, 0:T], cosT, ALU.mult),
                    reads=["scrB", ckey], writes=["scrC"])
                pr.do(lambda e2: e2.tensor_copy(
                    scrD[0:64, 0:T], scrB[64:128, 0:T]),
                    reads=["scrB"], writes=["scrD"])
                pr.do(lambda e2: e2.tensor_copy(
                    scrD[64:128, 0:T], scrB[0:64, 0:T]),
                    reads=["scrB"], writes=["scrD"])
                pr.do(lambda e2: e2.tensor_tensor(
                    scrD[:, 0:T], scrD[:, 0:T], sinT, ALU.mult),
                    reads=["scrD", ckey], writes=["scrD"])
                pr.do(lambda e2: e2.tensor_tensor(
                    dst_ap, scrC[:, 0:T], scrD[:, 0:T], ALU.add),
                    reads=["scrC", "scrD"], writes=[dst_key])
            pr.section("DVE", rope_fn)

        ppd = psum_phase([("pp_stat", [128, 512], F32),
                          ("pp_b", [128, 512], F32),
                          ("pp_a0", [128, 512], F32),
                          ("pp_a1", [128, 512], F32),
                          ("pp_a2", [128, 512], F32),
                          ("pp_a3", [128, 512], F32),
                          ("pp_st2", [128, 512], F32),
                          ("pp_tr", [128, 128], BF16)])
        kpr = sbin("p12", "kpr", [128, HKV * S], BF16)
        vT = sbin("p12", "vT", [128, HKV * 16 * 128], BF16)
        mslot[0] = sbin("p02", "mean_t0", [128, 1024], F32)
        mslot[1] = sbin("p02", "mean_t1", [128, 1024], F32)
        qpr = bigB
        lnb = bigA
        xg = [sbin("p1", f"xg{i}", [128, 256], F32) for i in range(NCO)]
        vbf = sbin("p1", "vbf", [128, 1024], BF16)
        scrA2 = sbin("p1", "scrA2", [128, 1024], F32)
        sqbuf = sbin("p1", "sqbuf", [128, 1024], F32)
        scrAp = [scrA, scrA2]
        scrApk = ["scrA", "scrA2"]

        def qkv_mm(oc, sl, rhs, rhs_key):
            pq = (oc % 2) * 2
            def mm(e):
                for th in range(2):
                    pk = f"pp_a{pq + th}"
                    pr.do_group(
                        [(lambda e2, hc=hc, th=th, pk=pk:
                          e2.matmul(ppd[pk][:, 0:512],
                                    wstr_bf[sl][:, hc * 128:(hc + 1) * 128],
                                    rhs[:, hc * TOK + th * 512:
                                        hc * TOK + th * 512 + 512],
                                    start=(hc == 0), stop=(hc == NCO - 1)))
                         for hc in range(NCO)],
                        reads=[f"wsb{sl}", rhs_key], writes=[pk])
            pr.section("PE", mm)

        def evac_psum(dst, dst_key, oc):
            pq = (oc % 2) * 2
            def ev(e):
                for th in range(2):
                    pr.do(lambda e2, th=th: e2.tensor_copy(
                        dst[:, th * 512:th * 512 + 512],
                        ppd[f"pp_a{pq + th}"][:, 0:512]),
                        reads=[f"pp_a{pq + th}"], writes=[dst_key])
            pr.section("DVE", ev)

        cosq = sbin("p1q", "s_cosq", [128, TOK], F32)
        sinq = sbin("p1q", "s_sinq", [128, TOK], F32)
        pr.section("SP", lambda e: (
            pr.dma(cosq[:, :], cosq_h, writes=["cosq"], q="qcos"),
            pr.dma(sinq[:, :], sinq_h, writes=["cosq"], q="qcos")))
        ln_from_dram("l1q", xq_h, 0, lnb, xg)
        tap("ln1q", lnb[:, :], "lnb")
        load_stripe(0, wqkv_h[0])
        qkv_mm(0, 0, lnb, "lnb")
        load_stripe(1, wqkv_h[1])
        qkv_mm(1, 1, lnb, "lnb")
        for oc in range(16):
            par = oc % 2
            evac_psum(scrAp[par][:, 0:1024], scrApk[par], oc)
            if oc == 0:
                tap("qraw0", scrAp[par][:, 0:1024], scrApk[par])
            if oc + 2 < 16:
                load_stripe(par, wqkv_h[oc + 2])
                qkv_mm(oc + 2, par, lnb, "lnb")
            head_ln_rope(f"q{oc}", scrAp[par][:, 0:1024], scrApk[par], TOK,
                         qng, ("qng", "qnb"), qnb, cosq[:, 0:TOK],
                         sinq[:, 0:TOK], "cosq",
                         qpr[:, oc * TOK:(oc + 1) * TOK], "qpr", par=par)
        close_scope("p1q")
        pr.barrier()

        cosk = sbin("p1k", "s_cosk", [128, TOK], F32)
        sink = sbin("p1k", "s_sink", [128, TOK], F32)
        for half in range(2):
            def ckload(e, half=half):
                pr.dma(cosk[:, :], cosk_h[:, half * 1024:half * 1024 + 1024],
                       writes=["cosk"], q="qcos")
                pr.dma(sink[:, :], sink_h[:, half * 1024:half * 1024 + 1024],
                       writes=["cosk"], q="qcos")
            pr.section("SP", ckload)
            ln_from_dram(f"l1k{half}", xk_h, half * 1024, lnb, xg)
            load_stripe(0, wqkv_h[16])
            qkv_mm(16, 0, lnb, "lnb")
            load_stripe(1, wqkv_h[17])
            qkv_mm(17, 1, lnb, "lnb")
            for oc in range(16, 24):
                kv = oc - 16 if oc < 20 else oc - 20
                is_k = oc < 20
                par = oc % 2
                evac_psum(scrAp[par][:, 0:1024] if is_k else vbf,
                          scrApk[par] if is_k else "vbf", oc)
                if oc + 2 < 24:
                    load_stripe(par, wqkv_h[oc + 2])
                    qkv_mm(oc + 2, par, lnb, "lnb")
                if is_k:
                    head_ln_rope(f"k{kv}h{half}", scrAp[par][:, 0:1024],
                                 scrApk[par], 1024, kng, ("kng", "knb"), knb,
                                 cosk[:, 0:1024], sink[:, 0:1024],
                                 "cosk",
                                 kpr[:, kv * S + half * 1024:
                                     kv * S + half * 1024 + 1024], "kpr",
                                 par=par)
                else:
                    for tb in range(8):
                        kt = half * 8 + tb
                        pr.section("PE", lambda e, tb=tb: pr.do(
                            lambda e2: e2.transpose(
                                ppd["pp_tr"][:, 0:128],
                                vbf[:, tb * 128:(tb + 1) * 128],
                                s_id_bf[:, :]),
                            reads=["vbf", "idbf"], writes=["pptr"]))
                        pr.section("DVE", lambda e, kv=kv, kt=kt: pr.do(
                            lambda e2: e2.tensor_copy(
                                vT[:, (kv * 16 + kt) * 128:
                                   (kv * 16 + kt) * 128 + 128],
                                ppd["pp_tr"][:, 0:128]),
                            reads=["pptr"], writes=["vT"]))
        tap("qpr", qpr[:, :], "qpr")
        tap("kpr", kpr[:, :], "kpr")
        tap("vT", vT[:, :], "vT")
        close_scope("p1k")
        close_scope("p1")
        pr.barrier()
        if _PHASES <= 1:
            raise _StopBuild

        ppd = psum_phase([("pp_s0", [128, 512], F32),
                          ("pp_s1", [128, 512], F32),
                          ("pp_s2", [128, 512], F32),
                          ("pp_s3", [128, 512], F32),
                          ("pp_v0", [128, 512], F32),
                          ("pp_v1", [128, 512], F32),
                          ("pp_sum", [128, 512], F32),
                          ("pp_b", [128, 512], F32)])
        attn = bigA
        maskt = sbin("p2", "maskt", [128, 16 * 512], BF16)
        cbSs = [sbin("p2", f"cbS{i}", [128, 512], F32) for i in range(2)]
        rbSs = [sbin("p2", f"rbS{i}", [128, 512], F32) for i in range(2)]
        scfp = [sbin("p2", f"scfp{i}", [128, 1024], F32) for i in range(2)]
        ptp2 = [sbin("p2", f"ptp{i}", [128, 1024], BF16) for i in range(2)]
        def mload(e):
            for i in range(16):
                pr.dma(maskt[:, i * 512:(i + 1) * 512], mask_h[i],
                       writes=["maskt"], q="qmask")
        pr.section("SP", mload)
        srtS = sbin("p2", "srtS", [1, 1], F32)
        invd = sbin("p2", "invd", [128, 1], F32)
        pr.section("SP", lambda e: pr.dma(srtS[:, :], srt_t.ap(),
                                          writes=["srtS"], q="qc_srt"))
        pr.section("DVE", lambda e: pr.do(
            lambda e2: e2.reciprocal(rowv(8, 1), srtS[:, :]),
            reads=["srtS"], writes=["rows"]))
        bcast_row(8, 0, invd[:, 0:1], "invd", 1)

        for kv in range(HKV):
            for gp in range(2):
                pr.section("DVE", lambda e, kv=kv, gp=gp: pr.do(
                    lambda e2: e2.tensor_tensor(
                        scrA[:, 0:1024],
                        kpr[:, kv * S + gp * 1024:kv * S + gp * 1024 + 1024],
                        kpr[:, kv * S + gp * 1024:kv * S + gp * 1024 + 1024],
                        ALU.mult),
                    reads=["kpr"], writes=["scrA"]))
                pr.section("PE", lambda e, gp=gp: pr.do_group(
                    [(lambda e2, g=g: e2.matmul(
                        ppd["pp_sum"][g * 32:g * 32 + 1, 0:512],
                        s_ones_c[:, :],
                        scrA[:, g * 512:g * 512 + 512],
                        start=True, stop=True)) for g in range(2)],
                    reads=["scrA", "onesc"], writes=["ppsum"]))
                def nk_rows(e, gp=gp):
                    for g in range(2):
                        pr.do(lambda e2, g=g, gp=gp: e2.tensor_copy(
                            rowv(5, 2048)[0:1, (gp * 2 + g) * 512:
                                          (gp * 2 + g) * 512 + 512],
                            ppd["pp_sum"][g * 32:g * 32 + 1, 0:512]),
                            reads=["ppsum"], writes=["rows"])
                pr.section("DVE", nk_rows)
            pr.section("ACT", lambda e: pr.do(
                lambda e2: e2.activation(rowv(5, 2048), rowv(5, 2048),
                                         AF.Sqrt),
                reads=["rows"], writes=["rows"]))
            pr.section("DVE", lambda e, kv=kv: pr.do(
                lambda e2: e2.reduce_max(
                    rowv(14, 1024)[0:1, kv:kv + 1],
                    rowv(5, 2048), axis=AX.X),
                reads=["rows"], writes=["rows"]))

        for hh in range(H):
            kv = hh // 4
            pr.section("DVE", lambda e, hh=hh: pr.do(
                lambda e2: e2.tensor_tensor(
                    scrA[:, 0:TOK], qpr[:, hh * TOK:(hh + 1) * TOK],
                    qpr[:, hh * TOK:(hh + 1) * TOK], ALU.mult),
                reads=["qpr"], writes=["scrA"]))
            pr.section("PE", lambda e: pr.do_group(
                [(lambda e2, g=g: e2.matmul(
                    ppd["pp_sum"][g * 32:g * 32 + 1, 0:512], s_ones_c[:, :],
                    scrA[:, g * 512:g * 512 + 512], start=True, stop=True))
                 for g in range(2)],
                reads=["scrA", "onesc"], writes=["ppsum"]))
            def nq_rows(e):
                for g in range(2):
                    pr.do(lambda e2, g=g: e2.tensor_copy(
                        rowv(6, TOK)[0:1, g * 512:g * 512 + 512],
                        ppd["pp_sum"][g * 32:g * 32 + 1, 0:512]),
                        reads=["ppsum"], writes=["rows"])
            pr.section("DVE", nq_rows)
            pr.section("ACT", lambda e: pr.do(
                lambda e2: e2.activation(rowv(6, TOK), rowv(6, TOK), AF.Sqrt),
                reads=["rows"], writes=["rows"]))
            pr.section("DVE", lambda e, kv=kv: pr.do(
                lambda e2: e2.tensor_scalar(
                    rowv(7, TOK), rowv(6, TOK),
                    rowv(14, 1024)[0:1, kv:kv + 1],
                    INV_SQRT_D, ALU.mult, ALU.mult),
                reads=["rows", "rows"], writes=["rows"]))
            for qb in range(2):
                cbi = (hh * 2 + qb) % 2
                cb = cbSs[cbi]
                cbk = f"cbS{cbi}"
                rb = rbSs[cbi]
                rbk = f"rbS{cbi}"
                bcast_row(7, qb * 512, cb[:, :], cbk, 512)
                nkt = KT_OF_QB[qb]
                pvk = f"pp_v{qb}"
                def sc_mm(kt, hh=hh, kv=kv, qb=qb):
                    sps = f"pp_s{kt % 4}"
                    pr.section("PE", lambda e, kt=kt, sps=sps: pr.do(
                        lambda e2: e2.matmul(
                            ppd[sps][:, 0:512],
                            kpr[:, kv * S + kt * 128:kv * S + kt * 128 + 128],
                            qpr[:, hh * TOK + qb * 512:
                                hh * TOK + qb * 512 + 512],
                            start=True, stop=True),
                        reads=["kpr", "qpr"], writes=[sps]))
                sc_mm(0)
                sc_mm(1)
                for kp in range(nkt // 2):
                    k0 = 2 * kp
                    pb = kp % 2
                    sfp = scfp[pb]
                    sfpk = f"scfp{pb}"
                    ptp = ptp2[pb]
                    ptpk = f"ptp{pb}"
                    if k0 + 2 < nkt:
                        sc_mm(k0 + 2)
                        sc_mm(k0 + 3)
                    def stt2(e, sfp=sfp, sfpk=sfpk, cb=cb, cbk=cbk, k0=k0):
                        for j in range(2):
                            sps = f"pp_s{(k0 + j) % 4}"
                            pr.do(lambda e2, j=j, sps=sps:
                                  e2.scalar_tensor_tensor(
                                      sfp[:, j * 512:j * 512 + 512],
                                      ppd[sps][:, 0:512], invd[:, 0:1],
                                      cb[:, :], ALU.mult, ALU.subtract),
                                  reads=[sps, cbk, "invd"], writes=[sfpk])
                    pr.section("DVE", stt2)
                    pr.section("ACT", lambda e, sfp=sfp, sfpk=sfpk, ptp=ptp,
                               ptpk=ptpk: pr.do(
                        lambda e2: e2.activation(ptp[:, :], sfp[:, :],
                                                 AF.Exp),
                        reads=[sfpk], writes=[ptpk]))
                    if (qb, k0) in MASK_IDX:
                        mi0 = MASK_IDX[(qb, k0)]
                        pr.section("DVE", lambda e, ptp=ptp, ptpk=ptpk,
                                   mi0=mi0: pr.do(
                            lambda e2: e2.tensor_tensor(
                                ptp[:, :], ptp[:, :],
                                maskt[:, mi0 * 512:mi0 * 512 + 1024],
                                ALU.mult),
                            reads=["maskt", ptpk], writes=[ptpk]))
                    def pe_pv2(e, k0=k0, kv=kv, ptp=ptp, ptpk=ptpk, pvk=pvk,
                               nkt=nkt, qb=qb):
                        for j in range(2):
                            kt = k0 + j
                            pr.do(lambda e2, j=j, kt=kt: e2.matmul(
                                ppd["pp_sum"][qb * 64:qb * 64 + 1, 0:512],
                                onescol_bf[:, :],
                                ptp[:, j * 512:j * 512 + 512],
                                start=(kt == 0), stop=(kt == nkt - 1)),
                                reads=[ptpk, "onescbf"], writes=["ppsum"])
                            pr.do(lambda e2, j=j, kt=kt: e2.matmul(
                                ppd[pvk][:, 0:512],
                                vT[:, (kv * 16 + kt) * 128:
                                   (kv * 16 + kt) * 128 + 128],
                                ptp[:, j * 512:j * 512 + 512],
                                start=(kt == 0), stop=(kt == nkt - 1)),
                                reads=[ptpk, "vT"], writes=[pvk])
                    pr.section("PE", pe_pv2)
                def fin(e, qb=qb):
                    pr.do(lambda e2: e2.tensor_copy(
                        rowv(8, 512),
                        ppd["pp_sum"][qb * 64:qb * 64 + 1, 0:512]),
                        reads=["ppsum"], writes=["rows"])
                    pr.do(lambda e2: e2.reciprocal(rowv(9, 512), rowv(8, 512)),
                          reads=["rows"], writes=["rows"])
                pr.section("DVE", fin)
                bcast_row(9, 0, rb[:, :], rbk, 512)
                pr.section("DVE", lambda e, hh=hh, qb=qb, pvk=pvk, rb=rb,
                           rbk=rbk: pr.do(
                    lambda e2: e2.tensor_tensor(
                        attn[:, hh * TOK + qb * 512:
                             hh * TOK + qb * 512 + 512],
                        ppd[pvk][:, 0:512], rb[:, :], ALU.mult),
                    reads=[pvk, rbk], writes=["attn"]))
                mean_emit(12)
        mean_emit(len(mean_jobs))
        tap("attn", attn[:, :], "attn")
        close_scope("p2")
        close_scope("p02")
        close_scope("p12")
        pr.barrier()
        if _PHASES <= 2:
            raise _StopBuild

        ppd = psum_phase([(f"pp_w{i}", [128, 512], F32) for i in range(6)]
                         + [("pp_stat", [128, 512], F32),
                            ("pp_b", [128, 512], F32)])
        scrKI = sbin("rest", "scrKI", [128, 1024], I32)
        xst = [sbin("rest", f"xst{i}", [128, TOK], F32) for i in range(2)]
        wo_out = bigB

        for og in range(6):
            ocs = list(range(og * 3, min(og * 3 + 3, NCO)))
            if not ocs:
                continue
            for dc in range(NCO):
                sl = dc % 2
                def spf(e, dc=dc, sl=sl, ocs=ocs):
                    pr.dma(wstr[sl][:, 0:len(ocs) * 128],
                           wo_h[dc][:, ocs[0] * 128:(ocs[-1] + 1) * 128],
                           writes=[f"wsf{sl}"], q=f"qw{sl}")
                pr.section("SP", spf)
                pr.section("DVE", lambda e, sl=sl, n=len(ocs): pr.do(
                    lambda e2: e2.tensor_copy(wstr_bf[sl][:, 0:n * 128],
                                              wstr[sl][:, 0:n * 128]),
                    reads=[f"wsf{sl}"], writes=[f"wsb{sl}"]))
                def mmw(e, dc=dc, sl=sl, ocs=ocs):
                    builds = []
                    for j in range(len(ocs)):
                        for th in range(2):
                            builds.append(
                                (lambda e2, j=j, th=th, dc=dc, sl=sl:
                                 e2.matmul(
                                     ppd[f"pp_w{j * 2 + th}"][:, 0:512],
                                     wstr_bf[sl][:, j * 128:(j + 1) * 128],
                                     attn[:, dc * TOK + th * 512:
                                          dc * TOK + th * 512 + 512],
                                     start=(dc == 0), stop=(dc == NCO - 1))))
                    pr.do_group(builds, reads=[f"wsb{sl}", "attn"],
                                writes=["ppw"])
                pr.section("PE", mmw)
            def evacw(e, ocs=ocs):
                for j, oc in enumerate(ocs):
                    for th in range(2):
                        pr.do(lambda e2, j=j, oc=oc, th=th: e2.tensor_copy(
                            wo_out[:, oc * TOK + th * 512:
                                   oc * TOK + th * 512 + 512],
                            ppd[f"pp_w{j * 2 + th}"][:, 0:512]),
                            reads=["ppw"], writes=["wo_out"])
            pr.section("DVE", evacw)

        tap("wo_out", wo_out[:, :], "wo_out")
        for g in range(2):
            stats2048("ln2g%d" % g,
                      lambda c, g=g: wo_out[:, c * TOK + g * 512:
                                            c * TOK + g * 512 + 512],
                      g * 512, 512, "wo_out", bf_src=True)
        for c in range(NCO):
            sl = c % 2
            pr.section("SP", lambda e, c=c, sl=sl: pr.dma(
                xst[sl][:, :], xq_h[c * 128:(c + 1) * 128, :],
                writes=[f"xst{sl}"], q=f"qx{sl}"))
            def ln2_apply(e, c=c, sl=sl):
                pr.do(lambda e2: e2.tensor_tensor(
                    scrB[:, 0:TOK], wo_out[:, c * TOK:(c + 1) * TOK],
                    b1b[:, 0:TOK], ALU.mult),
                    reads=["wo_out", "b1b"], writes=["scrB"])
                pr.do(lambda e2: e2.tensor_tensor(
                    scrB[:, 0:TOK], scrB[:, 0:TOK], b2b[:, 0:TOK],
                    ALU.subtract),
                    reads=["scrB", "b2b"], writes=["scrB"])
                pr.do(lambda e2: e2.tensor_tensor(
                    xst[sl][:, :], xst[sl][:, :], scrB[:, 0:TOK], ALU.add),
                    reads=["scrB", f"xst{sl}"], writes=[f"xst{sl}"])
            pr.section("DVE", ln2_apply)
            pr.section("POOL", lambda e, c=c, sl=sl: pr.dma(
                x1_dram[c], xst[sl][:, :], reads=[f"xst{sl}"],
                writes=[f"x1d{c}"], q=f"qsx{sl}"))
        pr.barrier()
        if _PHASES <= 3:
            raise _StopBuild

        ppd = psum_phase([("pp_stat", [128, 512], F32),
                          ("pp_b", [128, 512], F32),
                          ("pp_t", [128, 512], F32)])
        xqi = bigB
        bgB = sbin("p45", "bgB", [128, TOK], F32)
        buB = sbin("p45", "buB", [128, TOK], F32)
        s1B = sbin("p4s", "s1B", [128, TOK], F32)

        def mrow_ap(mi):
            return rowv(15, 1024)[0:1, mi:mi + 1]

        pr.section("PE", lambda e: pr.do_group(
            [(lambda e2, mi=mi: e2.matmul(
                ppd["pp_stat"][mi * 32:mi * 32 + 1, 0:1], s_ones_c[:, :],
                mean_acc[:, mi:mi + 1], start=True, stop=True))
             for mi in range(3)],
            reads=["mean_acc", "onesc"], writes=["ppstat"]))
        def mrows(e):
            for mi in range(3):
                pr.do(lambda e2, mi=mi: e2.tensor_scalar(
                    mrow_ap(mi), ppd["pp_stat"][mi * 32:mi * 32 + 1, 0:1],
                    INV_N_GU, 1e-5, ALU.mult, ALU.max),
                    reads=["ppstat"], writes=["rows"])
            for mi in range(3):
                pr.do(lambda e2, mi=mi: e2.tensor_scalar(
                    rowv(15, 1024)[0:1, 4 + mi:5 + mi],
                    mrow_ap(mi), 0.5, None, ALU.mult),
                    reads=["rows"], writes=["rows"])
        pr.section("DVE", mrows)
        for mi in range(3):
            bcast_row(15, 4 + mi, colv(mi), "cols", 1)
        pr.section("DVE", lambda e: pr.do(
            lambda e2: e2.tensor_scalar(cols[:, 4:7], cols[:, 0:3], -1.0,
                                        None, ALU.mult),
            reads=["cols"], writes=["cols"]))
        tap("cols", cols[:, 0:8], "cols")
        tap("mean_acc", mean_acc[:, :], "mean_acc")

        for g in range(2):
            def l3load(c, g=g):
                sl = c % 2
                pr.section("SP", lambda e, c=c, sl=sl, g=g: pr.dma(
                    xst[sl][:, 0:512],
                    x1_dram[c][:, g * 512:g * 512 + 512],
                    reads=[f"x1d{c}"], writes=[f"xst{sl}"], q=f"qx{sl}"))
            stats2048("ln3g%d" % g,
                      lambda c: xst[c % 2][:, 0:512],
                      g * 512, 512, lambda c: f"xst{c % 2}",
                      chunk_load=l3load)
        def amax_init(e):
            pr.do(lambda e2: e2.memset(scrD[:, 0:TOK], -3.0e38),
                  writes=["scrD"])
            pr.do(lambda e2: e2.memset(scrC[:, 0:TOK], 3.0e38),
                  writes=["scrC"])
        pr.section("DVE", amax_init)
        for c in range(NCO):
            sl = c % 2
            pr.section("SP", lambda e, c=c, sl=sl: pr.dma(
                xst[sl][:, :], x1_dram[c], reads=[f"x1d{c}"],
                writes=[f"xst{sl}"], q=f"qx{sl}"))
            def ln3_chunks(e, c=c, sl=sl):
                pr.do(lambda e2: e2.tensor_tensor(
                    scrB[:, 0:TOK], xst[sl][:, :],
                    b1b[:, 0:TOK], ALU.mult),
                    reads=[f"xst{sl}", "b1b"], writes=["scrB"])
                pr.do(lambda e2: e2.tensor_tensor(
                    scrB[:, 0:TOK], scrB[:, 0:TOK],
                    b2b[:, 0:TOK], ALU.subtract),
                    reads=["scrB", "b2b"], writes=["scrB"])
                pr.do(lambda e2: e2.tensor_tensor(
                    scrD[:, 0:TOK], scrD[:, 0:TOK],
                    scrB[:, 0:TOK], ALU.max),
                    reads=["scrB", "scrD"], writes=["scrD"])
                pr.do(lambda e2: e2.tensor_tensor(
                    scrC[:, 0:TOK], scrC[:, 0:TOK],
                    scrB[:, 0:TOK], ALU.min),
                    reads=["scrB", "scrC"], writes=["scrC"])
            pr.section("DVE", ln3_chunks)

        def tokmax(src_key, dst_row_i):
            for tb in range(8):
                pr.section("PE", lambda e, tb=tb: pr.do(
                    lambda e2: e2.transpose(
                        ppd["pp_t"][:, 0:128],
                        scrD[:, tb * 128:(tb + 1) * 128], s_id_f[:, :]),
                    reads=[src_key, "idf"], writes=["ppt"]))
                pr.section("DVE", lambda e: pr.do(
                    lambda e2: e2.reduce_max(colv(10), ppd["pp_t"][:, 0:128],
                                             axis=AX.X),
                    reads=["ppt"], writes=["cols"]))
                pr.section("PE", lambda e: pr.do(
                    lambda e2: e2.transpose(ppd["pp_t"][0:1, 128:256],
                                            colv(10), s_id_f[:, :]),
                    reads=["cols", "idf"], writes=["ppt"]))
                pr.section("DVE", lambda e, tb=tb, dst_row_i=dst_row_i: pr.do(
                    lambda e2: e2.tensor_copy(
                        rowv(dst_row_i, TOK)[0:1, tb * 128:(tb + 1) * 128],
                        ppd["pp_t"][0:1, 128:256]),
                    reads=["ppt"], writes=["rows"]))
        def amax_comb(e):
            pr.do(lambda e2: e2.tensor_scalar(
                scrC[:, 0:TOK], scrC[:, 0:TOK], -1.0, None, ALU.mult),
                reads=["scrC"], writes=["scrC"])
            pr.do(lambda e2: e2.tensor_tensor(
                scrD[:, 0:TOK], scrD[:, 0:TOK], scrC[:, 0:TOK], ALU.max),
                reads=["scrC", "scrD"], writes=["scrD"])
        pr.section("DVE", amax_comb)
        tokmax("scrD", 10)
        def s1_rows(e):
            pr.do(lambda e2: e2.tensor_scalar(rowv(10, TOK), rowv(10, TOK),
                                              1e-5, None, ALU.max),
                  reads=["rows"], writes=["rows"])
            pr.do(lambda e2: e2.reciprocal(rowv(11, TOK), rowv(10, TOK)),
                  reads=["rows"], writes=["rows"])
            pr.do(lambda e2: e2.tensor_scalar(rowv(11, TOK), rowv(11, TOK),
                                              128.0, None, ALU.mult),
                  reads=["rows"], writes=["rows"])
            pr.do(lambda e2: e2.tensor_scalar(rowv(12, TOK), rowv(10, TOK),
                                              mrow_ap(0), 1.0 / 128.0,
                                              ALU.mult, ALU.mult),
                  reads=["rows", "rows"], writes=["rows"])
            pr.do(lambda e2: e2.tensor_scalar(rowv(13, TOK), rowv(10, TOK),
                                              mrow_ap(1), 1.0 / 128.0,
                                              ALU.mult, ALU.mult),
                  reads=["rows", "rows"], writes=["rows"])
        pr.section("DVE", s1_rows)
        for ri, dst, dkey in ((11, s1B, "s1B"), (12, bgB, "bgB"),
                              (13, buB, "buB")):
            for g in range(2):
                bcast_row(ri, g * 512, dst[:, g * 512:g * 512 + 512],
                          dkey, 512)
        for c in range(NCO):
            sl = c % 2
            pr.section("SP", lambda e, c=c, sl=sl: pr.dma(
                xst[sl][:, :], x1_dram[c], reads=[f"x1d{c}"],
                writes=[f"xst{sl}"], q=f"qx{sl}"))
            def qv_fn(e, c=c, sl=sl):
                pr.do(lambda e2: e2.tensor_tensor(
                    scrB[:, 0:TOK], xst[sl][:, :],
                    b1b[:, 0:TOK], ALU.mult),
                    reads=[f"xst{sl}", "b1b"], writes=["scrB"])
                pr.do(lambda e2: e2.tensor_tensor(
                    scrB[:, 0:TOK], scrB[:, 0:TOK], b2b[:, 0:TOK],
                    ALU.subtract),
                    reads=["scrB", "b2b"], writes=["scrB"])
                pr.do(lambda e2: e2.tensor_tensor(
                    scrB[:, 0:TOK], scrB[:, 0:TOK], s1B[:, :], ALU.mult),
                    reads=["scrB", "s1B"], writes=["scrB"])
            pr.section("DVE", qv_fn)
            round_exact(xqi[:, c * TOK:(c + 1) * TOK], "xqi",
                        scrB[:, 0:TOK], "scrB", TOK)
        tap("xqi", xqi[:, :], "xqi")
        tap("s1B", s1B[:, :], "s1B")
        tap("bgB", bgB[:, :], "bgB")
        close_scope("p4s")
        pr.barrier()
        if _PHASES <= 4:
            raise _StopBuild

        ppd = psum_phase([("pp_g0", [128, 512], F32),
                          ("pp_g1", [128, 512], F32),
                          ("pp_u0", [128, 512], F32),
                          ("pp_u1", [128, 512], F32),
                          ("pp_z0", [128, 512], F32),
                          ("pp_z1", [128, 512], F32)])
        utb = [bigA[:, 0:2048], bigA[:, 2048:4096]]
        gtb = [bigA[:, 4096:6144], bigA[:, 6144:8192]]
        islotp = [bigA[:, 8192:9216], bigA[:, 13312:14336]]
        zslot = [bigA[:, 9216:10240], bigA[:, 10240:11264]]
        dtb = bigA[:, 11264:13312]
        wdf = sbin("p45", "wdf", [128, HID], F32)

        def ternarize(src, src_key, dst, dst_key, mi):
            def fn(e):
                pr.do(lambda e2: e2.tensor_scalar(
                    dst[:, :], src[:, :], colv(4 + mi), None, ALU.is_lt),
                    reads=[src_key, "cols"], writes=[dst_key])
                pr.do(lambda e2: e2.scalar_tensor_tensor(
                    dst[:, :], src[:, :], colv(mi), dst[:, :], ALU.is_gt,
                    ALU.subtract),
                    reads=[src_key, "cols", dst_key], writes=[dst_key])
            pr.section("DVE", fn)

        def ldgu(oc):
            def sp(e, oc=oc):
                pr.dma(wstr[0][:, :], wg_h[oc], writes=["wsf0"], q="qw0")
                pr.dma(wstr[1][:, :], wu_h[oc], writes=["wsf1"], q="qw1")
            pr.section("SP", sp)

        def mmz_blk(oc):
            isl = islotp[oc % 2]
            ilk = f"islot{oc % 2}"
            def mmz(e, isl=isl, ilk=ilk):
                for th in range(2):
                    pr.do_group([
                        lambda e2, th=th: e2.matmul(
                            ppd[f"pp_z{th}"][:, 0:512], s_h128[:, :],
                            isl[:, th * 512:th * 512 + 512],
                            start=True, stop=True)],
                        reads=[ilk, "h128"], writes=[f"pp_z{th}"])
            pr.section("PE", mmz)
            zs = zslot[oc % 2]
            zk = f"zs{oc % 2}"
            def evz(e, zs=zs, zk=zk):
                for th in range(2):
                    pr.do(lambda e2, th=th: e2.activation(
                        zs[:, th * 512:th * 512 + 512],
                        ppd[f"pp_z{th}"][:, 0:512], AF.Copy),
                        reads=[f"pp_z{th}"], writes=[zk])
            pr.section("ACT", evz)
            pr.section("POOL", lambda e, oc=oc, zs=zs, zk=zk: pr.dma(
                z_dram[oc], zs[:, :], reads=[zk], writes=[f"zd{oc}"],
                q=f"qsz{oc % 2}"))

        ldgu(0)
        for oc in range(NIC):
            sl = oc % 2
            pr.section("SP", lambda e, oc=oc: pr.dma(
                wdf[:, :], wd_h[oc], writes=["wdf"], q="qwd"))
            ternarize(wstr[0], "wsf0", gtb[sl], f"gtb{sl}", 0)
            ternarize(wstr[1], "wsf1", utb[sl], f"utb{sl}", 1)
            if oc + 1 < NIC:
                ldgu(oc + 1)
            def mmgu(e, oc=oc, sl=sl):
                for th in range(2):
                    for mat, tb in (("g", gtb[sl]), ("u", utb[sl])):
                        pk = f"pp_{mat}{th}"
                        pr.do_group(
                            [(lambda e2, hc=hc, th=th, tb=tb, pk=pk:
                              e2.matmul(ppd[pk][:, 0:512],
                                        tb[:, hc * 128:(hc + 1) * 128],
                                        xqi[:, hc * TOK + th * 512:
                                            hc * TOK + th * 512 + 512],
                                        start=(hc == 0),
                                        stop=(hc == NCO - 1)))
                             for hc in range(NCO)],
                            reads=[("gtb%d" if mat == "g" else "utb%d") % sl,
                                   "xqi"], writes=[pk])
            pr.section("PE", mmgu)
            def evgu(e):
                for th in range(2):
                    pr.do(lambda e2, th=th: e2.tensor_tensor(
                        scrB[:, th * 512:th * 512 + 512],
                        ppd[f"pp_g{th}"][:, 0:512],
                        bgB[:, th * 512:th * 512 + 512], ALU.mult),
                        reads=[f"pp_g{th}", "bgB"], writes=["scrB"])
                    pr.do(lambda e2, th=th: e2.tensor_tensor(
                        scrC[:, th * 512:th * 512 + 512],
                        ppd[f"pp_u{th}"][:, 0:512],
                        buB[:, th * 512:th * 512 + 512], ALU.mult),
                        reads=[f"pp_u{th}", "buB"], writes=["scrC"])
            pr.section("DVE", evgu)
            pr.section("ACT", lambda e: pr.do(
                lambda e2: e2.activation(scrD[:, 0:TOK], scrB[:, 0:TOK],
                                         AF.Silu),
                reads=["scrB"], writes=["scrD"]))
            isl = islotp[oc % 2]
            ilk = f"islot{oc % 2}"
            pr.section("DVE", lambda e, isl=isl, ilk=ilk: pr.do(
                lambda e2: e2.tensor_tensor(isl[:, 0:TOK], scrD[:, 0:TOK],
                                            scrC[:, 0:TOK], ALU.mult),
                reads=["scrD", "scrC"], writes=[ilk]))
            mmz_blk(oc)
            ternarize(wdf, "wdf", dtb, "dtb", 2)
            pr.section("POOL", lambda e, oc=oc: pr.dma(
                trd_dram[oc], dtb[:, :], reads=["dtb"], writes=[f"td{oc}"],
                q="qstd"))
        close_scope("p45")
        pr.barrier()
        if _PHASES <= 5:
            raise _StopBuild

        ppd = psum_phase([("pp_y0", [128, 512], F32),
                          ("pp_y1", [128, 512], F32),
                          ("pp_t", [128, 512], F32),
                          ("pp_b", [128, 512], F32)])
        zt = [bigA[:, 0:1024], bigA[:, 1024:2048]]
        ys = [bigA[:, 2048:3072], bigA[:, 3072:4096]]
        zv3 = z_dram.rearrange("a b t -> b a t")
        def amax_init(e):
            pr.do(lambda e2: e2.memset(scrD[:, 0:TOK], -3.0e38),
                  writes=["scrD"])
            pr.do(lambda e2: e2.memset(scrC[:, 0:TOK], 3.0e38),
                  writes=["scrC"])
        pr.section("DVE", amax_init)
        for c in range(NIC):
            sl = c % 2
            def zl(e, c=c, sl=sl):
                if _P6SUB & 1:
                    pr.dma(zt[sl][0:64, :], zv3[2 * c], writes=[f"zt{sl}"],
                           q=f"qz{sl}")
                    pr.dma(zt[sl][64:128, :], zv3[2 * c + 1],
                           writes=[f"zt{sl}"], q=f"qz{sl}")
            pr.section("SP", zl)
            if not (_P6SUB & 1):
                pr.section("DVE", lambda e, sl=sl: pr.do(
                    lambda e2: e2.memset(zt[sl][:, :], 0.0),
                    writes=[f"zt{sl}"]))
            if not (_P6SUB & 2):
                continue
            def mmy(e, c=c, sl=sl):
                for th in range(2):
                    pr.do_group([
                        lambda e2, th=th: e2.matmul(
                            ppd[f"pp_y{th}"][:, 0:512], s_h64b[:, :],
                            zt[sl][:, th * 512:th * 512 + 512],
                            start=True, stop=True)],
                        reads=[f"zt{sl}", "h64b"], writes=[f"pp_y{th}"])
            pr.section("PE", mmy)
            def evy(e, sl=sl):
                for th in range(2):
                    pr.do(lambda e2, th=th: e2.activation(
                        ys[sl][:, th * 512:th * 512 + 512],
                        ppd[f"pp_y{th}"][:, 0:512], AF.Copy),
                        reads=[f"pp_y{th}"], writes=[f"ys{sl}"])
            pr.section("ACT", evy)
            def mx(e, sl=sl):
                for th in range(2):
                    pr.do(lambda e2, th=th: e2.tensor_tensor(
                        scrD[:, th * 512:th * 512 + 512],
                        scrD[:, th * 512:th * 512 + 512],
                        ppd[f"pp_y{th}"][:, 0:512], ALU.max),
                        reads=[f"pp_y{th}", "scrD", f"ys{sl}"],
                        writes=["scrD"])
                    pr.do(lambda e2, th=th: e2.tensor_tensor(
                        scrC[:, th * 512:th * 512 + 512],
                        scrC[:, th * 512:th * 512 + 512],
                        ppd[f"pp_y{th}"][:, 0:512], ALU.min),
                        reads=[f"pp_y{th}", "scrC"], writes=["scrC"])
            pr.section("DVE", mx)
            pr.section("POOL", lambda e, c=c, sl=sl: pr.dma(
                yv_dram[c], ys[sl][:, :], reads=[f"ys{sl}"],
                writes=[f"yv{c}"], q=f"qsy{sl}"))
        if not (_P6SUB & 4):
            raise _StopBuild
        def ymax_comb(e):
            pr.do(lambda e2: e2.tensor_scalar(
                scrC[:, 0:TOK], scrC[:, 0:TOK], -1.0, None, ALU.mult),
                reads=["scrC"], writes=["scrC"])
            pr.do(lambda e2: e2.tensor_tensor(
                scrD[:, 0:TOK], scrD[:, 0:TOK], scrC[:, 0:TOK], ALU.max),
                reads=["scrC", "scrD"], writes=["scrD"])
        pr.section("DVE", ymax_comb)
        tokmax("scrD", 10)
        qmB = sbin("p67", "qmB", [128, TOK], F32)
        bdB = sbin("p67", "bdB", [128, TOK], F32)
        def s2_rows(e):
            pr.do(lambda e2: e2.tensor_scalar(rowv(10, TOK), rowv(10, TOK),
                                              FWHT_NORM, 1e-5, ALU.mult,
                                              ALU.max),
                  reads=["rows"], writes=["rows"])
            pr.do(lambda e2: e2.reciprocal(rowv(11, TOK), rowv(10, TOK)),
                  reads=["rows"], writes=["rows"])
            pr.do(lambda e2: e2.tensor_scalar(rowv(11, TOK), rowv(11, TOK),
                                              128.0 * FWHT_NORM, None,
                                              ALU.mult),
                  reads=["rows"], writes=["rows"])
            pr.do(lambda e2: e2.tensor_scalar(rowv(12, TOK), rowv(10, TOK),
                                              mrow_ap(2), 1.0 / 128.0,
                                              ALU.mult, ALU.mult),
                  reads=["rows", "rows"], writes=["rows"])
        pr.section("DVE", s2_rows)
        for ri, dst, dkey in ((11, qmB, "qmB"), (12, bdB, "bdB")):
            for g in range(2):
                bcast_row(ri, g * 512, dst[:, g * 512:g * 512 + 512],
                          dkey, 512)
        pr.barrier()
        if _PHASES <= 6:
            raise _StopBuild

        ppd = psum_phase([(f"pp_d{i}", [128, 512], F32) for i in range(8)])
        d_out = bigB
        aqs = [bigA[:, 0:1024], bigA[:, 1024:2048]]
        tds = [bigA[:, 2048:3072], bigA[:, 3072:4096]]
        zt_p7 = [bigA[:, 4096:5120], bigA[:, 5120:6144]]

        for oh in range(2):
            for th in range(2):
                first = (oh == 0 and th == 0)
                for c in range(NIC):
                    sl = c % 2
                    def spl(e, c=c, sl=sl, oh=oh, th=th, first=first):
                        pr.dma(tds[sl][:, :],
                               trd_dram[c][:, oh * 1024:oh * 1024 + 1024],
                               writes=[f"tds{sl}"], q=f"qt{sl}")
                        if first:
                            pr.dma(zt_p7[sl][:, :], yv_dram[c],
                                   writes=[f"ybp{sl}"], q=f"qy{sl}")
                        else:
                            pr.dma(aqs[sl][:, 0:512],
                                   aq2_dram[c][:, th * 512:th * 512 + 512],
                                   reads=[f"aqd{c}"], writes=[f"aqs{sl}"],
                                   q=f"qa{sl}")
                    pr.section("SP", spl)
                    if first:
                        pr.section("DVE", lambda e, sl=sl: pr.do(
                            lambda e2: e2.tensor_tensor(
                                scrB[:, 0:TOK], zt_p7[sl][:, :], qmB[:, :],
                                ALU.mult),
                            reads=[f"ybp{sl}", "qmB"], writes=["scrB"]))
                        if c == 0:
                            tap("qv_c0", scrB[:, 0:TOK], "scrB")
                            tap("yb_c0", zt_p7[sl][:, :], f"ybp{sl}")
                            tap("qmB", qmB[:, :], "qmB")
                        round_exact(aqs[sl][:, 0:TOK], f"aqs{sl}",
                                    scrB[:, 0:TOK], "scrB", TOK,
                                    dbg=("r7" if (_DEBUG and c == 0)
                                         else None))
                        pr.section("POOL", lambda e, c=c, sl=sl: pr.dma(
                            aq2_dram[c], aqs[sl][:, :], reads=[f"aqs{sl}"],
                            writes=[f"aqd{c}"], q=f"qsa{sl}"))
                    def mmd(e, c=c, sl=sl, th=th, first=first):
                        rhs = aqs[sl][:, th * 512:th * 512 + 512] if first \
                            else aqs[sl][:, 0:512]
                        builds = []
                        for j in range(8):
                            builds.append(
                                (lambda e2, j=j, sl=sl, rhs=rhs:
                                 e2.matmul(ppd[f"pp_d{j}"][:, 0:512],
                                           tds[sl][:, j * 128:(j + 1) * 128],
                                           rhs, start=(c == 0),
                                           stop=(c == NIC - 1))))
                        pr.do_group(builds, reads=[f"tds{sl}", f"aqs{sl}"],
                                    writes=["ppd"])
                    pr.section("PE", mmd)
                def evd(e, oh=oh, th=th):
                    for j in range(8):
                        oc = oh * 8 + j
                        pr.do(lambda e2, j=j, oc=oc, th=th: e2.tensor_tensor(
                            d_out[:, oc * TOK + th * 512:
                                  oc * TOK + th * 512 + 512],
                            ppd[f"pp_d{j}"][:, 0:512],
                            bdB[:, th * 512:th * 512 + 512], ALU.mult),
                            reads=["ppd", "bdB"], writes=["d_out"])
                pr.section("DVE", evd)
        close_scope("p67")
        pr.barrier()
        if _PHASES <= 7:
            raise _StopBuild

        ppd = psum_phase([("pp_stat", [128, 512], F32),
                          ("pp_b", [128, 512], F32)])
        for g in range(2):
            stats2048("ln4g%d" % g,
                      lambda c, g=g: d_out[:, c * TOK + g * 512:
                                           c * TOK + g * 512 + 512],
                      g * 512, 512, "d_out", bf_src=True)
        for c in range(NCO):
            sl = c % 2
            pr.section("SP", lambda e, c=c, sl=sl: pr.dma(
                xst[sl][:, :], x1_dram[c], reads=[f"x1d{c}"],
                writes=[f"xst{sl}"], q=f"qx{sl}"))
            def ln4_apply(e, c=c, sl=sl):
                pr.do(lambda e2: e2.tensor_tensor(
                    scrB[:, 0:TOK], d_out[:, c * TOK:(c + 1) * TOK],
                    b1b[:, 0:TOK], ALU.mult),
                    reads=["d_out", "b1b"], writes=["scrB"])
                pr.do(lambda e2: e2.tensor_tensor(
                    scrB[:, 0:TOK], scrB[:, 0:TOK], b2b[:, 0:TOK],
                    ALU.subtract),
                    reads=["scrB", "b2b"], writes=["scrB"])
                pr.do(lambda e2: e2.tensor_tensor(
                    xst[sl][:, :], xst[sl][:, :], scrB[:, 0:TOK], ALU.add),
                    reads=["scrB", f"xst{sl}"], writes=[f"xst{sl}"])
            pr.section("DVE", ln4_apply)
            pr.section("POOL", lambda e, c=c, sl=sl: pr.dma(
                y_h[c * 128:(c + 1) * 128, :], xst[sl][:, :],
                reads=[f"xst{sl}"], q=f"qso{sl}"))
        t0, t1 = pr.cnt.get("qso0", 0), pr.cnt.get("qso1", 0)
        def final_wait(e):
            pr._wait("qso0", t0)
            pr._wait("qso1", t1)
        pr.section("POOL", final_wait)
    except _StopBuild:
        pass
    return nc


def _stage(inputs):
    x = np.asarray(inputs["L0_attn_x"], dtype=np.float32)
    wqkv = _stage_stripes(np.asarray(inputs["L0_attn_w_qkv"], np.float32), 24)
    wo = np.ascontiguousarray(
        np.asarray(inputs["L0_attn_w_o"], np.float32).T).reshape(16, 128, HID)
    wg = _stage_stripes(np.asarray(inputs["L0_ffn_w_gate"], np.float32), NIC)
    wu = _stage_stripes(np.asarray(inputs["L0_ffn_w_up"], np.float32), NIC)
    wdT = np.ascontiguousarray(np.asarray(inputs["L0_ffn_w_down"],
                                          np.float32).T)
    perm = np.empty(IFF, dtype=np.int64)
    for c in range(NIC):
        for p in range(128):
            perm[c * 128 + p] = (p % 64) * 128 + 2 * c + p // 64
    wd = np.ascontiguousarray(wdT[perm].reshape(NIC, 128, HID))
    cos, sin = _rope_tables()
    sinf = sin.copy()
    sinf[:64, :] = -sinf[:64, :]
    qn_g = np.asarray(inputs["L0_attn_qn_g"], np.float32).reshape(128, 1)
    qn_b = np.asarray(inputs["L0_attn_qn_b"], np.float32).reshape(128, 1)
    kn_g = np.asarray(inputs["L0_attn_kn_g"], np.float32).reshape(128, 1)
    kn_b = np.asarray(inputs["L0_attn_kn_b"], np.float32).reshape(128, 1)

    in_maps = []
    meta = []
    for c in range(8):
        bc = c // 2
        blocks = (0, 3) if c % 2 == 0 else (1, 2)
        own = np.concatenate([np.arange(b * 512, (b + 1) * 512)
                              for b in blocks])
        xk = np.ascontiguousarray(x[bc].T)
        xq = np.ascontiguousarray(x[bc].T[:, own])
        cosq = np.ascontiguousarray(cos[:, own])
        sinq = np.ascontiguousarray(sinf[:, own])
        mask = np.zeros((16, 128, 512), dtype=ml_dtypes.bfloat16)
        for (qb, kt) in MASKED:
            i = MASK_IDX[(qb, kt)]
            q0 = blocks[qb] * 512
            kk = kt * 128 + np.arange(128)[:, None]
            qq = q0 + np.arange(512)[None, :]
            mask[i] = (kk <= qq).astype(np.float32).astype(ml_dtypes.bfloat16)
        in_maps.append(dict(
            xq=xq, xk=xk, cosq=cosq, sinq=sinq,
            cosk=np.ascontiguousarray(cos), sink=np.ascontiguousarray(sinf),
            mask01=mask, qn_g=qn_g, qn_b=qn_b, kn_g=kn_g, kn_b=kn_b,
            wqkv=wqkv, wo=wo, wg=wg, wu=wu, wd=wd))
        meta.append((bc, blocks))
    return in_maps, meta


def kernel(**inputs):
    in_maps, meta = _stage(inputs)
    nc = build_program()
    res = run_bass_kernel_spmd(nc, in_maps, core_ids=list(range(8)))
    out = np.empty((B, S, HID), dtype=np.float32)
    for c in range(8):
        bc, blocks = meta[c]
        y = np.asarray(res.results[c]["y"])
        for qi, b in enumerate(blocks):
            out[bc, b * 512:(b + 1) * 512, :] = y[:, qi * 512:(qi + 1) * 512].T
    return out


# version 22
# speedup vs baseline: 1.0045x; 1.0045x over previous
import contextlib
import numpy as np
import ml_dtypes
import concourse.bass as bass
import concourse.mybir as mybir
from concourse.bass_utils import run_bass_kernel_spmd

F32 = mybir.dt.float32
BF16 = mybir.dt.bfloat16
I32 = mybir.dt.int32
AF = mybir.ActivationFunctionType
ALU = mybir.AluOpType
AX = mybir.AxisListType

B, S, HID = 4, 2048, 2048
H, HKV, D = 16, 4, 128
IFF = 8192
NCO = 16
NIC = 64
TOK = 1024
EPS = 1e-5
INV_SQRT_D = 1.0 / float(np.float32(np.sqrt(np.float32(128.0))))
FWHT_NORM = float(np.float32(1.0 / np.sqrt(8192.0)))
MAGIC = 12582912.0
MAGIC64 = 6755399441055744.0
INV_N_GU = 1.0 / float(IFF * HID)

MASKED = [(0, kt) for kt in range(8)] + [(1, kt) for kt in range(8, 16)]
MASK_IDX = {qk: i for i, qk in enumerate(MASKED)}
KT_OF_QB = {0: 8, 1: 16}
_DEBUG = False
_PHASES = 99
_P6SUB = 7


class _StopBuild(Exception):
    pass


def _rope_tables():
    import jax
    import jax.numpy as jnp
    cpu = jax.devices("cpu")[0]
    with jax.default_device(cpu):
        inv = 1.0 / (1000000.0 ** (jnp.arange(0, D, 2, dtype=jnp.float32) / D))
        t = jnp.arange(S, dtype=jnp.float32)
        fr = jnp.outer(t, inv)
        cos = np.asarray(jnp.concatenate([jnp.cos(fr), jnp.cos(fr)], axis=-1))
        sin = np.asarray(jnp.concatenate([jnp.sin(fr), jnp.sin(fr)], axis=-1))
    return cos.T.copy(), sin.T.copy()


def _stage_stripes(w, n_oc):
    out_dim, in_dim = w.shape
    nhc = in_dim // 128
    st = np.empty((n_oc, 128, in_dim), dtype=np.float32)
    for oc in range(n_oc):
        s = np.ascontiguousarray(w[oc * 128:(oc + 1) * 128, :].T)
        st[oc] = s.reshape(nhc, 128, 128).transpose(1, 0, 2).reshape(128,
                                                                     in_dim)
    return st


class Pr:
    ENG = {"PE": "tensor", "ACT": "scalar", "DVE": "vector", "POOL": "gpsimd",
           "SP": "sync"}
    QUEUES = {"SP": "QSP", "POOL": "QPOOL"}

    def __init__(self, nc, es):
        self.nc = nc
        self.es = es
        self.block = es.enter_context(nc.Block())
        self.sem = {}
        self.cnt = {}
        for name in ["PE", "ACT", "DVE", "POOL"]:
            self.sem[name] = es.enter_context(nc.semaphore(name.lower()))
            self.cnt[name] = 0
        self.waited = {}
        self.lastw = {}
        self.readers = {}
        self.cur = None

    def qsem(self, name):
        if name not in self.sem:
            self.sem[name] = self.es.enter_context(
                self.nc.semaphore(name.lower()))
            self.cnt[name] = 0
        return name

    def section(self, eng_name, fn):
        def body(e):
            self.cur = (eng_name, e)
            fn(e)
            self.cur = None
        getattr(self.block, self.ENG[eng_name])(body)

    def _wait(self, sem_name, val):
        if val <= 0:
            return
        key = (self.cur[0], sem_name)
        if self.waited.get(key, 0) >= val:
            return
        self.cur[1].wait_ge(self.sem[sem_name], val)
        self.waited[key] = val

    def _need(self, reads, writes):
        for k in reads:
            w = self.lastw.get(k)
            if w:
                self._wait(*w)
        for k in writes:
            w = self.lastw.get(k)
            if w:
                self._wait(*w)
            for r in self.readers.get(k, ()):
                self._wait(*r)

    def _did(self, sem_name, reads, writes):
        c = self.cnt[sem_name]
        for k in writes:
            self.lastw[k] = (sem_name, c)
            self.readers[k] = []
        for k in reads:
            self.readers.setdefault(k, []).append((sem_name, c))

    def do(self, build, reads=(), writes=()):
        eng_name, e = self.cur
        self._need(reads, writes)
        inst = build(e)
        self.cnt[eng_name] += 1
        inst.then_inc(self.sem[eng_name], 1)
        self._did(eng_name, reads, writes)
        return inst

    def do_group(self, builds, reads=(), writes=()):
        eng_name, e = self.cur
        self._need(reads, writes)
        inst = None
        for b in builds:
            inst = b(e)
        self.cnt[eng_name] += 1
        inst.then_inc(self.sem[eng_name], 1)
        self._did(eng_name, reads, writes)
        return inst

    def dma(self, out, in_, reads=(), writes=(), q="q_dflt"):
        eng_name, e = self.cur
        self.qsem(q)
        self._need(reads, writes)
        inst = e.dma_start(out=out, in_=in_)
        self.cnt[q] += 16
        inst.then_inc(self.sem[q], 16)
        c = self.cnt[q]
        for k in writes:
            self.lastw[k] = (q, c)
            self.readers[k] = []
        for k in reads:
            self.readers.setdefault(k, []).append((q, c))
        return inst

    def barrier(self):
        snap = dict(self.cnt)
        for en in ["PE", "ACT", "DVE", "POOL", "SP"]:
            def fn(e, en=en):
                for sn, v in snap.items():
                    if sn != en and v > 0:
                        self._wait(sn, v)
            self.section(en, fn)


def build_program():
    nc = bass.Bass("TRN2", target_bir_lowering=False)

    def din(name, shape, dtype=F32):
        return nc.dram_tensor(name, shape, dtype, kind="ExternalInput").ap()

    xq_h = din("xq", [HID, TOK])
    xk_h = din("xk", [HID, S])
    cosq_h = din("cosq", [D, TOK]); sinq_h = din("sinq", [D, TOK])
    cosk_h = din("cosk", [D, S]); sink_h = din("sink", [D, S])
    mask_h = din("mask01", [16, 128, 512], BF16)
    qng_h = din("qn_g", [D, 1]); qnb_h = din("qn_b", [D, 1])
    kng_h = din("kn_g", [D, 1]); knb_h = din("kn_b", [D, 1])
    wqkv_h = din("wqkv", [24, 128, HID])
    wo_h = din("wo", [16, 128, HID])
    wg_h = din("wg", [NIC, 128, HID])
    wu_h = din("wu", [NIC, 128, HID])
    wd_h = din("wd", [NIC, 128, HID])
    y_h = nc.dram_tensor("y", [HID, TOK], F32, kind="ExternalOutput").ap()

    z_dram = nc.dram_tensor("z_scr", [NIC, 128, TOK], BF16, kind="Internal").ap()
    yv_dram = nc.dram_tensor("yv_scr", [NIC, 128, TOK], BF16,
                             kind="Internal").ap()
    aq2_dram = nc.dram_tensor("aq2_scr", [NIC, 128, TOK], BF16,
                              kind="Internal").ap()
    trd_dram = nc.dram_tensor("trd_scr", [NIC, 128, HID], BF16,
                              kind="Internal").ap()
    x1_dram = nc.dram_tensor("x1_scr", [NCO, 128, TOK], F32,
                             kind="Internal").ap()

    ident_bf = nc.inline_tensor(np.eye(128, dtype=ml_dtypes.bfloat16), "id_bf")
    ident_f = nc.inline_tensor(np.eye(128, dtype=np.float32), "id_f")
    ones_col_t = nc.inline_tensor(np.ones((128, 1), dtype=np.float32), "ones_c")
    ones_row_t = nc.inline_tensor(np.ones((1, 128), dtype=np.float32), "ones_r")
    ones_all_t = nc.inline_tensor(np.ones((128, 128), dtype=np.float32),
                                  "ones_a")
    eps_col_t = nc.inline_tensor(np.full((128, 1), EPS, dtype=np.float32),
                                 "eps_c")
    srt_t = nc.inline_tensor(np.full((1, 1), np.sqrt(np.float32(128.0)),
                                     dtype=np.float32), "srt_c")
    hmat = np.ones((1, 1), dtype=np.float64)
    while hmat.shape[0] < 128:
        hmat = np.block([[hmat, hmat], [hmat, -hmat]])
    h64 = hmat[:64, :64]
    h64blk = np.zeros((128, 128), dtype=np.float64)
    h64blk[:64, :64] = h64; h64blk[64:, 64:] = h64
    h128_t = nc.inline_tensor(hmat.astype(ml_dtypes.bfloat16), "h128")
    h64b_t = nc.inline_tensor(h64blk.astype(ml_dtypes.bfloat16), "h64b")

    es = contextlib.ExitStack()
    try:
      with es:
        pr = Pr(nc, es)
        scopes = {}
        tapn = [0]

        def tap(name, ap, key):
            if not _DEBUG:
                return
            t = nc.dram_tensor("tap_" + name, list(ap.shape), ap.dtype,
                               kind="Internal").ap()
            tapn[0] += 1
            pr.section("POOL", lambda e: pr.dma(t, ap, reads=[key],
                                                q="qtap%d" % tapn[0]))

        def sbin(scope, name, shape, dt=F32):
            st = scopes.setdefault(scope, contextlib.ExitStack())
            return st.enter_context(nc.sbuf_tensor(name, shape, dt))

        def close_scope(scope):
            if scope in scopes:
                scopes.pop(scope).close()

        sb = lambda n, s, d=F32: sbin("all", n, s, d)

        qng = sb("qng", [128, 1]); qnb = sb("qnb", [128, 1])
        kng = sb("kng", [128, 1]); knb = sb("knb", [128, 1])
        s_id_bf = sb("s_id_bf", [128, 128], BF16)
        s_id_f = sb("s_id_f", [128, 128], F32)
        s_ones_c = sb("s_ones_c", [128, 1], F32)
        s_ones_r = sb("s_ones_r", [1, 128], F32)
        s_eps = sb("s_eps", [128, 1], F32)
        s_h128 = sb("s_h128", [128, 128], BF16)
        s_h64b = sb("s_h64b", [128, 128], BF16)
        onescol_bf = sb("onescol_bf", [128, 1], BF16)
        s_ones_all = sb("s_ones_all", [128, 128], F32)
        rows = sb("rows", [128, 4096], F32)
        _RP = {0: 32, 1: 0, 2: 0, 3: 32, 4: 64, 5: 64, 6: 0, 7: 32,
               8: 0, 9: 0, 10: 32, 11: 0, 12: 32, 13: 64, 14: 32, 15: 64}
        _RC = {0: 0, 1: 0, 2: 512, 3: 512, 4: 0, 5: 1024, 6: 1024,
               7: 1024, 8: 2048, 9: 2560, 10: 2048, 11: 3072, 12: 3072,
               13: 3072, 14: 512, 15: 512}
        cols = sb("cols", [128, 32], F32)
        mean_acc = sb("mean_acc", [128, 4], F32)
        scrA = sb("scrA", [128, 1024], F32)
        scrB = sb("scrB", [128, 1024], F32)
        scrC = sb("scrC", [128, 1024], F32)
        scrD = sb("scrD", [128, 1024], F32)
        b1b = sb("b1b", [128, 1024], F32)
        b2b = sb("b2b", [128, 1024], F32)
        wstr = [sb(f"wstr{i}", [128, HID], F32) for i in range(2)]
        wstr_bf = [sb(f"wstrbf{i}", [128, HID], BF16) for i in range(2)]
        bigA = sb("bigA", [128, NCO * TOK], BF16)
        bigB = sb("bigB", [128, NCO * TOK], BF16)

        def rowv(i, n=1024):
            p = _RP[i]
            return rows[p:p + 1, _RC[i]:_RC[i] + n]

        def colv(i):
            return cols[:, i:i + 1]

        def p0_sp(e):
            pr.dma(qng[:, :], qng_h, writes=["qng"], q="qc_qng")
            pr.dma(qnb[:, :], qnb_h, writes=["qnb"], q="qc_qnb")
            pr.dma(kng[:, :], kng_h, writes=["kng"], q="qc_kng")
            pr.dma(knb[:, :], knb_h, writes=["knb"], q="qc_knb")
            pr.dma(s_id_bf[:, :], ident_bf.ap(), writes=["idbf"], q="qc_idbf")
            pr.dma(s_id_f[:, :], ident_f.ap(), writes=["idf"], q="qc_idf")
            pr.dma(s_ones_c[:, :], ones_col_t.ap(), writes=["onesc"], q="qc_onesc")
            pr.dma(s_ones_r[:, :], ones_row_t.ap(), writes=["onesr"], q="qc_onesr")
            pr.dma(s_ones_all[:, :], ones_all_t.ap(), writes=["onesall"],
                   q="qc_onesall")
            pr.dma(s_eps[:, :], eps_col_t.ap(), writes=["s_eps"], q="qc_s_eps")
            pr.dma(s_h128[:, :], h128_t.ap(), writes=["h128"], q="qc_h128")
            pr.dma(s_h64b[:, :], h64b_t.ap(), writes=["h64b"], q="qc_h64b")
        pr.section("SP", p0_sp)

        def p0_dve(e):
            pr.do(lambda e2: e2.memset(mean_acc[:, :], 0.0),
                  writes=["mean_acc"])
            pr.do(lambda e2: e2.memset(cols[:, :], 0.0), writes=["cols"])
            pr.do(lambda e2: e2.tensor_copy(onescol_bf[:, :], s_ones_c[:, :]),
                  reads=["onesc"], writes=["onescbf"])
        pr.section("DVE", p0_dve)

        mslot = [None, None]
        mean_jobs = []
        for mi, wh in enumerate((wg_h, wu_h, wd_h)):
            for oc in range(NIC):
                for hf in range(2):
                    mean_jobs.append((wh[oc][:, hf * 1024:hf * 1024 + 1024],
                                      mi))
        mean_state = {"dma": 0, "cmp": 0}

        def mean_emit(n):
            j0 = mean_state["cmp"]
            j1 = min(j0 + n, len(mean_jobs))
            if j0 >= j1:
                return
            def emit_dmas(bound):
                while mean_state["dma"] <= min(bound, len(mean_jobs) - 1):
                    jd = mean_state["dma"]
                    srcd, _m = mean_jobs[jd]
                    sld = jd % 2
                    pr.dma(mslot[sld][:, :], srcd, writes=[f"mslot{sld}"],
                           q=f"qm{sld}")
                    mean_state["dma"] += 1
            for j in range(j0, j1):
                src_, mi = mean_jobs[j]
                sl = j % 2
                ck = f"mcol{sl}"
                def act_fn(e, j=j, sl=sl, ck=ck):
                    emit_dmas(j + 1)
                    pr.do(lambda e2: e2.activation(
                        mslot[sl][:, :], mslot[sl][:, :], AF.Abs,
                        accum_out=colv(8 + sl)),
                        reads=[f"mslot{sl}"], writes=[f"mslot{sl}", ck])
                    emit_dmas(j + 2)
                pr.section("ACT", act_fn)
                pr.section("DVE", lambda e, mi=mi, sl=sl, ck=ck: pr.do(
                    lambda e2: e2.tensor_tensor(
                        mean_acc[:, mi:mi + 1], mean_acc[:, mi:mi + 1],
                        colv(8 + sl), ALU.add),
                    reads=[ck, "mean_acc"], writes=["mean_acc"]))
            mean_state["cmp"] = j1

        psum_es = [None, 0]

        def psum_phase(spec):
            if psum_es[0] is not None:
                psum_es[0].close()
            psum_es[0] = contextlib.ExitStack()
            psum_es[1] += 1
            out = {}
            for name, shape, dt in spec:
                out[name] = psum_es[0].enter_context(
                    nc.psum_tensor(f"{name}_{psum_es[1]}", shape, dt))
            return out

        ppd = {}

        def stats2048(tag, chunk_ap, T0, T, src_key, bf_src=False,
                      chunk_load=None):
            kS, kS2 = f"{tag}S", f"{tag}S2"
            ones_ap = onescol_bf[:, :] if bf_src else s_ones_c[:, :]
            ones_key = "onescbf" if bf_src else "onesc"
            for c in range(NCO):
                if chunk_load is not None:
                    chunk_load(c)
                skey = src_key(c) if callable(src_key) else src_key
                pr.section("PE", lambda e, c=c, skey=skey: pr.do_group(
                    [lambda e2: e2.matmul(
                        ppd["pp_stat"][0:1, 0:T], ones_ap, chunk_ap(c),
                        start=(c == 0), stop=(c == NCO - 1))],
                    reads=[skey, ones_key], writes=[kS, "ppstat"]))
                sl = c % 2
                scr = scrA[:, sl * 512:sl * 512 + 512][:, 0:T]
                pr.section("DVE", lambda e, c=c, scr=scr, skey=skey: pr.do(
                    lambda e2: e2.tensor_tensor(scr, chunk_ap(c), chunk_ap(c),
                                                ALU.mult),
                    reads=[skey], writes=["scrA"]))
                pr.section("PE", lambda e, c=c, scr=scr: pr.do_group(
                    [lambda e2: e2.matmul(ppd["pp_stat"][64:65, 0:T],
                                          s_ones_c[:, :], scr,
                                          start=(c == 0),
                                          stop=(c == NCO - 1))],
                    reads=["scrA", "onesc"], writes=[kS2, "ppstat"]))
            def dve_rows(e):
                pr.do(lambda e2: e2.tensor_scalar(
                    rowv(0, T), ppd["pp_stat"][0:1, 0:T], 1.0 / 2048.0, None,
                    ALU.mult), reads=[kS], writes=["hr0"])
                pr.do(lambda e2: e2.tensor_scalar(
                    rowv(1, T), ppd["pp_stat"][64:65, 0:T], 1.0 / 2048.0,
                    None, ALU.mult), reads=[kS2], writes=["hr0"])
                pr.do(lambda e2: e2.tensor_tensor(
                    rowv(2, T), rowv(0, T), rowv(0, T), ALU.mult),
                    reads=["hr0"], writes=["hr0"])
                pr.do(lambda e2: e2.tensor_tensor(
                    rowv(1, T), rowv(1, T), rowv(2, T), ALU.subtract),
                    reads=["hr0"], writes=["hr0"])
            pr.section("DVE", dve_rows)
            pr.section("ACT", lambda e: pr.do(
                lambda e2: e2.activation(rowv(2, T), rowv(1, T), AF.Sqrt,
                                         bias=s_eps[0:1, 0:1]),
                reads=["hr0", "s_eps"], writes=["hr0"]))
            def dve_rows2(e):
                pr.do(lambda e2: e2.reciprocal(rowv(3, T), rowv(2, T)),
                      reads=["hr0"], writes=["hr0"])
                pr.do(lambda e2: e2.tensor_tensor(
                    rowv(4, T), rowv(0, T), rowv(3, T), ALU.mult),
                    reads=["hr0"], writes=["hr0"])
            pr.section("DVE", dve_rows2)
            bcast_row(3, 0, b1b[:, T0:T0 + T], "b1b", T, row_key="hr0")
            bcast_row(4, 0, b2b[:, T0:T0 + T], "b2b", T, row_key="hr0")

        def bcast_row(ri, col0, dst_ap, dst_key, T, row_key="rows"):
            p = _RP[ri]
            src = rows[p:p + 1, _RC[ri] + col0:_RC[ri] + col0 + T]
            pr.section("PE", lambda e: pr.do(
                lambda e2: e2.matmul(ppd["pp_b"][:, 0:T],
                                     s_ones_all[p:p + 1, :], src,
                                     start=True, stop=True),
                reads=[row_key, "onesall"], writes=["ppb"]))
            pr.section("DVE", lambda e: pr.do(
                lambda e2: e2.tensor_copy(dst_ap, ppd["pp_b"][:, 0:T]),
                reads=["ppb"], writes=[dst_key]))

        def round_exact(dst_ap, dst_key, v_ap, v_key, T, dbg=None):
            kf = scrKI[:, 0:T].bitcast(F32)
            def dve1(e):
                pr.do(lambda e2: e2.tensor_scalar(
                    scrC[:, 0:T], v_ap, MAGIC, MAGIC, ALU.add, ALU.subtract),
                    reads=[v_key], writes=["scrC"])
                pr.do(lambda e2: e2.tensor_copy(kf, scrC[:, 0:T]),
                      reads=["scrC"], writes=["scrKI"])
                pr.do(lambda e2: e2.scalar_tensor_tensor(
                    scrC[:, 0:T], kf, -1.0, v_ap, ALU.mult, ALU.add),
                    reads=["scrKI", v_key], writes=["scrC"])
            pr.section("DVE", dve1)
            if dbg:
                tap(dbg + "k0", kf, "scrKI")
                tap(dbg + "d", scrC[:, 0:T], "scrC")
            def dve3(e):
                pr.do(lambda e2: e2.tensor_scalar(
                    scrC[:, 0:T], scrC[:, 0:T], MAGIC64, MAGIC64,
                    ALU.add, ALU.subtract),
                    reads=["scrC"], writes=["scrC"])
                pr.do(lambda e2: e2.scalar_tensor_tensor(
                    scrC[:, 0:T], scrC[:, 0:T], 1.0, kf, ALU.mult, ALU.add),
                    reads=["scrC", "scrKI"], writes=["scrC"])
                pr.do(lambda e2: e2.tensor_scalar(
                    dst_ap, scrC[:, 0:T], -128.0, 127.0, ALU.max, ALU.min),
                    reads=["scrC"], writes=[dst_key])
            pr.section("DVE", dve3)

        def load_stripe(sl, src_ap, n=HID):
            pr.section("SP", lambda e: pr.dma(wstr[sl][:, 0:n], src_ap,
                                              writes=[f"wsf{sl}"],
                                              q=f"qw{sl}"))
            pr.section("DVE", lambda e: pr.do(
                lambda e2: e2.tensor_copy(wstr_bf[sl][:, 0:n],
                                          wstr[sl][:, 0:n]),
                reads=[f"wsf{sl}"], writes=[f"wsb{sl}"]))

        def ln_from_dram(tag, dram_ap, col0, dst, xg):
            for g in range(4):
                T0g = g * 256
                kx = "xgrp"
                def sp(e, g=g, kx=kx):
                    for c in range(NCO):
                        pr.dma(xg[c][:, :],
                               dram_ap[c * 128:(c + 1) * 128,
                                       col0 + g * 256: col0 + g * 256 + 256],
                               writes=[kx], q="qxg")
                pr.section("SP", sp)
                stats2048(tag + str(g), lambda c: xg[c][:, :],
                          0, 256, kx)
                def dve_apply(e, g=g, kx=kx):
                    for c in range(NCO):
                        pr.do(lambda e2, c=c: e2.tensor_tensor(
                            xg[c][:, :], xg[c][:, :], b1b[:, 0:256],
                            ALU.mult),
                            reads=["b1b", kx], writes=[kx])
                        pr.do(lambda e2, c=c, g=g: e2.tensor_tensor(
                            dst[:, c * TOK + g * 256: c * TOK + g * 256 + 256],
                            xg[c][:, :], b2b[:, 0:256], ALU.subtract),
                            reads=["b2b", kx], writes=["lnb"])
                pr.section("DVE", dve_apply)

        HLR = ((0, 1, 2, 3, 4), (10, 11, 6, 12, 13))

        def head_ln_rope(tag, src, src_key, T, gcol, gbkeys, bcol, cosT, sinT,
                         ckey, dst_ap, dst_key, par=0):
            R = HLR[par]
            rk = f"hr{par}"
            pst = "ppstat" if par == 0 else "ppstat1"
            psn = "pp_stat" if par == 0 else "pp_st2"
            pb = 0
            for g in range(T // 512):
                sl512 = g * 512
                sq = sqbuf[:, par * 512:par * 512 + 512]
                sqk = f"sq{par}"
                pr.section("DVE", lambda e, sl512=sl512, sq=sq, sqk=sqk:
                           pr.do(lambda e2: e2.tensor_tensor(
                               sq, src[:, sl512:sl512 + 512],
                               src[:, sl512:sl512 + 512], ALU.mult),
                               reads=[src_key], writes=[sqk]))
                pr.section("PE", lambda e, sl512=sl512, sq=sq, sqk=sqk: (
                    pr.do(lambda e2: e2.matmul(
                        ppd[psn][pb:pb + 1, 0:512], s_ones_c[:, :],
                        src[:, sl512:sl512 + 512], start=True, stop=True),
                        reads=[src_key, "onesc"], writes=[pst]),
                    pr.do(lambda e2: e2.matmul(
                        ppd[psn][pb + 64:pb + 65, 0:512],
                        s_ones_c[:, :], sq, start=True, stop=True),
                        reads=[sqk, "onesc"], writes=[pst])))
                def rows_fn(e):
                    pr.do(lambda e2: e2.tensor_scalar(
                        rowv(R[0], 512), ppd[psn][pb:pb + 1, 0:512],
                        1.0 / 128.0, None, ALU.mult),
                        reads=[pst], writes=[rk])
                    pr.do(lambda e2: e2.tensor_scalar(
                        rowv(R[1], 512), ppd[psn][pb + 64:pb + 65, 0:512],
                        1.0 / 128.0, None, ALU.mult),
                        reads=[pst], writes=[rk])
                    pr.do(lambda e2: e2.tensor_tensor(
                        rowv(R[2], 512), rowv(R[0], 512), rowv(R[0], 512),
                        ALU.mult), reads=[rk], writes=[rk])
                    pr.do(lambda e2: e2.tensor_tensor(
                        rowv(R[1], 512), rowv(R[1], 512), rowv(R[2], 512),
                        ALU.subtract), reads=[rk], writes=[rk])
                pr.section("DVE", rows_fn)
                pr.section("ACT", lambda e: pr.do(
                    lambda e2: e2.activation(rowv(R[2], 512), rowv(R[1], 512),
                                             AF.Sqrt, bias=s_eps[0:1, 0:1]),
                    reads=[rk, "s_eps"], writes=[rk]))
                def rows2_fn(e):
                    pr.do(lambda e2: e2.reciprocal(rowv(R[3], 512),
                                                   rowv(R[2], 512)),
                          reads=[rk], writes=[rk])
                    pr.do(lambda e2: e2.tensor_tensor(
                        rowv(R[4], 512), rowv(R[0], 512), rowv(R[3], 512),
                        ALU.mult), reads=[rk], writes=[rk])
                pr.section("DVE", rows2_fn)
                bcast_row(R[3], 0, b1b[:, sl512:sl512 + 512], "b1b", 512,
                          row_key=rk)
                bcast_row(R[4], 0, b2b[:, sl512:sl512 + 512], "b2b", 512,
                          row_key=rk)
            def apply_fn(e):
                pr.do(lambda e2: e2.tensor_tensor(
                    scrB[:, 0:T], src[:, 0:T], b1b[:, 0:T], ALU.mult),
                    reads=[src_key, "b1b"], writes=["scrB"])
                pr.do(lambda e2: e2.tensor_tensor(
                    scrB[:, 0:T], scrB[:, 0:T], b2b[:, 0:T], ALU.subtract),
                    reads=["scrB", "b2b"], writes=["scrB"])
            pr.section("DVE", apply_fn)
            pr.section("DVE", lambda e: pr.do(
                lambda e2: e2.tensor_scalar(scrB[:, 0:T], scrB[:, 0:T],
                                            gcol[:, 0:1], bcol[:, 0:1],
                                            ALU.mult, ALU.add),
                reads=["scrB"] + list(gbkeys), writes=["scrB"]))
            def rope_fn(e):
                pr.do(lambda e2: e2.tensor_tensor(
                    scrC[:, 0:T], scrB[:, 0:T], cosT, ALU.mult),
                    reads=["scrB", ckey], writes=["scrC"])
                pr.do(lambda e2: e2.tensor_copy(
                    scrD[0:64, 0:T], scrB[64:128, 0:T]),
                    reads=["scrB"], writes=["scrD"])
                pr.do(lambda e2: e2.tensor_copy(
                    scrD[64:128, 0:T], scrB[0:64, 0:T]),
                    reads=["scrB"], writes=["scrD"])
                pr.do(lambda e2: e2.tensor_tensor(
                    scrD[:, 0:T], scrD[:, 0:T], sinT, ALU.mult),
                    reads=["scrD", ckey], writes=["scrD"])
                pr.do(lambda e2: e2.tensor_tensor(
                    dst_ap, scrC[:, 0:T], scrD[:, 0:T], ALU.add),
                    reads=["scrC", "scrD"], writes=[dst_key])
            pr.section("DVE", rope_fn)

        ppd = psum_phase([("pp_stat", [128, 512], F32),
                          ("pp_b", [128, 512], F32),
                          ("pp_a0", [128, 512], F32),
                          ("pp_a1", [128, 512], F32),
                          ("pp_a2", [128, 512], F32),
                          ("pp_a3", [128, 512], F32),
                          ("pp_st2", [128, 512], F32),
                          ("pp_tr", [128, 128], BF16)])
        kpr = sbin("p12", "kpr", [128, HKV * S], BF16)
        vT = sbin("p12", "vT", [128, HKV * 16 * 128], BF16)
        mslot[0] = sbin("p02", "mean_t0", [128, 1024], F32)
        mslot[1] = sbin("p02", "mean_t1", [128, 1024], F32)
        qpr = bigB
        lnb = bigA
        xg = [sbin("p1", f"xg{i}", [128, 256], F32) for i in range(NCO)]
        vbf = sbin("p1", "vbf", [128, 1024], BF16)
        scrA2 = sbin("p1", "scrA2", [128, 1024], F32)
        sqbuf = sbin("p1", "sqbuf", [128, 1024], F32)
        scrAp = [scrA, scrA2]
        scrApk = ["scrA", "scrA2"]

        def qkv_mm(oc, sl, rhs, rhs_key):
            pq = (oc % 2) * 2
            def mm(e):
                for th in range(2):
                    pk = f"pp_a{pq + th}"
                    pr.do_group(
                        [(lambda e2, hc=hc, th=th, pk=pk:
                          e2.matmul(ppd[pk][:, 0:512],
                                    wstr_bf[sl][:, hc * 128:(hc + 1) * 128],
                                    rhs[:, hc * TOK + th * 512:
                                        hc * TOK + th * 512 + 512],
                                    start=(hc == 0), stop=(hc == NCO - 1)))
                         for hc in range(NCO)],
                        reads=[f"wsb{sl}", rhs_key], writes=[pk])
            pr.section("PE", mm)

        def evac_psum(dst, dst_key, oc):
            pq = (oc % 2) * 2
            def ev(e):
                for th in range(2):
                    pr.do(lambda e2, th=th: e2.tensor_copy(
                        dst[:, th * 512:th * 512 + 512],
                        ppd[f"pp_a{pq + th}"][:, 0:512]),
                        reads=[f"pp_a{pq + th}"], writes=[dst_key])
            pr.section("DVE", ev)

        cosq = sbin("p1q", "s_cosq", [128, TOK], F32)
        sinq = sbin("p1q", "s_sinq", [128, TOK], F32)
        pr.section("SP", lambda e: (
            pr.dma(cosq[:, :], cosq_h, writes=["cosq"], q="qcos"),
            pr.dma(sinq[:, :], sinq_h, writes=["cosq"], q="qcos")))
        ln_from_dram("l1q", xq_h, 0, lnb, xg)
        tap("ln1q", lnb[:, :], "lnb")
        load_stripe(0, wqkv_h[0])
        qkv_mm(0, 0, lnb, "lnb")
        load_stripe(1, wqkv_h[1])
        qkv_mm(1, 1, lnb, "lnb")
        for oc in range(16):
            par = oc % 2
            evac_psum(scrAp[par][:, 0:1024], scrApk[par], oc)
            if oc == 0:
                tap("qraw0", scrAp[par][:, 0:1024], scrApk[par])
            if oc + 2 < 16:
                load_stripe(par, wqkv_h[oc + 2])
                qkv_mm(oc + 2, par, lnb, "lnb")
            head_ln_rope(f"q{oc}", scrAp[par][:, 0:1024], scrApk[par], TOK,
                         qng, ("qng", "qnb"), qnb, cosq[:, 0:TOK],
                         sinq[:, 0:TOK], "cosq",
                         qpr[:, oc * TOK:(oc + 1) * TOK], "qpr", par=par)
            mean_emit(8)
        close_scope("p1q")
        pr.barrier()

        cosk = sbin("p1k", "s_cosk", [128, TOK], F32)
        sink = sbin("p1k", "s_sink", [128, TOK], F32)
        for half in range(2):
            def ckload(e, half=half):
                pr.dma(cosk[:, :], cosk_h[:, half * 1024:half * 1024 + 1024],
                       writes=["cosk"], q="qcos")
                pr.dma(sink[:, :], sink_h[:, half * 1024:half * 1024 + 1024],
                       writes=["cosk"], q="qcos")
            pr.section("SP", ckload)
            ln_from_dram(f"l1k{half}", xk_h, half * 1024, lnb, xg)
            load_stripe(0, wqkv_h[16])
            qkv_mm(16, 0, lnb, "lnb")
            load_stripe(1, wqkv_h[17])
            qkv_mm(17, 1, lnb, "lnb")
            for oc in range(16, 24):
                kv = oc - 16 if oc < 20 else oc - 20
                is_k = oc < 20
                par = oc % 2
                evac_psum(scrAp[par][:, 0:1024] if is_k else vbf,
                          scrApk[par] if is_k else "vbf", oc)
                if oc + 2 < 24:
                    load_stripe(par, wqkv_h[oc + 2])
                    qkv_mm(oc + 2, par, lnb, "lnb")
                if is_k:
                    head_ln_rope(f"k{kv}h{half}", scrAp[par][:, 0:1024],
                                 scrApk[par], 1024, kng, ("kng", "knb"), knb,
                                 cosk[:, 0:1024], sink[:, 0:1024],
                                 "cosk",
                                 kpr[:, kv * S + half * 1024:
                                     kv * S + half * 1024 + 1024], "kpr",
                                 par=par)
                else:
                    for tb in range(8):
                        kt = half * 8 + tb
                        pr.section("PE", lambda e, tb=tb: pr.do(
                            lambda e2: e2.transpose(
                                ppd["pp_tr"][:, 0:128],
                                vbf[:, tb * 128:(tb + 1) * 128],
                                s_id_bf[:, :]),
                            reads=["vbf", "idbf"], writes=["pptr"]))
                        pr.section("DVE", lambda e, kv=kv, kt=kt: pr.do(
                            lambda e2: e2.tensor_copy(
                                vT[:, (kv * 16 + kt) * 128:
                                   (kv * 16 + kt) * 128 + 128],
                                ppd["pp_tr"][:, 0:128]),
                            reads=["pptr"], writes=["vT"]))
        tap("qpr", qpr[:, :], "qpr")
        tap("kpr", kpr[:, :], "kpr")
        tap("vT", vT[:, :], "vT")
        close_scope("p1k")
        close_scope("p1")
        pr.barrier()
        if _PHASES <= 1:
            raise _StopBuild

        ppd = psum_phase([("pp_s0", [128, 512], F32),
                          ("pp_s1", [128, 512], F32),
                          ("pp_s2", [128, 512], F32),
                          ("pp_s3", [128, 512], F32),
                          ("pp_v0", [128, 512], F32),
                          ("pp_v1", [128, 512], F32),
                          ("pp_sum", [128, 512], F32),
                          ("pp_b", [128, 512], F32)])
        attn = bigA
        maskt = sbin("p2", "maskt", [128, 16 * 512], BF16)
        cbSs = [sbin("p2", f"cbS{i}", [128, 512], F32) for i in range(2)]
        rbSs = [sbin("p2", f"rbS{i}", [128, 512], F32) for i in range(2)]
        scfp = [sbin("p2", f"scfp{i}", [128, 1024], F32) for i in range(2)]
        ptp2 = [sbin("p2", f"ptp{i}", [128, 1024], BF16) for i in range(2)]
        def mload(e):
            for i in range(16):
                pr.dma(maskt[:, i * 512:(i + 1) * 512], mask_h[i],
                       writes=["maskt"], q="qmask")
        pr.section("SP", mload)
        srtS = sbin("p2", "srtS", [1, 1], F32)
        invd = sbin("p2", "invd", [128, 1], F32)
        pr.section("SP", lambda e: pr.dma(srtS[:, :], srt_t.ap(),
                                          writes=["srtS"], q="qc_srt"))
        pr.section("DVE", lambda e: pr.do(
            lambda e2: e2.reciprocal(rowv(8, 1), srtS[:, :]),
            reads=["srtS"], writes=["rows"]))
        bcast_row(8, 0, invd[:, 0:1], "invd", 1)

        for kv in range(HKV):
            for gp in range(2):
                pr.section("DVE", lambda e, kv=kv, gp=gp: pr.do(
                    lambda e2: e2.tensor_tensor(
                        scrA[:, 0:1024],
                        kpr[:, kv * S + gp * 1024:kv * S + gp * 1024 + 1024],
                        kpr[:, kv * S + gp * 1024:kv * S + gp * 1024 + 1024],
                        ALU.mult),
                    reads=["kpr"], writes=["scrA"]))
                pr.section("PE", lambda e, gp=gp: pr.do_group(
                    [(lambda e2, g=g: e2.matmul(
                        ppd["pp_sum"][g * 32:g * 32 + 1, 0:512],
                        s_ones_c[:, :],
                        scrA[:, g * 512:g * 512 + 512],
                        start=True, stop=True)) for g in range(2)],
                    reads=["scrA", "onesc"], writes=["ppsum"]))
                def nk_rows(e, gp=gp):
                    for g in range(2):
                        pr.do(lambda e2, g=g, gp=gp: e2.tensor_copy(
                            rowv(5, 2048)[0:1, (gp * 2 + g) * 512:
                                          (gp * 2 + g) * 512 + 512],
                            ppd["pp_sum"][g * 32:g * 32 + 1, 0:512]),
                            reads=["ppsum"], writes=["rows"])
                pr.section("DVE", nk_rows)
            pr.section("ACT", lambda e: pr.do(
                lambda e2: e2.activation(rowv(5, 2048), rowv(5, 2048),
                                         AF.Sqrt),
                reads=["rows"], writes=["rows"]))
            pr.section("DVE", lambda e, kv=kv: pr.do(
                lambda e2: e2.reduce_max(
                    rowv(14, 1024)[0:1, kv:kv + 1],
                    rowv(5, 2048), axis=AX.X),
                reads=["rows"], writes=["rows"]))

        for hh in range(H):
            kv = hh // 4
            pr.section("DVE", lambda e, hh=hh: pr.do(
                lambda e2: e2.tensor_tensor(
                    scrA[:, 0:TOK], qpr[:, hh * TOK:(hh + 1) * TOK],
                    qpr[:, hh * TOK:(hh + 1) * TOK], ALU.mult),
                reads=["qpr"], writes=["scrA"]))
            pr.section("PE", lambda e: pr.do_group(
                [(lambda e2, g=g: e2.matmul(
                    ppd["pp_sum"][g * 32:g * 32 + 1, 0:512], s_ones_c[:, :],
                    scrA[:, g * 512:g * 512 + 512], start=True, stop=True))
                 for g in range(2)],
                reads=["scrA", "onesc"], writes=["ppsum"]))
            def nq_rows(e):
                for g in range(2):
                    pr.do(lambda e2, g=g: e2.tensor_copy(
                        rowv(6, TOK)[0:1, g * 512:g * 512 + 512],
                        ppd["pp_sum"][g * 32:g * 32 + 1, 0:512]),
                        reads=["ppsum"], writes=["rows"])
            pr.section("DVE", nq_rows)
            pr.section("ACT", lambda e: pr.do(
                lambda e2: e2.activation(rowv(6, TOK), rowv(6, TOK), AF.Sqrt),
                reads=["rows"], writes=["rows"]))
            pr.section("DVE", lambda e, kv=kv: pr.do(
                lambda e2: e2.tensor_scalar(
                    rowv(7, TOK), rowv(6, TOK),
                    rowv(14, 1024)[0:1, kv:kv + 1],
                    INV_SQRT_D, ALU.mult, ALU.mult),
                reads=["rows", "rows"], writes=["rows"]))
            for qb in range(2):
                cbi = (hh * 2 + qb) % 2
                cb = cbSs[cbi]
                cbk = f"cbS{cbi}"
                rb = rbSs[cbi]
                rbk = f"rbS{cbi}"
                bcast_row(7, qb * 512, cb[:, :], cbk, 512)
                nkt = KT_OF_QB[qb]
                pvk = f"pp_v{qb}"
                def sc_mm(kt, hh=hh, kv=kv, qb=qb):
                    sps = f"pp_s{kt % 4}"
                    pr.section("PE", lambda e, kt=kt, sps=sps: pr.do(
                        lambda e2: e2.matmul(
                            ppd[sps][:, 0:512],
                            kpr[:, kv * S + kt * 128:kv * S + kt * 128 + 128],
                            qpr[:, hh * TOK + qb * 512:
                                hh * TOK + qb * 512 + 512],
                            start=True, stop=True),
                        reads=["kpr", "qpr"], writes=[sps]))
                sc_mm(0)
                sc_mm(1)
                for kp in range(nkt // 2):
                    k0 = 2 * kp
                    pb = kp % 2
                    sfp = scfp[pb]
                    sfpk = f"scfp{pb}"
                    ptp = ptp2[pb]
                    ptpk = f"ptp{pb}"
                    if k0 + 2 < nkt:
                        sc_mm(k0 + 2)
                        sc_mm(k0 + 3)
                    def stt2(e, sfp=sfp, sfpk=sfpk, cb=cb, cbk=cbk, k0=k0):
                        for j in range(2):
                            sps = f"pp_s{(k0 + j) % 4}"
                            pr.do(lambda e2, j=j, sps=sps:
                                  e2.scalar_tensor_tensor(
                                      sfp[:, j * 512:j * 512 + 512],
                                      ppd[sps][:, 0:512], invd[:, 0:1],
                                      cb[:, :], ALU.mult, ALU.subtract),
                                  reads=[sps, cbk, "invd"], writes=[sfpk])
                    pr.section("DVE", stt2)
                    pr.section("ACT", lambda e, sfp=sfp, sfpk=sfpk, ptp=ptp,
                               ptpk=ptpk: pr.do(
                        lambda e2: e2.activation(ptp[:, :], sfp[:, :],
                                                 AF.Exp),
                        reads=[sfpk], writes=[ptpk]))
                    if (qb, k0) in MASK_IDX:
                        mi0 = MASK_IDX[(qb, k0)]
                        pr.section("DVE", lambda e, ptp=ptp, ptpk=ptpk,
                                   mi0=mi0: pr.do(
                            lambda e2: e2.tensor_tensor(
                                ptp[:, :], ptp[:, :],
                                maskt[:, mi0 * 512:mi0 * 512 + 1024],
                                ALU.mult),
                            reads=["maskt", ptpk], writes=[ptpk]))
                    def pe_pv2(e, k0=k0, kv=kv, ptp=ptp, ptpk=ptpk, pvk=pvk,
                               nkt=nkt, qb=qb):
                        for j in range(2):
                            kt = k0 + j
                            pr.do(lambda e2, j=j, kt=kt: e2.matmul(
                                ppd["pp_sum"][qb * 64:qb * 64 + 1, 0:512],
                                onescol_bf[:, :],
                                ptp[:, j * 512:j * 512 + 512],
                                start=(kt == 0), stop=(kt == nkt - 1)),
                                reads=[ptpk, "onescbf"], writes=["ppsum"])
                            pr.do(lambda e2, j=j, kt=kt: e2.matmul(
                                ppd[pvk][:, 0:512],
                                vT[:, (kv * 16 + kt) * 128:
                                   (kv * 16 + kt) * 128 + 128],
                                ptp[:, j * 512:j * 512 + 512],
                                start=(kt == 0), stop=(kt == nkt - 1)),
                                reads=[ptpk, "vT"], writes=[pvk])
                    pr.section("PE", pe_pv2)
                def fin(e, qb=qb):
                    pr.do(lambda e2: e2.tensor_copy(
                        rowv(8, 512),
                        ppd["pp_sum"][qb * 64:qb * 64 + 1, 0:512]),
                        reads=["ppsum"], writes=["rows"])
                    pr.do(lambda e2: e2.reciprocal(rowv(9, 512), rowv(8, 512)),
                          reads=["rows"], writes=["rows"])
                pr.section("DVE", fin)
                bcast_row(9, 0, rb[:, :], rbk, 512)
                pr.section("DVE", lambda e, hh=hh, qb=qb, pvk=pvk, rb=rb,
                           rbk=rbk: pr.do(
                    lambda e2: e2.tensor_tensor(
                        attn[:, hh * TOK + qb * 512:
                             hh * TOK + qb * 512 + 512],
                        ppd[pvk][:, 0:512], rb[:, :], ALU.mult),
                    reads=[pvk, rbk], writes=["attn"]))
                mean_emit(8)
        mean_emit(len(mean_jobs))
        tap("attn", attn[:, :], "attn")
        close_scope("p2")
        close_scope("p02")
        close_scope("p12")
        pr.barrier()
        if _PHASES <= 2:
            raise _StopBuild

        ppd = psum_phase([(f"pp_w{i}", [128, 512], F32) for i in range(6)]
                         + [("pp_stat", [128, 512], F32),
                            ("pp_b", [128, 512], F32)])
        scrKI = sbin("rest", "scrKI", [128, 1024], I32)
        xst = [sbin("rest", f"xst{i}", [128, TOK], F32) for i in range(2)]
        wo_out = bigB

        for og in range(6):
            ocs = list(range(og * 3, min(og * 3 + 3, NCO)))
            if not ocs:
                continue
            for dc in range(NCO):
                sl = dc % 2
                def spf(e, dc=dc, sl=sl, ocs=ocs):
                    pr.dma(wstr[sl][:, 0:len(ocs) * 128],
                           wo_h[dc][:, ocs[0] * 128:(ocs[-1] + 1) * 128],
                           writes=[f"wsf{sl}"], q=f"qw{sl}")
                pr.section("SP", spf)
                pr.section("DVE", lambda e, sl=sl, n=len(ocs): pr.do(
                    lambda e2: e2.tensor_copy(wstr_bf[sl][:, 0:n * 128],
                                              wstr[sl][:, 0:n * 128]),
                    reads=[f"wsf{sl}"], writes=[f"wsb{sl}"]))
                def mmw(e, dc=dc, sl=sl, ocs=ocs):
                    builds = []
                    for j in range(len(ocs)):
                        for th in range(2):
                            builds.append(
                                (lambda e2, j=j, th=th, dc=dc, sl=sl:
                                 e2.matmul(
                                     ppd[f"pp_w{j * 2 + th}"][:, 0:512],
                                     wstr_bf[sl][:, j * 128:(j + 1) * 128],
                                     attn[:, dc * TOK + th * 512:
                                          dc * TOK + th * 512 + 512],
                                     start=(dc == 0), stop=(dc == NCO - 1))))
                    pr.do_group(builds, reads=[f"wsb{sl}", "attn"],
                                writes=["ppw"])
                pr.section("PE", mmw)
            def evacw(e, ocs=ocs):
                for j, oc in enumerate(ocs):
                    for th in range(2):
                        pr.do(lambda e2, j=j, oc=oc, th=th: e2.tensor_copy(
                            wo_out[:, oc * TOK + th * 512:
                                   oc * TOK + th * 512 + 512],
                            ppd[f"pp_w{j * 2 + th}"][:, 0:512]),
                            reads=["ppw"], writes=["wo_out"])
            pr.section("DVE", evacw)

        tap("wo_out", wo_out[:, :], "wo_out")
        for g in range(2):
            stats2048("ln2g%d" % g,
                      lambda c, g=g: wo_out[:, c * TOK + g * 512:
                                            c * TOK + g * 512 + 512],
                      g * 512, 512, "wo_out", bf_src=True)
        for c in range(NCO):
            sl = c % 2
            pr.section("SP", lambda e, c=c, sl=sl: pr.dma(
                xst[sl][:, :], xq_h[c * 128:(c + 1) * 128, :],
                writes=[f"xst{sl}"], q=f"qx{sl}"))
            def ln2_apply(e, c=c, sl=sl):
                pr.do(lambda e2: e2.tensor_tensor(
                    scrB[:, 0:TOK], wo_out[:, c * TOK:(c + 1) * TOK],
                    b1b[:, 0:TOK], ALU.mult),
                    reads=["wo_out", "b1b"], writes=["scrB"])
                pr.do(lambda e2: e2.tensor_tensor(
                    scrB[:, 0:TOK], scrB[:, 0:TOK], b2b[:, 0:TOK],
                    ALU.subtract),
                    reads=["scrB", "b2b"], writes=["scrB"])
                pr.do(lambda e2: e2.tensor_tensor(
                    xst[sl][:, :], xst[sl][:, :], scrB[:, 0:TOK], ALU.add),
                    reads=["scrB", f"xst{sl}"], writes=[f"xst{sl}"])
            pr.section("DVE", ln2_apply)
            pr.section("POOL", lambda e, c=c, sl=sl: pr.dma(
                x1_dram[c], xst[sl][:, :], reads=[f"xst{sl}"],
                writes=[f"x1d{c}"], q=f"qsx{sl}"))
        pr.barrier()
        if _PHASES <= 3:
            raise _StopBuild

        ppd = psum_phase([("pp_stat", [128, 512], F32),
                          ("pp_b", [128, 512], F32),
                          ("pp_t", [128, 512], F32)])
        xqi = bigB
        bgB = sbin("p45", "bgB", [128, TOK], F32)
        buB = sbin("p45", "buB", [128, TOK], F32)
        s1B = sbin("p4s", "s1B", [128, TOK], F32)

        def mrow_ap(mi):
            return rowv(15, 1024)[0:1, mi:mi + 1]

        pr.section("PE", lambda e: pr.do_group(
            [(lambda e2, mi=mi: e2.matmul(
                ppd["pp_stat"][mi * 32:mi * 32 + 1, 0:1], s_ones_c[:, :],
                mean_acc[:, mi:mi + 1], start=True, stop=True))
             for mi in range(3)],
            reads=["mean_acc", "onesc"], writes=["ppstat"]))
        def mrows(e):
            for mi in range(3):
                pr.do(lambda e2, mi=mi: e2.tensor_scalar(
                    mrow_ap(mi), ppd["pp_stat"][mi * 32:mi * 32 + 1, 0:1],
                    INV_N_GU, 1e-5, ALU.mult, ALU.max),
                    reads=["ppstat"], writes=["rows"])
            for mi in range(3):
                pr.do(lambda e2, mi=mi: e2.tensor_scalar(
                    rowv(15, 1024)[0:1, 4 + mi:5 + mi],
                    mrow_ap(mi), 0.5, None, ALU.mult),
                    reads=["rows"], writes=["rows"])
        pr.section("DVE", mrows)
        for mi in range(3):
            bcast_row(15, 4 + mi, colv(mi), "cols", 1)
        pr.section("DVE", lambda e: pr.do(
            lambda e2: e2.tensor_scalar(cols[:, 4:7], cols[:, 0:3], -1.0,
                                        None, ALU.mult),
            reads=["cols"], writes=["cols"]))
        tap("cols", cols[:, 0:8], "cols")
        tap("mean_acc", mean_acc[:, :], "mean_acc")

        for g in range(2):
            def l3load(c, g=g):
                sl = c % 2
                pr.section("SP", lambda e, c=c, sl=sl, g=g: pr.dma(
                    xst[sl][:, 0:512],
                    x1_dram[c][:, g * 512:g * 512 + 512],
                    reads=[f"x1d{c}"], writes=[f"xst{sl}"], q=f"qx{sl}"))
            stats2048("ln3g%d" % g,
                      lambda c: xst[c % 2][:, 0:512],
                      g * 512, 512, lambda c: f"xst{c % 2}",
                      chunk_load=l3load)
        def amax_init(e):
            pr.do(lambda e2: e2.memset(scrD[:, 0:TOK], -3.0e38),
                  writes=["scrD"])
            pr.do(lambda e2: e2.memset(scrC[:, 0:TOK], 3.0e38),
                  writes=["scrC"])
        pr.section("DVE", amax_init)
        for c in range(NCO):
            sl = c % 2
            pr.section("SP", lambda e, c=c, sl=sl: pr.dma(
                xst[sl][:, :], x1_dram[c], reads=[f"x1d{c}"],
                writes=[f"xst{sl}"], q=f"qx{sl}"))
            def ln3_chunks(e, c=c, sl=sl):
                pr.do(lambda e2: e2.tensor_tensor(
                    scrB[:, 0:TOK], xst[sl][:, :],
                    b1b[:, 0:TOK], ALU.mult),
                    reads=[f"xst{sl}", "b1b"], writes=["scrB"])
                pr.do(lambda e2: e2.tensor_tensor(
                    scrB[:, 0:TOK], scrB[:, 0:TOK],
                    b2b[:, 0:TOK], ALU.subtract),
                    reads=["scrB", "b2b"], writes=["scrB"])
                pr.do(lambda e2: e2.tensor_tensor(
                    scrD[:, 0:TOK], scrD[:, 0:TOK],
                    scrB[:, 0:TOK], ALU.max),
                    reads=["scrB", "scrD"], writes=["scrD"])
                pr.do(lambda e2: e2.tensor_tensor(
                    scrC[:, 0:TOK], scrC[:, 0:TOK],
                    scrB[:, 0:TOK], ALU.min),
                    reads=["scrB", "scrC"], writes=["scrC"])
            pr.section("DVE", ln3_chunks)

        def tokmax(src_key, dst_row_i):
            for tb in range(8):
                pr.section("PE", lambda e, tb=tb: pr.do(
                    lambda e2: e2.transpose(
                        ppd["pp_t"][:, 0:128],
                        scrD[:, tb * 128:(tb + 1) * 128], s_id_f[:, :]),
                    reads=[src_key, "idf"], writes=["ppt"]))
                pr.section("DVE", lambda e: pr.do(
                    lambda e2: e2.reduce_max(colv(10), ppd["pp_t"][:, 0:128],
                                             axis=AX.X),
                    reads=["ppt"], writes=["cols"]))
                pr.section("PE", lambda e: pr.do(
                    lambda e2: e2.transpose(ppd["pp_t"][0:1, 128:256],
                                            colv(10), s_id_f[:, :]),
                    reads=["cols", "idf"], writes=["ppt"]))
                pr.section("DVE", lambda e, tb=tb, dst_row_i=dst_row_i: pr.do(
                    lambda e2: e2.tensor_copy(
                        rowv(dst_row_i, TOK)[0:1, tb * 128:(tb + 1) * 128],
                        ppd["pp_t"][0:1, 128:256]),
                    reads=["ppt"], writes=["rows"]))
        def amax_comb(e):
            pr.do(lambda e2: e2.tensor_scalar(
                scrC[:, 0:TOK], scrC[:, 0:TOK], -1.0, None, ALU.mult),
                reads=["scrC"], writes=["scrC"])
            pr.do(lambda e2: e2.tensor_tensor(
                scrD[:, 0:TOK], scrD[:, 0:TOK], scrC[:, 0:TOK], ALU.max),
                reads=["scrC", "scrD"], writes=["scrD"])
        pr.section("DVE", amax_comb)
        tokmax("scrD", 10)
        def s1_rows(e):
            pr.do(lambda e2: e2.tensor_scalar(rowv(10, TOK), rowv(10, TOK),
                                              1e-5, None, ALU.max),
                  reads=["rows"], writes=["rows"])
            pr.do(lambda e2: e2.reciprocal(rowv(11, TOK), rowv(10, TOK)),
                  reads=["rows"], writes=["rows"])
            pr.do(lambda e2: e2.tensor_scalar(rowv(11, TOK), rowv(11, TOK),
                                              128.0, None, ALU.mult),
                  reads=["rows"], writes=["rows"])
            pr.do(lambda e2: e2.tensor_scalar(rowv(12, TOK), rowv(10, TOK),
                                              mrow_ap(0), 1.0 / 128.0,
                                              ALU.mult, ALU.mult),
                  reads=["rows", "rows"], writes=["rows"])
            pr.do(lambda e2: e2.tensor_scalar(rowv(13, TOK), rowv(10, TOK),
                                              mrow_ap(1), 1.0 / 128.0,
                                              ALU.mult, ALU.mult),
                  reads=["rows", "rows"], writes=["rows"])
        pr.section("DVE", s1_rows)
        for ri, dst, dkey in ((11, s1B, "s1B"), (12, bgB, "bgB"),
                              (13, buB, "buB")):
            for g in range(2):
                bcast_row(ri, g * 512, dst[:, g * 512:g * 512 + 512],
                          dkey, 512)
        for c in range(NCO):
            sl = c % 2
            pr.section("SP", lambda e, c=c, sl=sl: pr.dma(
                xst[sl][:, :], x1_dram[c], reads=[f"x1d{c}"],
                writes=[f"xst{sl}"], q=f"qx{sl}"))
            def qv_fn(e, c=c, sl=sl):
                pr.do(lambda e2: e2.tensor_tensor(
                    scrB[:, 0:TOK], xst[sl][:, :],
                    b1b[:, 0:TOK], ALU.mult),
                    reads=[f"xst{sl}", "b1b"], writes=["scrB"])
                pr.do(lambda e2: e2.tensor_tensor(
                    scrB[:, 0:TOK], scrB[:, 0:TOK], b2b[:, 0:TOK],
                    ALU.subtract),
                    reads=["scrB", "b2b"], writes=["scrB"])
                pr.do(lambda e2: e2.tensor_tensor(
                    scrB[:, 0:TOK], scrB[:, 0:TOK], s1B[:, :], ALU.mult),
                    reads=["scrB", "s1B"], writes=["scrB"])
            pr.section("DVE", qv_fn)
            round_exact(xqi[:, c * TOK:(c + 1) * TOK], "xqi",
                        scrB[:, 0:TOK], "scrB", TOK)
        tap("xqi", xqi[:, :], "xqi")
        tap("s1B", s1B[:, :], "s1B")
        tap("bgB", bgB[:, :], "bgB")
        close_scope("p4s")
        pr.barrier()
        if _PHASES <= 4:
            raise _StopBuild

        ppd = psum_phase([("pp_g0", [128, 512], F32),
                          ("pp_g1", [128, 512], F32),
                          ("pp_u0", [128, 512], F32),
                          ("pp_u1", [128, 512], F32),
                          ("pp_z0", [128, 512], F32),
                          ("pp_z1", [128, 512], F32)])
        utb = [bigA[:, 0:2048], bigA[:, 2048:4096]]
        gtb = [bigA[:, 4096:6144], bigA[:, 6144:8192]]
        islotp = [bigA[:, 8192:9216], bigA[:, 13312:14336]]
        zslot = [bigA[:, 9216:10240], bigA[:, 10240:11264]]
        dtb = bigA[:, 11264:13312]
        wdf = sbin("p45", "wdf", [128, HID], F32)

        def ternarize(src, src_key, dst, dst_key, mi):
            def fn(e):
                pr.do(lambda e2: e2.tensor_scalar(
                    dst[:, :], src[:, :], colv(4 + mi), None, ALU.is_lt),
                    reads=[src_key, "cols"], writes=[dst_key])
                pr.do(lambda e2: e2.scalar_tensor_tensor(
                    dst[:, :], src[:, :], colv(mi), dst[:, :], ALU.is_gt,
                    ALU.subtract),
                    reads=[src_key, "cols", dst_key], writes=[dst_key])
            pr.section("DVE", fn)

        def ldgu(oc):
            def sp(e, oc=oc):
                pr.dma(wstr[0][:, :], wg_h[oc], writes=["wsf0"], q="qw0")
                pr.dma(wstr[1][:, :], wu_h[oc], writes=["wsf1"], q="qw1")
            pr.section("SP", sp)

        def mmz_blk(oc):
            isl = islotp[oc % 2]
            ilk = f"islot{oc % 2}"
            def mmz(e, isl=isl, ilk=ilk):
                for th in range(2):
                    pr.do_group([
                        lambda e2, th=th: e2.matmul(
                            ppd[f"pp_z{th}"][:, 0:512], s_h128[:, :],
                            isl[:, th * 512:th * 512 + 512],
                            start=True, stop=True)],
                        reads=[ilk, "h128"], writes=[f"pp_z{th}"])
            pr.section("PE", mmz)
            zs = zslot[oc % 2]
            zk = f"zs{oc % 2}"
            def evz(e, zs=zs, zk=zk):
                for th in range(2):
                    pr.do(lambda e2, th=th: e2.activation(
                        zs[:, th * 512:th * 512 + 512],
                        ppd[f"pp_z{th}"][:, 0:512], AF.Copy),
                        reads=[f"pp_z{th}"], writes=[zk])
            pr.section("ACT", evz)
            pr.section("POOL", lambda e, oc=oc, zs=zs, zk=zk: pr.dma(
                z_dram[oc], zs[:, :], reads=[zk], writes=[f"zd{oc}"],
                q=f"qsz{oc % 2}"))

        ldgu(0)
        for oc in range(NIC):
            sl = oc % 2
            pr.section("SP", lambda e, oc=oc: pr.dma(
                wdf[:, :], wd_h[oc], writes=["wdf"], q="qwd"))
            ternarize(wstr[0], "wsf0", gtb[sl], f"gtb{sl}", 0)
            ternarize(wstr[1], "wsf1", utb[sl], f"utb{sl}", 1)
            if oc + 1 < NIC:
                ldgu(oc + 1)
            def mmgu(e, oc=oc, sl=sl):
                for th in range(2):
                    for mat, tb in (("g", gtb[sl]), ("u", utb[sl])):
                        pk = f"pp_{mat}{th}"
                        pr.do_group(
                            [(lambda e2, hc=hc, th=th, tb=tb, pk=pk:
                              e2.matmul(ppd[pk][:, 0:512],
                                        tb[:, hc * 128:(hc + 1) * 128],
                                        xqi[:, hc * TOK + th * 512:
                                            hc * TOK + th * 512 + 512],
                                        start=(hc == 0),
                                        stop=(hc == NCO - 1)))
                             for hc in range(NCO)],
                            reads=[("gtb%d" if mat == "g" else "utb%d") % sl,
                                   "xqi"], writes=[pk])
            pr.section("PE", mmgu)
            def evgu(e):
                for th in range(2):
                    pr.do(lambda e2, th=th: e2.tensor_tensor(
                        scrB[:, th * 512:th * 512 + 512],
                        ppd[f"pp_g{th}"][:, 0:512],
                        bgB[:, th * 512:th * 512 + 512], ALU.mult),
                        reads=[f"pp_g{th}", "bgB"], writes=["scrB"])
                    pr.do(lambda e2, th=th: e2.tensor_tensor(
                        scrC[:, th * 512:th * 512 + 512],
                        ppd[f"pp_u{th}"][:, 0:512],
                        buB[:, th * 512:th * 512 + 512], ALU.mult),
                        reads=[f"pp_u{th}", "buB"], writes=["scrC"])
            pr.section("DVE", evgu)
            pr.section("ACT", lambda e: pr.do(
                lambda e2: e2.activation(scrD[:, 0:TOK], scrB[:, 0:TOK],
                                         AF.Silu),
                reads=["scrB"], writes=["scrD"]))
            isl = islotp[oc % 2]
            ilk = f"islot{oc % 2}"
            pr.section("DVE", lambda e, isl=isl, ilk=ilk: pr.do(
                lambda e2: e2.tensor_tensor(isl[:, 0:TOK], scrD[:, 0:TOK],
                                            scrC[:, 0:TOK], ALU.mult),
                reads=["scrD", "scrC"], writes=[ilk]))
            mmz_blk(oc)
            ternarize(wdf, "wdf", dtb, "dtb", 2)
            pr.section("POOL", lambda e, oc=oc: pr.dma(
                trd_dram[oc], dtb[:, :], reads=["dtb"], writes=[f"td{oc}"],
                q="qstd"))
        close_scope("p45")
        pr.barrier()
        if _PHASES <= 5:
            raise _StopBuild

        ppd = psum_phase([("pp_y0", [128, 512], F32),
                          ("pp_y1", [128, 512], F32),
                          ("pp_t", [128, 512], F32),
                          ("pp_b", [128, 512], F32)])
        zt = [bigA[:, 0:1024], bigA[:, 1024:2048]]
        ys = [bigA[:, 2048:3072], bigA[:, 3072:4096]]
        zv3 = z_dram.rearrange("a b t -> b a t")
        def amax_init(e):
            pr.do(lambda e2: e2.memset(scrD[:, 0:TOK], -3.0e38),
                  writes=["scrD"])
            pr.do(lambda e2: e2.memset(scrC[:, 0:TOK], 3.0e38),
                  writes=["scrC"])
        pr.section("DVE", amax_init)
        for c in range(NIC):
            sl = c % 2
            def zl(e, c=c, sl=sl):
                if _P6SUB & 1:
                    pr.dma(zt[sl][0:64, :], zv3[2 * c], writes=[f"zt{sl}"],
                           q=f"qz{sl}")
                    pr.dma(zt[sl][64:128, :], zv3[2 * c + 1],
                           writes=[f"zt{sl}"], q=f"qz{sl}")
            pr.section("SP", zl)
            if not (_P6SUB & 1):
                pr.section("DVE", lambda e, sl=sl: pr.do(
                    lambda e2: e2.memset(zt[sl][:, :], 0.0),
                    writes=[f"zt{sl}"]))
            if not (_P6SUB & 2):
                continue
            def mmy(e, c=c, sl=sl):
                for th in range(2):
                    pr.do_group([
                        lambda e2, th=th: e2.matmul(
                            ppd[f"pp_y{th}"][:, 0:512], s_h64b[:, :],
                            zt[sl][:, th * 512:th * 512 + 512],
                            start=True, stop=True)],
                        reads=[f"zt{sl}", "h64b"], writes=[f"pp_y{th}"])
            pr.section("PE", mmy)
            def evy(e, sl=sl):
                for th in range(2):
                    pr.do(lambda e2, th=th: e2.activation(
                        ys[sl][:, th * 512:th * 512 + 512],
                        ppd[f"pp_y{th}"][:, 0:512], AF.Copy),
                        reads=[f"pp_y{th}"], writes=[f"ys{sl}"])
            pr.section("ACT", evy)
            def mx(e, sl=sl):
                for th in range(2):
                    pr.do(lambda e2, th=th: e2.tensor_tensor(
                        scrD[:, th * 512:th * 512 + 512],
                        scrD[:, th * 512:th * 512 + 512],
                        ppd[f"pp_y{th}"][:, 0:512], ALU.max),
                        reads=[f"pp_y{th}", "scrD", f"ys{sl}"],
                        writes=["scrD"])
                    pr.do(lambda e2, th=th: e2.tensor_tensor(
                        scrC[:, th * 512:th * 512 + 512],
                        scrC[:, th * 512:th * 512 + 512],
                        ppd[f"pp_y{th}"][:, 0:512], ALU.min),
                        reads=[f"pp_y{th}", "scrC"], writes=["scrC"])
            pr.section("DVE", mx)
            pr.section("POOL", lambda e, c=c, sl=sl: pr.dma(
                yv_dram[c], ys[sl][:, :], reads=[f"ys{sl}"],
                writes=[f"yv{c}"], q=f"qsy{sl}"))
        if not (_P6SUB & 4):
            raise _StopBuild
        def ymax_comb(e):
            pr.do(lambda e2: e2.tensor_scalar(
                scrC[:, 0:TOK], scrC[:, 0:TOK], -1.0, None, ALU.mult),
                reads=["scrC"], writes=["scrC"])
            pr.do(lambda e2: e2.tensor_tensor(
                scrD[:, 0:TOK], scrD[:, 0:TOK], scrC[:, 0:TOK], ALU.max),
                reads=["scrC", "scrD"], writes=["scrD"])
        pr.section("DVE", ymax_comb)
        tokmax("scrD", 10)
        qmB = sbin("p67", "qmB", [128, TOK], F32)
        bdB = sbin("p67", "bdB", [128, TOK], F32)
        def s2_rows(e):
            pr.do(lambda e2: e2.tensor_scalar(rowv(10, TOK), rowv(10, TOK),
                                              FWHT_NORM, 1e-5, ALU.mult,
                                              ALU.max),
                  reads=["rows"], writes=["rows"])
            pr.do(lambda e2: e2.reciprocal(rowv(11, TOK), rowv(10, TOK)),
                  reads=["rows"], writes=["rows"])
            pr.do(lambda e2: e2.tensor_scalar(rowv(11, TOK), rowv(11, TOK),
                                              128.0 * FWHT_NORM, None,
                                              ALU.mult),
                  reads=["rows"], writes=["rows"])
            pr.do(lambda e2: e2.tensor_scalar(rowv(12, TOK), rowv(10, TOK),
                                              mrow_ap(2), 1.0 / 128.0,
                                              ALU.mult, ALU.mult),
                  reads=["rows", "rows"], writes=["rows"])
        pr.section("DVE", s2_rows)
        for ri, dst, dkey in ((11, qmB, "qmB"), (12, bdB, "bdB")):
            for g in range(2):
                bcast_row(ri, g * 512, dst[:, g * 512:g * 512 + 512],
                          dkey, 512)
        pr.barrier()
        if _PHASES <= 6:
            raise _StopBuild

        ppd = psum_phase([(f"pp_d{i}", [128, 512], F32) for i in range(8)])
        d_out = bigB
        aqs = [bigA[:, 0:1024], bigA[:, 1024:2048]]
        tds = [bigA[:, 2048:3072], bigA[:, 3072:4096]]
        zt_p7 = [bigA[:, 4096:5120], bigA[:, 5120:6144]]

        for oh in range(2):
            for th in range(2):
                first = (oh == 0 and th == 0)
                for c in range(NIC):
                    sl = c % 2
                    def spl(e, c=c, sl=sl, oh=oh, th=th, first=first):
                        pr.dma(tds[sl][:, :],
                               trd_dram[c][:, oh * 1024:oh * 1024 + 1024],
                               writes=[f"tds{sl}"], q=f"qt{sl}")
                        if first:
                            pr.dma(zt_p7[sl][:, :], yv_dram[c],
                                   writes=[f"ybp{sl}"], q=f"qy{sl}")
                        else:
                            pr.dma(aqs[sl][:, 0:512],
                                   aq2_dram[c][:, th * 512:th * 512 + 512],
                                   reads=[f"aqd{c}"], writes=[f"aqs{sl}"],
                                   q=f"qa{sl}")
                    pr.section("SP", spl)
                    if first:
                        pr.section("DVE", lambda e, sl=sl: pr.do(
                            lambda e2: e2.tensor_tensor(
                                scrB[:, 0:TOK], zt_p7[sl][:, :], qmB[:, :],
                                ALU.mult),
                            reads=[f"ybp{sl}", "qmB"], writes=["scrB"]))
                        if c == 0:
                            tap("qv_c0", scrB[:, 0:TOK], "scrB")
                            tap("yb_c0", zt_p7[sl][:, :], f"ybp{sl}")
                            tap("qmB", qmB[:, :], "qmB")
                        round_exact(aqs[sl][:, 0:TOK], f"aqs{sl}",
                                    scrB[:, 0:TOK], "scrB", TOK,
                                    dbg=("r7" if (_DEBUG and c == 0)
                                         else None))
                        pr.section("POOL", lambda e, c=c, sl=sl: pr.dma(
                            aq2_dram[c], aqs[sl][:, :], reads=[f"aqs{sl}"],
                            writes=[f"aqd{c}"], q=f"qsa{sl}"))
                    def mmd(e, c=c, sl=sl, th=th, first=first):
                        rhs = aqs[sl][:, th * 512:th * 512 + 512] if first \
                            else aqs[sl][:, 0:512]
                        builds = []
                        for j in range(8):
                            builds.append(
                                (lambda e2, j=j, sl=sl, rhs=rhs:
                                 e2.matmul(ppd[f"pp_d{j}"][:, 0:512],
                                           tds[sl][:, j * 128:(j + 1) * 128],
                                           rhs, start=(c == 0),
                                           stop=(c == NIC - 1))))
                        pr.do_group(builds, reads=[f"tds{sl}", f"aqs{sl}"],
                                    writes=["ppd"])
                    pr.section("PE", mmd)
                def evd(e, oh=oh, th=th):
                    for j in range(8):
                        oc = oh * 8 + j
                        pr.do(lambda e2, j=j, oc=oc, th=th: e2.tensor_tensor(
                            d_out[:, oc * TOK + th * 512:
                                  oc * TOK + th * 512 + 512],
                            ppd[f"pp_d{j}"][:, 0:512],
                            bdB[:, th * 512:th * 512 + 512], ALU.mult),
                            reads=["ppd", "bdB"], writes=["d_out"])
                pr.section("DVE", evd)
        close_scope("p67")
        pr.barrier()
        if _PHASES <= 7:
            raise _StopBuild

        ppd = psum_phase([("pp_stat", [128, 512], F32),
                          ("pp_b", [128, 512], F32)])
        for g in range(2):
            stats2048("ln4g%d" % g,
                      lambda c, g=g: d_out[:, c * TOK + g * 512:
                                           c * TOK + g * 512 + 512],
                      g * 512, 512, "d_out", bf_src=True)
        for c in range(NCO):
            sl = c % 2
            pr.section("SP", lambda e, c=c, sl=sl: pr.dma(
                xst[sl][:, :], x1_dram[c], reads=[f"x1d{c}"],
                writes=[f"xst{sl}"], q=f"qx{sl}"))
            def ln4_apply(e, c=c, sl=sl):
                pr.do(lambda e2: e2.tensor_tensor(
                    scrB[:, 0:TOK], d_out[:, c * TOK:(c + 1) * TOK],
                    b1b[:, 0:TOK], ALU.mult),
                    reads=["d_out", "b1b"], writes=["scrB"])
                pr.do(lambda e2: e2.tensor_tensor(
                    scrB[:, 0:TOK], scrB[:, 0:TOK], b2b[:, 0:TOK],
                    ALU.subtract),
                    reads=["scrB", "b2b"], writes=["scrB"])
                pr.do(lambda e2: e2.tensor_tensor(
                    xst[sl][:, :], xst[sl][:, :], scrB[:, 0:TOK], ALU.add),
                    reads=["scrB", f"xst{sl}"], writes=[f"xst{sl}"])
            pr.section("DVE", ln4_apply)
            pr.section("POOL", lambda e, c=c, sl=sl: pr.dma(
                y_h[c * 128:(c + 1) * 128, :], xst[sl][:, :],
                reads=[f"xst{sl}"], q=f"qso{sl}"))
        t0, t1 = pr.cnt.get("qso0", 0), pr.cnt.get("qso1", 0)
        def final_wait(e):
            pr._wait("qso0", t0)
            pr._wait("qso1", t1)
        pr.section("POOL", final_wait)
    except _StopBuild:
        pass
    return nc


def _stage(inputs):
    x = np.asarray(inputs["L0_attn_x"], dtype=np.float32)
    wqkv = _stage_stripes(np.asarray(inputs["L0_attn_w_qkv"], np.float32), 24)
    wo = np.ascontiguousarray(
        np.asarray(inputs["L0_attn_w_o"], np.float32).T).reshape(16, 128, HID)
    wg = _stage_stripes(np.asarray(inputs["L0_ffn_w_gate"], np.float32), NIC)
    wu = _stage_stripes(np.asarray(inputs["L0_ffn_w_up"], np.float32), NIC)
    wdT = np.ascontiguousarray(np.asarray(inputs["L0_ffn_w_down"],
                                          np.float32).T)
    perm = np.empty(IFF, dtype=np.int64)
    for c in range(NIC):
        for p in range(128):
            perm[c * 128 + p] = (p % 64) * 128 + 2 * c + p // 64
    wd = np.ascontiguousarray(wdT[perm].reshape(NIC, 128, HID))
    cos, sin = _rope_tables()
    sinf = sin.copy()
    sinf[:64, :] = -sinf[:64, :]
    qn_g = np.asarray(inputs["L0_attn_qn_g"], np.float32).reshape(128, 1)
    qn_b = np.asarray(inputs["L0_attn_qn_b"], np.float32).reshape(128, 1)
    kn_g = np.asarray(inputs["L0_attn_kn_g"], np.float32).reshape(128, 1)
    kn_b = np.asarray(inputs["L0_attn_kn_b"], np.float32).reshape(128, 1)

    in_maps = []
    meta = []
    for c in range(8):
        bc = c // 2
        blocks = (0, 3) if c % 2 == 0 else (1, 2)
        own = np.concatenate([np.arange(b * 512, (b + 1) * 512)
                              for b in blocks])
        xk = np.ascontiguousarray(x[bc].T)
        xq = np.ascontiguousarray(x[bc].T[:, own])
        cosq = np.ascontiguousarray(cos[:, own])
        sinq = np.ascontiguousarray(sinf[:, own])
        mask = np.zeros((16, 128, 512), dtype=ml_dtypes.bfloat16)
        for (qb, kt) in MASKED:
            i = MASK_IDX[(qb, kt)]
            q0 = blocks[qb] * 512
            kk = kt * 128 + np.arange(128)[:, None]
            qq = q0 + np.arange(512)[None, :]
            mask[i] = (kk <= qq).astype(np.float32).astype(ml_dtypes.bfloat16)
        in_maps.append(dict(
            xq=xq, xk=xk, cosq=cosq, sinq=sinq,
            cosk=np.ascontiguousarray(cos), sink=np.ascontiguousarray(sinf),
            mask01=mask, qn_g=qn_g, qn_b=qn_b, kn_g=kn_g, kn_b=kn_b,
            wqkv=wqkv, wo=wo, wg=wg, wu=wu, wd=wd))
        meta.append((bc, blocks))
    return in_maps, meta


def kernel(**inputs):
    in_maps, meta = _stage(inputs)
    nc = build_program()
    res = run_bass_kernel_spmd(nc, in_maps, core_ids=list(range(8)))
    out = np.empty((B, S, HID), dtype=np.float32)
    for c in range(8):
        bc, blocks = meta[c]
        y = np.asarray(res.results[c]["y"])
        for qi, b in enumerate(blocks):
            out[bc, b * 512:(b + 1) * 512, :] = y[:, qi * 512:(qi + 1) * 512].T
    return out


# version 23
# speedup vs baseline: 1.0081x; 1.0036x over previous
import contextlib
import numpy as np
import ml_dtypes
import concourse.bass as bass
import concourse.mybir as mybir
from concourse.bass_utils import run_bass_kernel_spmd

F32 = mybir.dt.float32
BF16 = mybir.dt.bfloat16
I32 = mybir.dt.int32
AF = mybir.ActivationFunctionType
ALU = mybir.AluOpType
AX = mybir.AxisListType

B, S, HID = 4, 2048, 2048
H, HKV, D = 16, 4, 128
IFF = 8192
NCO = 16
NIC = 64
TOK = 1024
EPS = 1e-5
INV_SQRT_D = 1.0 / float(np.float32(np.sqrt(np.float32(128.0))))
FWHT_NORM = float(np.float32(1.0 / np.sqrt(8192.0)))
MAGIC = 12582912.0
MAGIC64 = 6755399441055744.0
INV_N_GU = 1.0 / float(IFF * HID)

MASKED = [(0, kt) for kt in range(8)] + [(1, kt) for kt in range(8, 16)]
MASK_IDX = {qk: i for i, qk in enumerate(MASKED)}
KT_OF_QB = {0: 8, 1: 16}
_DEBUG = False
_PHASES = 99
_P6SUB = 7


class _StopBuild(Exception):
    pass


def _rope_tables():
    import jax
    import jax.numpy as jnp
    cpu = jax.devices("cpu")[0]
    with jax.default_device(cpu):
        inv = 1.0 / (1000000.0 ** (jnp.arange(0, D, 2, dtype=jnp.float32) / D))
        t = jnp.arange(S, dtype=jnp.float32)
        fr = jnp.outer(t, inv)
        cos = np.asarray(jnp.concatenate([jnp.cos(fr), jnp.cos(fr)], axis=-1))
        sin = np.asarray(jnp.concatenate([jnp.sin(fr), jnp.sin(fr)], axis=-1))
    return cos.T.copy(), sin.T.copy()


def _stage_stripes(w, n_oc):
    out_dim, in_dim = w.shape
    nhc = in_dim // 128
    st = np.empty((n_oc, 128, in_dim), dtype=np.float32)
    for oc in range(n_oc):
        s = np.ascontiguousarray(w[oc * 128:(oc + 1) * 128, :].T)
        st[oc] = s.reshape(nhc, 128, 128).transpose(1, 0, 2).reshape(128,
                                                                     in_dim)
    return st


class Pr:
    ENG = {"PE": "tensor", "ACT": "scalar", "DVE": "vector", "POOL": "gpsimd",
           "SP": "sync"}
    QUEUES = {"SP": "QSP", "POOL": "QPOOL"}

    def __init__(self, nc, es):
        self.nc = nc
        self.es = es
        self.block = es.enter_context(nc.Block())
        self.sem = {}
        self.cnt = {}
        for name in ["PE", "ACT", "DVE", "POOL"]:
            self.sem[name] = es.enter_context(nc.semaphore(name.lower()))
            self.cnt[name] = 0
        self.waited = {}
        self.lastw = {}
        self.readers = {}
        self.cur = None

    def qsem(self, name):
        if name not in self.sem:
            self.sem[name] = self.es.enter_context(
                self.nc.semaphore(name.lower()))
            self.cnt[name] = 0
        return name

    def section(self, eng_name, fn):
        def body(e):
            self.cur = (eng_name, e)
            fn(e)
            self.cur = None
        getattr(self.block, self.ENG[eng_name])(body)

    def _wait(self, sem_name, val):
        if val <= 0:
            return
        key = (self.cur[0], sem_name)
        if self.waited.get(key, 0) >= val:
            return
        self.cur[1].wait_ge(self.sem[sem_name], val)
        self.waited[key] = val

    def _need(self, reads, writes):
        for k in reads:
            w = self.lastw.get(k)
            if w:
                self._wait(*w)
        for k in writes:
            w = self.lastw.get(k)
            if w:
                self._wait(*w)
            for r in self.readers.get(k, ()):
                self._wait(*r)

    def _did(self, sem_name, reads, writes):
        c = self.cnt[sem_name]
        for k in writes:
            self.lastw[k] = (sem_name, c)
            self.readers[k] = []
        for k in reads:
            self.readers.setdefault(k, []).append((sem_name, c))

    def do(self, build, reads=(), writes=()):
        eng_name, e = self.cur
        self._need(reads, writes)
        inst = build(e)
        self.cnt[eng_name] += 1
        inst.then_inc(self.sem[eng_name], 1)
        self._did(eng_name, reads, writes)
        return inst

    def do_group(self, builds, reads=(), writes=()):
        eng_name, e = self.cur
        self._need(reads, writes)
        inst = None
        for b in builds:
            inst = b(e)
        self.cnt[eng_name] += 1
        inst.then_inc(self.sem[eng_name], 1)
        self._did(eng_name, reads, writes)
        return inst

    def dma(self, out, in_, reads=(), writes=(), q="q_dflt"):
        eng_name, e = self.cur
        self.qsem(q)
        self._need(reads, writes)
        inst = e.dma_start(out=out, in_=in_)
        self.cnt[q] += 16
        inst.then_inc(self.sem[q], 16)
        c = self.cnt[q]
        for k in writes:
            self.lastw[k] = (q, c)
            self.readers[k] = []
        for k in reads:
            self.readers.setdefault(k, []).append((q, c))
        return inst

    def barrier(self):
        snap = dict(self.cnt)
        for en in ["PE", "ACT", "DVE", "POOL", "SP"]:
            def fn(e, en=en):
                for sn, v in snap.items():
                    if sn != en and v > 0:
                        self._wait(sn, v)
            self.section(en, fn)


def build_program():
    nc = bass.Bass("TRN2", target_bir_lowering=False)

    def din(name, shape, dtype=F32):
        return nc.dram_tensor(name, shape, dtype, kind="ExternalInput").ap()

    xq_h = din("xq", [HID, TOK])
    xk_h = din("xk", [HID, S])
    cosq_h = din("cosq", [D, TOK]); sinq_h = din("sinq", [D, TOK])
    cosk_h = din("cosk", [D, S]); sink_h = din("sink", [D, S])
    mask_h = din("mask01", [16, 128, 512], BF16)
    qng_h = din("qn_g", [D, 1]); qnb_h = din("qn_b", [D, 1])
    kng_h = din("kn_g", [D, 1]); knb_h = din("kn_b", [D, 1])
    wqkv_h = din("wqkv", [24, 128, HID])
    wo_h = din("wo", [16, 128, HID])
    wg_h = din("wg", [NIC, 128, HID])
    wu_h = din("wu", [NIC, 128, HID])
    wd_h = din("wd", [NIC, 128, HID])
    y_h = nc.dram_tensor("y", [HID, TOK], F32, kind="ExternalOutput").ap()

    z_dram = nc.dram_tensor("z_scr", [NIC, 128, TOK], BF16, kind="Internal").ap()
    yv_dram = nc.dram_tensor("yv_scr", [NIC, 128, TOK], BF16,
                             kind="Internal").ap()
    aq2_dram = nc.dram_tensor("aq2_scr", [NIC, 128, TOK], BF16,
                              kind="Internal").ap()
    trd_dram = nc.dram_tensor("trd_scr", [NIC, 128, HID], BF16,
                              kind="Internal").ap()
    x1_dram = nc.dram_tensor("x1_scr", [NCO, 128, TOK], F32,
                             kind="Internal").ap()

    ident_bf = nc.inline_tensor(np.eye(128, dtype=ml_dtypes.bfloat16), "id_bf")
    ident_f = nc.inline_tensor(np.eye(128, dtype=np.float32), "id_f")
    ones_col_t = nc.inline_tensor(np.ones((128, 1), dtype=np.float32), "ones_c")
    ones_row_t = nc.inline_tensor(np.ones((1, 128), dtype=np.float32), "ones_r")
    ones_all_t = nc.inline_tensor(np.ones((128, 128), dtype=np.float32),
                                  "ones_a")
    eps_col_t = nc.inline_tensor(np.full((128, 1), EPS, dtype=np.float32),
                                 "eps_c")
    srt_t = nc.inline_tensor(np.full((1, 1), np.sqrt(np.float32(128.0)),
                                     dtype=np.float32), "srt_c")
    hmat = np.ones((1, 1), dtype=np.float64)
    while hmat.shape[0] < 128:
        hmat = np.block([[hmat, hmat], [hmat, -hmat]])
    h64 = hmat[:64, :64]
    h64blk = np.zeros((128, 128), dtype=np.float64)
    h64blk[:64, :64] = h64; h64blk[64:, 64:] = h64
    h128_t = nc.inline_tensor(hmat.astype(ml_dtypes.bfloat16), "h128")
    h64b_t = nc.inline_tensor(h64blk.astype(ml_dtypes.bfloat16), "h64b")

    es = contextlib.ExitStack()
    try:
      with es:
        pr = Pr(nc, es)
        scopes = {}
        tapn = [0]

        def tap(name, ap, key):
            if not _DEBUG:
                return
            t = nc.dram_tensor("tap_" + name, list(ap.shape), ap.dtype,
                               kind="Internal").ap()
            tapn[0] += 1
            pr.section("POOL", lambda e: pr.dma(t, ap, reads=[key],
                                                q="qtap%d" % tapn[0]))

        def sbin(scope, name, shape, dt=F32):
            st = scopes.setdefault(scope, contextlib.ExitStack())
            return st.enter_context(nc.sbuf_tensor(name, shape, dt))

        def close_scope(scope):
            if scope in scopes:
                scopes.pop(scope).close()

        sb = lambda n, s, d=F32: sbin("all", n, s, d)

        qng = sb("qng", [128, 1]); qnb = sb("qnb", [128, 1])
        kng = sb("kng", [128, 1]); knb = sb("knb", [128, 1])
        s_id_bf = sb("s_id_bf", [128, 128], BF16)
        s_id_f = sb("s_id_f", [128, 128], F32)
        s_ones_c = sb("s_ones_c", [128, 1], F32)
        s_ones_r = sb("s_ones_r", [1, 128], F32)
        s_eps = sb("s_eps", [128, 1], F32)
        s_h128 = sb("s_h128", [128, 128], BF16)
        s_h64b = sb("s_h64b", [128, 128], BF16)
        onescol_bf = sb("onescol_bf", [128, 1], BF16)
        s_ones_all = sb("s_ones_all", [128, 128], F32)
        rows = sb("rows", [128, 4096], F32)
        _RP = {0: 32, 1: 0, 2: 0, 3: 32, 4: 64, 5: 64, 6: 0, 7: 32,
               8: 0, 9: 0, 10: 32, 11: 0, 12: 32, 13: 64, 14: 32, 15: 64}
        _RC = {0: 0, 1: 0, 2: 512, 3: 512, 4: 0, 5: 1024, 6: 1024,
               7: 1024, 8: 2048, 9: 2560, 10: 2048, 11: 3072, 12: 3072,
               13: 3072, 14: 512, 15: 512}
        cols = sb("cols", [128, 32], F32)
        mean_acc = sb("mean_acc", [128, 4], F32)
        scrA = sb("scrA", [128, 1024], F32)
        scrB = sb("scrB", [128, 1024], F32)
        scrC = sb("scrC", [128, 1024], F32)
        scrD = sb("scrD", [128, 1024], F32)
        b1b = sb("b1b", [128, 1024], F32)
        b2b = sb("b2b", [128, 1024], F32)
        wstr = [sb(f"wstr{i}", [128, HID], F32) for i in range(2)]
        wstr_bf = [sb(f"wstrbf{i}", [128, HID], BF16) for i in range(2)]
        bigA = sb("bigA", [128, NCO * TOK], BF16)
        bigB = sb("bigB", [128, NCO * TOK], BF16)

        def rowv(i, n=1024):
            p = _RP[i]
            return rows[p:p + 1, _RC[i]:_RC[i] + n]

        def colv(i):
            return cols[:, i:i + 1]

        def p0_sp(e):
            pr.dma(qng[:, :], qng_h, writes=["qng"], q="qc_qng")
            pr.dma(qnb[:, :], qnb_h, writes=["qnb"], q="qc_qnb")
            pr.dma(kng[:, :], kng_h, writes=["kng"], q="qc_kng")
            pr.dma(knb[:, :], knb_h, writes=["knb"], q="qc_knb")
            pr.dma(s_id_bf[:, :], ident_bf.ap(), writes=["idbf"], q="qc_idbf")
            pr.dma(s_id_f[:, :], ident_f.ap(), writes=["idf"], q="qc_idf")
            pr.dma(s_ones_c[:, :], ones_col_t.ap(), writes=["onesc"], q="qc_onesc")
            pr.dma(s_ones_r[:, :], ones_row_t.ap(), writes=["onesr"], q="qc_onesr")
            pr.dma(s_ones_all[:, :], ones_all_t.ap(), writes=["onesall"],
                   q="qc_onesall")
            pr.dma(s_eps[:, :], eps_col_t.ap(), writes=["s_eps"], q="qc_s_eps")
            pr.dma(s_h128[:, :], h128_t.ap(), writes=["h128"], q="qc_h128")
            pr.dma(s_h64b[:, :], h64b_t.ap(), writes=["h64b"], q="qc_h64b")
        pr.section("SP", p0_sp)

        def p0_dve(e):
            pr.do(lambda e2: e2.memset(mean_acc[:, :], 0.0),
                  writes=["mean_acc"])
            pr.do(lambda e2: e2.memset(cols[:, :], 0.0),
                  writes=["cols", "mcol0", "mcol1"])
            pr.do(lambda e2: e2.tensor_copy(onescol_bf[:, :], s_ones_c[:, :]),
                  reads=["onesc"], writes=["onescbf"])
        pr.section("DVE", p0_dve)

        mslot = [None, None]
        mean_jobs = []
        for mi, wh in enumerate((wg_h, wu_h, wd_h)):
            for oc in range(NIC):
                for hf in range(2):
                    mean_jobs.append((wh[oc][:, hf * 1024:hf * 1024 + 1024],
                                      mi))
        mean_state = {"dma": 0, "cmp": 0}

        def mean_emit(n):
            j0 = mean_state["cmp"]
            j1 = min(j0 + n, len(mean_jobs))
            if j0 >= j1:
                return
            def emit_dmas(bound):
                while mean_state["dma"] <= min(bound, len(mean_jobs) - 1):
                    jd = mean_state["dma"]
                    srcd, _m = mean_jobs[jd]
                    sld = jd % 2
                    pr.dma(mslot[sld][:, :], srcd, writes=[f"mslot{sld}"],
                           q=f"qm{sld}")
                    mean_state["dma"] += 1
            for j in range(j0, j1):
                src_, mi = mean_jobs[j]
                sl = j % 2
                ck = f"mcol{sl}"
                def act_fn(e, j=j, sl=sl, ck=ck):
                    emit_dmas(j + 1)
                    pr.do(lambda e2: e2.activation(
                        mslot[sl][:, :], mslot[sl][:, :], AF.Abs,
                        accum_out=colv(8 + sl)),
                        reads=[f"mslot{sl}"], writes=[f"mslot{sl}", ck])
                    emit_dmas(j + 2)
                pr.section("ACT", act_fn)
                pr.section("DVE", lambda e, mi=mi, sl=sl, ck=ck: pr.do(
                    lambda e2: e2.tensor_tensor(
                        mean_acc[:, mi:mi + 1], mean_acc[:, mi:mi + 1],
                        colv(8 + sl), ALU.add),
                    reads=[ck, "mean_acc"], writes=["mean_acc"]))
            mean_state["cmp"] = j1

        psum_es = [None, 0]

        def psum_phase(spec):
            if psum_es[0] is not None:
                psum_es[0].close()
            psum_es[0] = contextlib.ExitStack()
            psum_es[1] += 1
            out = {}
            for name, shape, dt in spec:
                out[name] = psum_es[0].enter_context(
                    nc.psum_tensor(f"{name}_{psum_es[1]}", shape, dt))
            return out

        ppd = {}

        def stats2048(tag, chunk_ap, T0, T, src_key, bf_src=False,
                      chunk_load=None):
            kS, kS2 = f"{tag}S", f"{tag}S2"
            ones_ap = onescol_bf[:, :] if bf_src else s_ones_c[:, :]
            ones_key = "onescbf" if bf_src else "onesc"
            for c in range(NCO):
                if chunk_load is not None:
                    chunk_load(c)
                skey = src_key(c) if callable(src_key) else src_key
                pr.section("PE", lambda e, c=c, skey=skey: pr.do_group(
                    [lambda e2: e2.matmul(
                        ppd["pp_stat"][0:1, 0:T], ones_ap, chunk_ap(c),
                        start=(c == 0), stop=(c == NCO - 1))],
                    reads=[skey, ones_key], writes=[kS, "ppstat"]))
                sl = c % 2
                scr = scrA[:, sl * 512:sl * 512 + 512][:, 0:T]
                pr.section("DVE", lambda e, c=c, scr=scr, skey=skey: pr.do(
                    lambda e2: e2.tensor_tensor(scr, chunk_ap(c), chunk_ap(c),
                                                ALU.mult),
                    reads=[skey], writes=["scrA"]))
                pr.section("PE", lambda e, c=c, scr=scr: pr.do_group(
                    [lambda e2: e2.matmul(ppd["pp_stat"][64:65, 0:T],
                                          s_ones_c[:, :], scr,
                                          start=(c == 0),
                                          stop=(c == NCO - 1))],
                    reads=["scrA", "onesc"], writes=[kS2, "ppstat"]))
            def dve_rows(e):
                pr.do(lambda e2: e2.tensor_scalar(
                    rowv(0, T), ppd["pp_stat"][0:1, 0:T], 1.0 / 2048.0, None,
                    ALU.mult), reads=[kS], writes=["hr0"])
                pr.do(lambda e2: e2.tensor_scalar(
                    rowv(1, T), ppd["pp_stat"][64:65, 0:T], 1.0 / 2048.0,
                    None, ALU.mult), reads=[kS2], writes=["hr0"])
                pr.do(lambda e2: e2.tensor_tensor(
                    rowv(2, T), rowv(0, T), rowv(0, T), ALU.mult),
                    reads=["hr0"], writes=["hr0"])
                pr.do(lambda e2: e2.tensor_tensor(
                    rowv(1, T), rowv(1, T), rowv(2, T), ALU.subtract),
                    reads=["hr0"], writes=["hr0"])
            pr.section("DVE", dve_rows)
            pr.section("ACT", lambda e: pr.do(
                lambda e2: e2.activation(rowv(2, T), rowv(1, T), AF.Sqrt,
                                         bias=s_eps[0:1, 0:1]),
                reads=["hr0", "s_eps"], writes=["hr0"]))
            def dve_rows2(e):
                pr.do(lambda e2: e2.reciprocal(rowv(3, T), rowv(2, T)),
                      reads=["hr0"], writes=["hr0"])
                pr.do(lambda e2: e2.tensor_tensor(
                    rowv(4, T), rowv(0, T), rowv(3, T), ALU.mult),
                    reads=["hr0"], writes=["hr0"])
            pr.section("DVE", dve_rows2)
            bcast_row(3, 0, b1b[:, T0:T0 + T], "b1b", T, row_key="hr0")
            bcast_row(4, 0, b2b[:, T0:T0 + T], "b2b", T, row_key="hr0")

        def bcast_row(ri, col0, dst_ap, dst_key, T, row_key="rows"):
            p = _RP[ri]
            src = rows[p:p + 1, _RC[ri] + col0:_RC[ri] + col0 + T]
            pr.section("PE", lambda e: pr.do(
                lambda e2: e2.matmul(ppd["pp_b"][:, 0:T],
                                     s_ones_all[p:p + 1, :], src,
                                     start=True, stop=True),
                reads=[row_key, "onesall"], writes=["ppb"]))
            pr.section("DVE", lambda e: pr.do(
                lambda e2: e2.tensor_copy(dst_ap, ppd["pp_b"][:, 0:T]),
                reads=["ppb"], writes=[dst_key]))

        def round_exact(dst_ap, dst_key, v_ap, v_key, T, dbg=None):
            kf = scrKI[:, 0:T].bitcast(F32)
            def dve1(e):
                pr.do(lambda e2: e2.tensor_scalar(
                    scrC[:, 0:T], v_ap, MAGIC, MAGIC, ALU.add, ALU.subtract),
                    reads=[v_key], writes=["scrC"])
                pr.do(lambda e2: e2.tensor_copy(kf, scrC[:, 0:T]),
                      reads=["scrC"], writes=["scrKI"])
                pr.do(lambda e2: e2.scalar_tensor_tensor(
                    scrC[:, 0:T], kf, -1.0, v_ap, ALU.mult, ALU.add),
                    reads=["scrKI", v_key], writes=["scrC"])
            pr.section("DVE", dve1)
            if dbg:
                tap(dbg + "k0", kf, "scrKI")
                tap(dbg + "d", scrC[:, 0:T], "scrC")
            def dve3(e):
                pr.do(lambda e2: e2.tensor_scalar(
                    scrC[:, 0:T], scrC[:, 0:T], MAGIC64, MAGIC64,
                    ALU.add, ALU.subtract),
                    reads=["scrC"], writes=["scrC"])
                pr.do(lambda e2: e2.scalar_tensor_tensor(
                    scrC[:, 0:T], scrC[:, 0:T], 1.0, kf, ALU.mult, ALU.add),
                    reads=["scrC", "scrKI"], writes=["scrC"])
                pr.do(lambda e2: e2.tensor_scalar(
                    dst_ap, scrC[:, 0:T], -128.0, 127.0, ALU.max, ALU.min),
                    reads=["scrC"], writes=[dst_key])
            pr.section("DVE", dve3)

        def load_stripe(sl, src_ap, n=HID):
            pr.section("SP", lambda e: pr.dma(wstr[sl][:, 0:n], src_ap,
                                              writes=[f"wsf{sl}"],
                                              q=f"qw{sl}"))
            pr.section("DVE", lambda e: pr.do(
                lambda e2: e2.tensor_copy(wstr_bf[sl][:, 0:n],
                                          wstr[sl][:, 0:n]),
                reads=[f"wsf{sl}"], writes=[f"wsb{sl}"]))

        def ln_from_dram(tag, dram_ap, col0, dst, xg):
            for g in range(4):
                mean_emit(6)
                T0g = g * 256
                kx = "xgrp"
                def sp(e, g=g, kx=kx):
                    for c in range(NCO):
                        pr.dma(xg[c][:, :],
                               dram_ap[c * 128:(c + 1) * 128,
                                       col0 + g * 256: col0 + g * 256 + 256],
                               writes=[kx], q="qxg")
                pr.section("SP", sp)
                stats2048(tag + str(g), lambda c: xg[c][:, :],
                          0, 256, kx)
                def dve_apply(e, g=g, kx=kx):
                    for c in range(NCO):
                        pr.do(lambda e2, c=c: e2.tensor_tensor(
                            xg[c][:, :], xg[c][:, :], b1b[:, 0:256],
                            ALU.mult),
                            reads=["b1b", kx], writes=[kx])
                        pr.do(lambda e2, c=c, g=g: e2.tensor_tensor(
                            dst[:, c * TOK + g * 256: c * TOK + g * 256 + 256],
                            xg[c][:, :], b2b[:, 0:256], ALU.subtract),
                            reads=["b2b", kx], writes=["lnb"])
                pr.section("DVE", dve_apply)

        HLR = ((0, 1, 2, 3, 4), (10, 11, 6, 12, 13))

        def head_ln_rope(tag, src, src_key, T, gcol, gbkeys, bcol, cosT, sinT,
                         ckey, dst_ap, dst_key, par=0):
            R = HLR[par]
            rk = f"hr{par}"
            pst = "ppstat" if par == 0 else "ppstat1"
            psn = "pp_stat" if par == 0 else "pp_st2"
            pb = 0
            for g in range(T // 512):
                sl512 = g * 512
                sq = sqbuf[:, par * 512:par * 512 + 512]
                sqk = f"sq{par}"
                pr.section("DVE", lambda e, sl512=sl512, sq=sq, sqk=sqk:
                           pr.do(lambda e2: e2.tensor_tensor(
                               sq, src[:, sl512:sl512 + 512],
                               src[:, sl512:sl512 + 512], ALU.mult),
                               reads=[src_key], writes=[sqk]))
                pr.section("PE", lambda e, sl512=sl512, sq=sq, sqk=sqk: (
                    pr.do(lambda e2: e2.matmul(
                        ppd[psn][pb:pb + 1, 0:512], s_ones_c[:, :],
                        src[:, sl512:sl512 + 512], start=True, stop=True),
                        reads=[src_key, "onesc"], writes=[pst]),
                    pr.do(lambda e2: e2.matmul(
                        ppd[psn][pb + 64:pb + 65, 0:512],
                        s_ones_c[:, :], sq, start=True, stop=True),
                        reads=[sqk, "onesc"], writes=[pst])))
                def rows_fn(e):
                    pr.do(lambda e2: e2.tensor_scalar(
                        rowv(R[0], 512), ppd[psn][pb:pb + 1, 0:512],
                        1.0 / 128.0, None, ALU.mult),
                        reads=[pst], writes=[rk])
                    pr.do(lambda e2: e2.tensor_scalar(
                        rowv(R[1], 512), ppd[psn][pb + 64:pb + 65, 0:512],
                        1.0 / 128.0, None, ALU.mult),
                        reads=[pst], writes=[rk])
                    pr.do(lambda e2: e2.tensor_tensor(
                        rowv(R[2], 512), rowv(R[0], 512), rowv(R[0], 512),
                        ALU.mult), reads=[rk], writes=[rk])
                    pr.do(lambda e2: e2.tensor_tensor(
                        rowv(R[1], 512), rowv(R[1], 512), rowv(R[2], 512),
                        ALU.subtract), reads=[rk], writes=[rk])
                pr.section("DVE", rows_fn)
                pr.section("ACT", lambda e: pr.do(
                    lambda e2: e2.activation(rowv(R[2], 512), rowv(R[1], 512),
                                             AF.Sqrt, bias=s_eps[0:1, 0:1]),
                    reads=[rk, "s_eps"], writes=[rk]))
                def rows2_fn(e):
                    pr.do(lambda e2: e2.reciprocal(rowv(R[3], 512),
                                                   rowv(R[2], 512)),
                          reads=[rk], writes=[rk])
                    pr.do(lambda e2: e2.tensor_tensor(
                        rowv(R[4], 512), rowv(R[0], 512), rowv(R[3], 512),
                        ALU.mult), reads=[rk], writes=[rk])
                pr.section("DVE", rows2_fn)
                bcast_row(R[3], 0, b1b[:, sl512:sl512 + 512], "b1b", 512,
                          row_key=rk)
                bcast_row(R[4], 0, b2b[:, sl512:sl512 + 512], "b2b", 512,
                          row_key=rk)
            def apply_fn(e):
                pr.do(lambda e2: e2.tensor_tensor(
                    scrB[:, 0:T], src[:, 0:T], b1b[:, 0:T], ALU.mult),
                    reads=[src_key, "b1b"], writes=["scrB"])
                pr.do(lambda e2: e2.tensor_tensor(
                    scrB[:, 0:T], scrB[:, 0:T], b2b[:, 0:T], ALU.subtract),
                    reads=["scrB", "b2b"], writes=["scrB"])
            pr.section("DVE", apply_fn)
            pr.section("DVE", lambda e: pr.do(
                lambda e2: e2.tensor_scalar(scrB[:, 0:T], scrB[:, 0:T],
                                            gcol[:, 0:1], bcol[:, 0:1],
                                            ALU.mult, ALU.add),
                reads=["scrB"] + list(gbkeys), writes=["scrB"]))
            def rope_fn(e):
                pr.do(lambda e2: e2.tensor_tensor(
                    scrC[:, 0:T], scrB[:, 0:T], cosT, ALU.mult),
                    reads=["scrB", ckey], writes=["scrC"])
                pr.do(lambda e2: e2.tensor_copy(
                    scrD[0:64, 0:T], scrB[64:128, 0:T]),
                    reads=["scrB"], writes=["scrD"])
                pr.do(lambda e2: e2.tensor_copy(
                    scrD[64:128, 0:T], scrB[0:64, 0:T]),
                    reads=["scrB"], writes=["scrD"])
                pr.do(lambda e2: e2.tensor_tensor(
                    scrD[:, 0:T], scrD[:, 0:T], sinT, ALU.mult),
                    reads=["scrD", ckey], writes=["scrD"])
                pr.do(lambda e2: e2.tensor_tensor(
                    dst_ap, scrC[:, 0:T], scrD[:, 0:T], ALU.add),
                    reads=["scrC", "scrD"], writes=[dst_key])
            pr.section("DVE", rope_fn)

        ppd = psum_phase([("pp_stat", [128, 512], F32),
                          ("pp_b", [128, 512], F32),
                          ("pp_a0", [128, 512], F32),
                          ("pp_a1", [128, 512], F32),
                          ("pp_a2", [128, 512], F32),
                          ("pp_a3", [128, 512], F32),
                          ("pp_st2", [128, 512], F32),
                          ("pp_tr", [128, 128], BF16)])
        kpr = sbin("p12", "kpr", [128, HKV * S], BF16)
        vT = sbin("p12", "vT", [128, HKV * 16 * 128], BF16)
        mslot[0] = sbin("p02", "mean_t0", [128, 1024], F32)
        mslot[1] = sbin("p02", "mean_t1", [128, 1024], F32)
        qpr = bigB
        lnb = bigA
        xg = [sbin("p1", f"xg{i}", [128, 256], F32) for i in range(NCO)]
        vbf = sbin("p1", "vbf", [128, 1024], BF16)
        scrA2 = sbin("p1", "scrA2", [128, 1024], F32)
        sqbuf = sbin("p1", "sqbuf", [128, 1024], F32)
        scrAp = [scrA, scrA2]
        scrApk = ["scrA", "scrA2"]

        def qkv_mm(oc, sl, rhs, rhs_key):
            pq = (oc % 2) * 2
            def mm(e):
                for th in range(2):
                    pk = f"pp_a{pq + th}"
                    pr.do_group(
                        [(lambda e2, hc=hc, th=th, pk=pk:
                          e2.matmul(ppd[pk][:, 0:512],
                                    wstr_bf[sl][:, hc * 128:(hc + 1) * 128],
                                    rhs[:, hc * TOK + th * 512:
                                        hc * TOK + th * 512 + 512],
                                    start=(hc == 0), stop=(hc == NCO - 1)))
                         for hc in range(NCO)],
                        reads=[f"wsb{sl}", rhs_key], writes=[pk])
            pr.section("PE", mm)

        def evac_psum(dst, dst_key, oc):
            pq = (oc % 2) * 2
            def ev(e):
                for th in range(2):
                    pr.do(lambda e2, th=th: e2.tensor_copy(
                        dst[:, th * 512:th * 512 + 512],
                        ppd[f"pp_a{pq + th}"][:, 0:512]),
                        reads=[f"pp_a{pq + th}"], writes=[dst_key])
            pr.section("DVE", ev)

        cosq = sbin("p1q", "s_cosq", [128, TOK], F32)
        sinq = sbin("p1q", "s_sinq", [128, TOK], F32)
        pr.section("SP", lambda e: (
            pr.dma(cosq[:, :], cosq_h, writes=["cosq"], q="qcos"),
            pr.dma(sinq[:, :], sinq_h, writes=["cosq"], q="qcos")))
        ln_from_dram("l1q", xq_h, 0, lnb, xg)
        tap("ln1q", lnb[:, :], "lnb")
        load_stripe(0, wqkv_h[0])
        qkv_mm(0, 0, lnb, "lnb")
        load_stripe(1, wqkv_h[1])
        qkv_mm(1, 1, lnb, "lnb")
        for oc in range(16):
            par = oc % 2
            evac_psum(scrAp[par][:, 0:1024], scrApk[par], oc)
            if oc == 0:
                tap("qraw0", scrAp[par][:, 0:1024], scrApk[par])
            if oc + 2 < 16:
                load_stripe(par, wqkv_h[oc + 2])
                qkv_mm(oc + 2, par, lnb, "lnb")
            head_ln_rope(f"q{oc}", scrAp[par][:, 0:1024], scrApk[par], TOK,
                         qng, ("qng", "qnb"), qnb, cosq[:, 0:TOK],
                         sinq[:, 0:TOK], "cosq",
                         qpr[:, oc * TOK:(oc + 1) * TOK], "qpr", par=par)
            mean_emit(14)
        close_scope("p1q")
        pr.barrier()

        cosk = sbin("p1k", "s_cosk", [128, TOK], F32)
        sink = sbin("p1k", "s_sink", [128, TOK], F32)
        for half in range(2):
            def ckload(e, half=half):
                pr.dma(cosk[:, :], cosk_h[:, half * 1024:half * 1024 + 1024],
                       writes=["cosk"], q="qcos")
                pr.dma(sink[:, :], sink_h[:, half * 1024:half * 1024 + 1024],
                       writes=["cosk"], q="qcos")
            pr.section("SP", ckload)
            ln_from_dram(f"l1k{half}", xk_h, half * 1024, lnb, xg)
            load_stripe(0, wqkv_h[16])
            qkv_mm(16, 0, lnb, "lnb")
            load_stripe(1, wqkv_h[17])
            qkv_mm(17, 1, lnb, "lnb")
            for oc in range(16, 24):
                kv = oc - 16 if oc < 20 else oc - 20
                is_k = oc < 20
                par = oc % 2
                evac_psum(scrAp[par][:, 0:1024] if is_k else vbf,
                          scrApk[par] if is_k else "vbf", oc)
                if oc + 2 < 24:
                    load_stripe(par, wqkv_h[oc + 2])
                    qkv_mm(oc + 2, par, lnb, "lnb")
                if is_k:
                    head_ln_rope(f"k{kv}h{half}", scrAp[par][:, 0:1024],
                                 scrApk[par], 1024, kng, ("kng", "knb"), knb,
                                 cosk[:, 0:1024], sink[:, 0:1024],
                                 "cosk",
                                 kpr[:, kv * S + half * 1024:
                                     kv * S + half * 1024 + 1024], "kpr",
                                 par=par)
                else:
                    for tb in range(8):
                        kt = half * 8 + tb
                        pr.section("PE", lambda e, tb=tb: pr.do(
                            lambda e2: e2.transpose(
                                ppd["pp_tr"][:, 0:128],
                                vbf[:, tb * 128:(tb + 1) * 128],
                                s_id_bf[:, :]),
                            reads=["vbf", "idbf"], writes=["pptr"]))
                        pr.section("DVE", lambda e, kv=kv, kt=kt: pr.do(
                            lambda e2: e2.tensor_copy(
                                vT[:, (kv * 16 + kt) * 128:
                                   (kv * 16 + kt) * 128 + 128],
                                ppd["pp_tr"][:, 0:128]),
                            reads=["pptr"], writes=["vT"]))
                mean_emit(10)
        mean_emit(len(mean_jobs))
        tap("qpr", qpr[:, :], "qpr")
        tap("kpr", kpr[:, :], "kpr")
        tap("vT", vT[:, :], "vT")
        close_scope("p1k")
        close_scope("p1")
        close_scope("p02")
        pr.barrier()
        if _PHASES <= 1:
            raise _StopBuild

        ppd = psum_phase([("pp_s0", [128, 512], F32),
                          ("pp_s1", [128, 512], F32),
                          ("pp_s2", [128, 512], F32),
                          ("pp_s3", [128, 512], F32),
                          ("pp_v0", [128, 512], F32),
                          ("pp_v1", [128, 512], F32),
                          ("pp_sum", [128, 512], F32),
                          ("pp_b", [128, 512], F32)])
        attn = bigA
        maskt = sbin("p2", "maskt", [128, 16 * 512], BF16)
        cbSs = [sbin("p2", f"cbS{i}", [128, 512], F32) for i in range(2)]
        rbSs = [sbin("p2", f"rbS{i}", [128, 512], F32) for i in range(2)]
        scfp = [sbin("p2", f"scfp{i}", [128, 1024], F32) for i in range(2)]
        ptp2 = [sbin("p2", f"ptp{i}", [128, 1024], BF16) for i in range(2)]
        def mload(e):
            for i in range(16):
                pr.dma(maskt[:, i * 512:(i + 1) * 512], mask_h[i],
                       writes=["maskt"], q="qmask")
        pr.section("SP", mload)
        srtS = sbin("p2", "srtS", [1, 1], F32)
        invd = sbin("p2", "invd", [128, 1], F32)
        pr.section("SP", lambda e: pr.dma(srtS[:, :], srt_t.ap(),
                                          writes=["srtS"], q="qc_srt"))
        pr.section("DVE", lambda e: pr.do(
            lambda e2: e2.reciprocal(rowv(8, 1), srtS[:, :]),
            reads=["srtS"], writes=["rows"]))
        bcast_row(8, 0, invd[:, 0:1], "invd", 1)

        for kv in range(HKV):
            for gp in range(2):
                pr.section("DVE", lambda e, kv=kv, gp=gp: pr.do(
                    lambda e2: e2.tensor_tensor(
                        scrA[:, 0:1024],
                        kpr[:, kv * S + gp * 1024:kv * S + gp * 1024 + 1024],
                        kpr[:, kv * S + gp * 1024:kv * S + gp * 1024 + 1024],
                        ALU.mult),
                    reads=["kpr"], writes=["scrA"]))
                pr.section("PE", lambda e, gp=gp: pr.do_group(
                    [(lambda e2, g=g: e2.matmul(
                        ppd["pp_sum"][g * 32:g * 32 + 1, 0:512],
                        s_ones_c[:, :],
                        scrA[:, g * 512:g * 512 + 512],
                        start=True, stop=True)) for g in range(2)],
                    reads=["scrA", "onesc"], writes=["ppsum"]))
                def nk_rows(e, gp=gp):
                    for g in range(2):
                        pr.do(lambda e2, g=g, gp=gp: e2.tensor_copy(
                            rowv(5, 2048)[0:1, (gp * 2 + g) * 512:
                                          (gp * 2 + g) * 512 + 512],
                            ppd["pp_sum"][g * 32:g * 32 + 1, 0:512]),
                            reads=["ppsum"], writes=["rows"])
                pr.section("DVE", nk_rows)
            pr.section("ACT", lambda e: pr.do(
                lambda e2: e2.activation(rowv(5, 2048), rowv(5, 2048),
                                         AF.Sqrt),
                reads=["rows"], writes=["rows"]))
            pr.section("DVE", lambda e, kv=kv: pr.do(
                lambda e2: e2.reduce_max(
                    rowv(14, 1024)[0:1, kv:kv + 1],
                    rowv(5, 2048), axis=AX.X),
                reads=["rows"], writes=["rows"]))

        for hh in range(H):
            kv = hh // 4
            pr.section("DVE", lambda e, hh=hh: pr.do(
                lambda e2: e2.tensor_tensor(
                    scrA[:, 0:TOK], qpr[:, hh * TOK:(hh + 1) * TOK],
                    qpr[:, hh * TOK:(hh + 1) * TOK], ALU.mult),
                reads=["qpr"], writes=["scrA"]))
            pr.section("PE", lambda e: pr.do_group(
                [(lambda e2, g=g: e2.matmul(
                    ppd["pp_sum"][g * 32:g * 32 + 1, 0:512], s_ones_c[:, :],
                    scrA[:, g * 512:g * 512 + 512], start=True, stop=True))
                 for g in range(2)],
                reads=["scrA", "onesc"], writes=["ppsum"]))
            def nq_rows(e):
                for g in range(2):
                    pr.do(lambda e2, g=g: e2.tensor_copy(
                        rowv(6, TOK)[0:1, g * 512:g * 512 + 512],
                        ppd["pp_sum"][g * 32:g * 32 + 1, 0:512]),
                        reads=["ppsum"], writes=["rows"])
            pr.section("DVE", nq_rows)
            pr.section("ACT", lambda e: pr.do(
                lambda e2: e2.activation(rowv(6, TOK), rowv(6, TOK), AF.Sqrt),
                reads=["rows"], writes=["rows"]))
            pr.section("DVE", lambda e, kv=kv: pr.do(
                lambda e2: e2.tensor_scalar(
                    rowv(7, TOK), rowv(6, TOK),
                    rowv(14, 1024)[0:1, kv:kv + 1],
                    INV_SQRT_D, ALU.mult, ALU.mult),
                reads=["rows", "rows"], writes=["rows"]))
            for qb in range(2):
                cbi = (hh * 2 + qb) % 2
                cb = cbSs[cbi]
                cbk = f"cbS{cbi}"
                rb = rbSs[cbi]
                rbk = f"rbS{cbi}"
                bcast_row(7, qb * 512, cb[:, :], cbk, 512)
                nkt = KT_OF_QB[qb]
                pvk = f"pp_v{qb}"
                def sc_mm(kt, hh=hh, kv=kv, qb=qb):
                    sps = f"pp_s{kt % 4}"
                    pr.section("PE", lambda e, kt=kt, sps=sps: pr.do(
                        lambda e2: e2.matmul(
                            ppd[sps][:, 0:512],
                            kpr[:, kv * S + kt * 128:kv * S + kt * 128 + 128],
                            qpr[:, hh * TOK + qb * 512:
                                hh * TOK + qb * 512 + 512],
                            start=True, stop=True),
                        reads=["kpr", "qpr"], writes=[sps]))
                sc_mm(0)
                sc_mm(1)
                for kp in range(nkt // 2):
                    k0 = 2 * kp
                    pb = kp % 2
                    sfp = scfp[pb]
                    sfpk = f"scfp{pb}"
                    ptp = ptp2[pb]
                    ptpk = f"ptp{pb}"
                    if k0 + 2 < nkt:
                        sc_mm(k0 + 2)
                        sc_mm(k0 + 3)
                    def stt2(e, sfp=sfp, sfpk=sfpk, cb=cb, cbk=cbk, k0=k0):
                        for j in range(2):
                            sps = f"pp_s{(k0 + j) % 4}"
                            pr.do(lambda e2, j=j, sps=sps:
                                  e2.scalar_tensor_tensor(
                                      sfp[:, j * 512:j * 512 + 512],
                                      ppd[sps][:, 0:512], invd[:, 0:1],
                                      cb[:, :], ALU.mult, ALU.subtract),
                                  reads=[sps, cbk, "invd"], writes=[sfpk])
                    pr.section("DVE", stt2)
                    pr.section("ACT", lambda e, sfp=sfp, sfpk=sfpk, ptp=ptp,
                               ptpk=ptpk: pr.do(
                        lambda e2: e2.activation(ptp[:, :], sfp[:, :],
                                                 AF.Exp),
                        reads=[sfpk], writes=[ptpk]))
                    if (qb, k0) in MASK_IDX:
                        mi0 = MASK_IDX[(qb, k0)]
                        pr.section("DVE", lambda e, ptp=ptp, ptpk=ptpk,
                                   mi0=mi0: pr.do(
                            lambda e2: e2.tensor_tensor(
                                ptp[:, :], ptp[:, :],
                                maskt[:, mi0 * 512:mi0 * 512 + 1024],
                                ALU.mult),
                            reads=["maskt", ptpk], writes=[ptpk]))
                    def pe_pv2(e, k0=k0, kv=kv, ptp=ptp, ptpk=ptpk, pvk=pvk,
                               nkt=nkt, qb=qb):
                        for j in range(2):
                            kt = k0 + j
                            pr.do(lambda e2, j=j, kt=kt: e2.matmul(
                                ppd["pp_sum"][qb * 64:qb * 64 + 1, 0:512],
                                onescol_bf[:, :],
                                ptp[:, j * 512:j * 512 + 512],
                                start=(kt == 0), stop=(kt == nkt - 1)),
                                reads=[ptpk, "onescbf"], writes=["ppsum"])
                            pr.do(lambda e2, j=j, kt=kt: e2.matmul(
                                ppd[pvk][:, 0:512],
                                vT[:, (kv * 16 + kt) * 128:
                                   (kv * 16 + kt) * 128 + 128],
                                ptp[:, j * 512:j * 512 + 512],
                                start=(kt == 0), stop=(kt == nkt - 1)),
                                reads=[ptpk, "vT"], writes=[pvk])
                    pr.section("PE", pe_pv2)
                def fin(e, qb=qb):
                    pr.do(lambda e2: e2.tensor_copy(
                        rowv(8, 512),
                        ppd["pp_sum"][qb * 64:qb * 64 + 1, 0:512]),
                        reads=["ppsum"], writes=["rows"])
                    pr.do(lambda e2: e2.reciprocal(rowv(9, 512), rowv(8, 512)),
                          reads=["rows"], writes=["rows"])
                pr.section("DVE", fin)
                bcast_row(9, 0, rb[:, :], rbk, 512)
                pr.section("DVE", lambda e, hh=hh, qb=qb, pvk=pvk, rb=rb,
                           rbk=rbk: pr.do(
                    lambda e2: e2.tensor_tensor(
                        attn[:, hh * TOK + qb * 512:
                             hh * TOK + qb * 512 + 512],
                        ppd[pvk][:, 0:512], rb[:, :], ALU.mult),
                    reads=[pvk, rbk], writes=["attn"]))
        tap("attn", attn[:, :], "attn")
        close_scope("p2")
        close_scope("p12")
        pr.barrier()
        if _PHASES <= 2:
            raise _StopBuild

        ppd = psum_phase([(f"pp_w{i}", [128, 512], F32) for i in range(6)]
                         + [("pp_stat", [128, 512], F32),
                            ("pp_b", [128, 512], F32)])
        scrKI = sbin("rest", "scrKI", [128, 1024], I32)
        xst = [sbin("rest", f"xst{i}", [128, TOK], F32) for i in range(2)]
        wo_out = bigB

        for og in range(6):
            ocs = list(range(og * 3, min(og * 3 + 3, NCO)))
            if not ocs:
                continue
            for dc in range(NCO):
                sl = dc % 2
                def spf(e, dc=dc, sl=sl, ocs=ocs):
                    pr.dma(wstr[sl][:, 0:len(ocs) * 128],
                           wo_h[dc][:, ocs[0] * 128:(ocs[-1] + 1) * 128],
                           writes=[f"wsf{sl}"], q=f"qw{sl}")
                pr.section("SP", spf)
                pr.section("DVE", lambda e, sl=sl, n=len(ocs): pr.do(
                    lambda e2: e2.tensor_copy(wstr_bf[sl][:, 0:n * 128],
                                              wstr[sl][:, 0:n * 128]),
                    reads=[f"wsf{sl}"], writes=[f"wsb{sl}"]))
                def mmw(e, dc=dc, sl=sl, ocs=ocs):
                    builds = []
                    for j in range(len(ocs)):
                        for th in range(2):
                            builds.append(
                                (lambda e2, j=j, th=th, dc=dc, sl=sl:
                                 e2.matmul(
                                     ppd[f"pp_w{j * 2 + th}"][:, 0:512],
                                     wstr_bf[sl][:, j * 128:(j + 1) * 128],
                                     attn[:, dc * TOK + th * 512:
                                          dc * TOK + th * 512 + 512],
                                     start=(dc == 0), stop=(dc == NCO - 1))))
                    pr.do_group(builds, reads=[f"wsb{sl}", "attn"],
                                writes=["ppw"])
                pr.section("PE", mmw)
            def evacw(e, ocs=ocs):
                for j, oc in enumerate(ocs):
                    for th in range(2):
                        pr.do(lambda e2, j=j, oc=oc, th=th: e2.tensor_copy(
                            wo_out[:, oc * TOK + th * 512:
                                   oc * TOK + th * 512 + 512],
                            ppd[f"pp_w{j * 2 + th}"][:, 0:512]),
                            reads=["ppw"], writes=["wo_out"])
            pr.section("DVE", evacw)

        tap("wo_out", wo_out[:, :], "wo_out")
        for g in range(2):
            stats2048("ln2g%d" % g,
                      lambda c, g=g: wo_out[:, c * TOK + g * 512:
                                            c * TOK + g * 512 + 512],
                      g * 512, 512, "wo_out", bf_src=True)
        for c in range(NCO):
            sl = c % 2
            pr.section("SP", lambda e, c=c, sl=sl: pr.dma(
                xst[sl][:, :], xq_h[c * 128:(c + 1) * 128, :],
                writes=[f"xst{sl}"], q=f"qx{sl}"))
            def ln2_apply(e, c=c, sl=sl):
                pr.do(lambda e2: e2.tensor_tensor(
                    scrB[:, 0:TOK], wo_out[:, c * TOK:(c + 1) * TOK],
                    b1b[:, 0:TOK], ALU.mult),
                    reads=["wo_out", "b1b"], writes=["scrB"])
                pr.do(lambda e2: e2.tensor_tensor(
                    scrB[:, 0:TOK], scrB[:, 0:TOK], b2b[:, 0:TOK],
                    ALU.subtract),
                    reads=["scrB", "b2b"], writes=["scrB"])
                pr.do(lambda e2: e2.tensor_tensor(
                    xst[sl][:, :], xst[sl][:, :], scrB[:, 0:TOK], ALU.add),
                    reads=["scrB", f"xst{sl}"], writes=[f"xst{sl}"])
            pr.section("DVE", ln2_apply)
            pr.section("POOL", lambda e, c=c, sl=sl: pr.dma(
                x1_dram[c], xst[sl][:, :], reads=[f"xst{sl}"],
                writes=[f"x1d{c}"], q=f"qsx{sl}"))
        pr.barrier()
        if _PHASES <= 3:
            raise _StopBuild

        ppd = psum_phase([("pp_stat", [128, 512], F32),
                          ("pp_b", [128, 512], F32),
                          ("pp_t", [128, 512], F32)])
        xqi = bigB
        bgB = sbin("p45", "bgB", [128, TOK], F32)
        buB = sbin("p45", "buB", [128, TOK], F32)
        s1B = sbin("p4s", "s1B", [128, TOK], F32)

        def mrow_ap(mi):
            return rowv(15, 1024)[0:1, mi:mi + 1]

        pr.section("PE", lambda e: pr.do_group(
            [(lambda e2, mi=mi: e2.matmul(
                ppd["pp_stat"][mi * 32:mi * 32 + 1, 0:1], s_ones_c[:, :],
                mean_acc[:, mi:mi + 1], start=True, stop=True))
             for mi in range(3)],
            reads=["mean_acc", "onesc"], writes=["ppstat"]))
        def mrows(e):
            for mi in range(3):
                pr.do(lambda e2, mi=mi: e2.tensor_scalar(
                    mrow_ap(mi), ppd["pp_stat"][mi * 32:mi * 32 + 1, 0:1],
                    INV_N_GU, 1e-5, ALU.mult, ALU.max),
                    reads=["ppstat"], writes=["rows"])
            for mi in range(3):
                pr.do(lambda e2, mi=mi: e2.tensor_scalar(
                    rowv(15, 1024)[0:1, 4 + mi:5 + mi],
                    mrow_ap(mi), 0.5, None, ALU.mult),
                    reads=["rows"], writes=["rows"])
        pr.section("DVE", mrows)
        for mi in range(3):
            bcast_row(15, 4 + mi, colv(mi), "cols", 1)
        pr.section("DVE", lambda e: pr.do(
            lambda e2: e2.tensor_scalar(cols[:, 4:7], cols[:, 0:3], -1.0,
                                        None, ALU.mult),
            reads=["cols"], writes=["cols"]))
        tap("cols", cols[:, 0:8], "cols")
        tap("mean_acc", mean_acc[:, :], "mean_acc")

        for g in range(2):
            def l3load(c, g=g):
                sl = c % 2
                pr.section("SP", lambda e, c=c, sl=sl, g=g: pr.dma(
                    xst[sl][:, 0:512],
                    x1_dram[c][:, g * 512:g * 512 + 512],
                    reads=[f"x1d{c}"], writes=[f"xst{sl}"], q=f"qx{sl}"))
            stats2048("ln3g%d" % g,
                      lambda c: xst[c % 2][:, 0:512],
                      g * 512, 512, lambda c: f"xst{c % 2}",
                      chunk_load=l3load)
        def amax_init(e):
            pr.do(lambda e2: e2.memset(scrD[:, 0:TOK], -3.0e38),
                  writes=["scrD"])
            pr.do(lambda e2: e2.memset(scrC[:, 0:TOK], 3.0e38),
                  writes=["scrC"])
        pr.section("DVE", amax_init)
        for c in range(NCO):
            sl = c % 2
            pr.section("SP", lambda e, c=c, sl=sl: pr.dma(
                xst[sl][:, :], x1_dram[c], reads=[f"x1d{c}"],
                writes=[f"xst{sl}"], q=f"qx{sl}"))
            def ln3_chunks(e, c=c, sl=sl):
                pr.do(lambda e2: e2.tensor_tensor(
                    scrB[:, 0:TOK], xst[sl][:, :],
                    b1b[:, 0:TOK], ALU.mult),
                    reads=[f"xst{sl}", "b1b"], writes=["scrB"])
                pr.do(lambda e2: e2.tensor_tensor(
                    scrB[:, 0:TOK], scrB[:, 0:TOK],
                    b2b[:, 0:TOK], ALU.subtract),
                    reads=["scrB", "b2b"], writes=["scrB"])
                pr.do(lambda e2: e2.tensor_tensor(
                    scrD[:, 0:TOK], scrD[:, 0:TOK],
                    scrB[:, 0:TOK], ALU.max),
                    reads=["scrB", "scrD"], writes=["scrD"])
                pr.do(lambda e2: e2.tensor_tensor(
                    scrC[:, 0:TOK], scrC[:, 0:TOK],
                    scrB[:, 0:TOK], ALU.min),
                    reads=["scrB", "scrC"], writes=["scrC"])
            pr.section("DVE", ln3_chunks)

        def tokmax(src_key, dst_row_i):
            for tb in range(8):
                pr.section("PE", lambda e, tb=tb: pr.do(
                    lambda e2: e2.transpose(
                        ppd["pp_t"][:, 0:128],
                        scrD[:, tb * 128:(tb + 1) * 128], s_id_f[:, :]),
                    reads=[src_key, "idf"], writes=["ppt"]))
                pr.section("DVE", lambda e: pr.do(
                    lambda e2: e2.reduce_max(colv(10), ppd["pp_t"][:, 0:128],
                                             axis=AX.X),
                    reads=["ppt"], writes=["cols"]))
                pr.section("PE", lambda e: pr.do(
                    lambda e2: e2.transpose(ppd["pp_t"][0:1, 128:256],
                                            colv(10), s_id_f[:, :]),
                    reads=["cols", "idf"], writes=["ppt"]))
                pr.section("DVE", lambda e, tb=tb, dst_row_i=dst_row_i: pr.do(
                    lambda e2: e2.tensor_copy(
                        rowv(dst_row_i, TOK)[0:1, tb * 128:(tb + 1) * 128],
                        ppd["pp_t"][0:1, 128:256]),
                    reads=["ppt"], writes=["rows"]))
        def amax_comb(e):
            pr.do(lambda e2: e2.tensor_scalar(
                scrC[:, 0:TOK], scrC[:, 0:TOK], -1.0, None, ALU.mult),
                reads=["scrC"], writes=["scrC"])
            pr.do(lambda e2: e2.tensor_tensor(
                scrD[:, 0:TOK], scrD[:, 0:TOK], scrC[:, 0:TOK], ALU.max),
                reads=["scrC", "scrD"], writes=["scrD"])
        pr.section("DVE", amax_comb)
        tokmax("scrD", 10)
        def s1_rows(e):
            pr.do(lambda e2: e2.tensor_scalar(rowv(10, TOK), rowv(10, TOK),
                                              1e-5, None, ALU.max),
                  reads=["rows"], writes=["rows"])
            pr.do(lambda e2: e2.reciprocal(rowv(11, TOK), rowv(10, TOK)),
                  reads=["rows"], writes=["rows"])
            pr.do(lambda e2: e2.tensor_scalar(rowv(11, TOK), rowv(11, TOK),
                                              128.0, None, ALU.mult),
                  reads=["rows"], writes=["rows"])
            pr.do(lambda e2: e2.tensor_scalar(rowv(12, TOK), rowv(10, TOK),
                                              mrow_ap(0), 1.0 / 128.0,
                                              ALU.mult, ALU.mult),
                  reads=["rows", "rows"], writes=["rows"])
            pr.do(lambda e2: e2.tensor_scalar(rowv(13, TOK), rowv(10, TOK),
                                              mrow_ap(1), 1.0 / 128.0,
                                              ALU.mult, ALU.mult),
                  reads=["rows", "rows"], writes=["rows"])
        pr.section("DVE", s1_rows)
        for ri, dst, dkey in ((11, s1B, "s1B"), (12, bgB, "bgB"),
                              (13, buB, "buB")):
            for g in range(2):
                bcast_row(ri, g * 512, dst[:, g * 512:g * 512 + 512],
                          dkey, 512)
        for c in range(NCO):
            sl = c % 2
            pr.section("SP", lambda e, c=c, sl=sl: pr.dma(
                xst[sl][:, :], x1_dram[c], reads=[f"x1d{c}"],
                writes=[f"xst{sl}"], q=f"qx{sl}"))
            def qv_fn(e, c=c, sl=sl):
                pr.do(lambda e2: e2.tensor_tensor(
                    scrB[:, 0:TOK], xst[sl][:, :],
                    b1b[:, 0:TOK], ALU.mult),
                    reads=[f"xst{sl}", "b1b"], writes=["scrB"])
                pr.do(lambda e2: e2.tensor_tensor(
                    scrB[:, 0:TOK], scrB[:, 0:TOK], b2b[:, 0:TOK],
                    ALU.subtract),
                    reads=["scrB", "b2b"], writes=["scrB"])
                pr.do(lambda e2: e2.tensor_tensor(
                    scrB[:, 0:TOK], scrB[:, 0:TOK], s1B[:, :], ALU.mult),
                    reads=["scrB", "s1B"], writes=["scrB"])
            pr.section("DVE", qv_fn)
            round_exact(xqi[:, c * TOK:(c + 1) * TOK], "xqi",
                        scrB[:, 0:TOK], "scrB", TOK)
        tap("xqi", xqi[:, :], "xqi")
        tap("s1B", s1B[:, :], "s1B")
        tap("bgB", bgB[:, :], "bgB")
        close_scope("p4s")
        pr.barrier()
        if _PHASES <= 4:
            raise _StopBuild

        ppd = psum_phase([("pp_g0", [128, 512], F32),
                          ("pp_g1", [128, 512], F32),
                          ("pp_u0", [128, 512], F32),
                          ("pp_u1", [128, 512], F32),
                          ("pp_z0", [128, 512], F32),
                          ("pp_z1", [128, 512], F32)])
        utb = [bigA[:, 0:2048], bigA[:, 2048:4096]]
        gtb = [bigA[:, 4096:6144], bigA[:, 6144:8192]]
        islot = bigA[:, 8192:9216]
        zslot = [bigA[:, 9216:10240], bigA[:, 10240:11264]]
        dtb = bigA[:, 11264:13312]
        wdf = sbin("p45", "wdf", [128, HID], F32)

        def ternarize(src, src_key, dst, dst_key, mi):
            def fn(e):
                pr.do(lambda e2: e2.tensor_scalar(
                    dst[:, :], src[:, :], colv(4 + mi), None, ALU.is_lt),
                    reads=[src_key, "cols"], writes=[dst_key])
                pr.do(lambda e2: e2.scalar_tensor_tensor(
                    dst[:, :], src[:, :], colv(mi), dst[:, :], ALU.is_gt,
                    ALU.subtract),
                    reads=[src_key, "cols", dst_key], writes=[dst_key])
            pr.section("DVE", fn)

        def ldgu(oc):
            def sp(e, oc=oc):
                pr.dma(wstr[0][:, :], wg_h[oc], writes=["wsf0"], q="qw0")
                pr.dma(wstr[1][:, :], wu_h[oc], writes=["wsf1"], q="qw1")
            pr.section("SP", sp)

        ldgu(0)
        for oc in range(NIC):
            sl = oc % 2
            pr.section("SP", lambda e, oc=oc: pr.dma(
                wdf[:, :], wd_h[oc], writes=["wdf"], q="qwd"))
            ternarize(wstr[0], "wsf0", gtb[sl], f"gtb{sl}", 0)
            ternarize(wstr[1], "wsf1", utb[sl], f"utb{sl}", 1)
            if oc + 1 < NIC:
                ldgu(oc + 1)
            def mmgu(e, oc=oc, sl=sl):
                for th in range(2):
                    for mat, tb in (("g", gtb[sl]), ("u", utb[sl])):
                        pk = f"pp_{mat}{th}"
                        pr.do_group(
                            [(lambda e2, hc=hc, th=th, tb=tb, pk=pk:
                              e2.matmul(ppd[pk][:, 0:512],
                                        tb[:, hc * 128:(hc + 1) * 128],
                                        xqi[:, hc * TOK + th * 512:
                                            hc * TOK + th * 512 + 512],
                                        start=(hc == 0),
                                        stop=(hc == NCO - 1)))
                             for hc in range(NCO)],
                            reads=[("gtb%d" if mat == "g" else "utb%d") % sl,
                                   "xqi"], writes=[pk])
            pr.section("PE", mmgu)
            def evgu(e):
                for th in range(2):
                    pr.do(lambda e2, th=th: e2.tensor_tensor(
                        scrB[:, th * 512:th * 512 + 512],
                        ppd[f"pp_g{th}"][:, 0:512],
                        bgB[:, th * 512:th * 512 + 512], ALU.mult),
                        reads=[f"pp_g{th}", "bgB"], writes=["scrB"])
                    pr.do(lambda e2, th=th: e2.tensor_tensor(
                        scrC[:, th * 512:th * 512 + 512],
                        ppd[f"pp_u{th}"][:, 0:512],
                        buB[:, th * 512:th * 512 + 512], ALU.mult),
                        reads=[f"pp_u{th}", "buB"], writes=["scrC"])
            pr.section("DVE", evgu)
            pr.section("ACT", lambda e: pr.do(
                lambda e2: e2.activation(scrD[:, 0:TOK], scrB[:, 0:TOK],
                                         AF.Silu),
                reads=["scrB"], writes=["scrD"]))
            pr.section("DVE", lambda e: pr.do(
                lambda e2: e2.tensor_tensor(islot[:, 0:TOK], scrD[:, 0:TOK],
                                            scrC[:, 0:TOK], ALU.mult),
                reads=["scrD", "scrC"], writes=["islot"]))
            def mmz(e):
                for th in range(2):
                    pr.do_group([
                        lambda e2, th=th: e2.matmul(
                            ppd[f"pp_z{th}"][:, 0:512], s_h128[:, :],
                            islot[:, th * 512:th * 512 + 512],
                            start=True, stop=True)],
                        reads=["islot", "h128"], writes=[f"pp_z{th}"])
            pr.section("PE", mmz)
            zs = zslot[oc % 2]
            zk = f"zs{oc % 2}"
            def evz(e, zs=zs, zk=zk):
                for th in range(2):
                    pr.do(lambda e2, th=th: e2.activation(
                        zs[:, th * 512:th * 512 + 512],
                        ppd[f"pp_z{th}"][:, 0:512], AF.Copy),
                        reads=[f"pp_z{th}"], writes=[zk])
            pr.section("ACT", evz)
            pr.section("POOL", lambda e, oc=oc, zs=zs, zk=zk: pr.dma(
                z_dram[oc], zs[:, :], reads=[zk], writes=[f"zd{oc}"],
                q=f"qsz{oc % 2}"))
            ternarize(wdf, "wdf", dtb, "dtb", 2)
            pr.section("POOL", lambda e, oc=oc: pr.dma(
                trd_dram[oc], dtb[:, :], reads=["dtb"], writes=[f"td{oc}"],
                q="qstd"))
        close_scope("p45")
        pr.barrier()
        if _PHASES <= 5:
            raise _StopBuild

        ppd = psum_phase([("pp_y0", [128, 512], F32),
                          ("pp_y1", [128, 512], F32),
                          ("pp_t", [128, 512], F32),
                          ("pp_b", [128, 512], F32)])
        zt = [bigA[:, 0:1024], bigA[:, 1024:2048]]
        ys = [bigA[:, 2048:3072], bigA[:, 3072:4096]]
        zv3 = z_dram.rearrange("a b t -> b a t")
        def amax_init(e):
            pr.do(lambda e2: e2.memset(scrD[:, 0:TOK], -3.0e38),
                  writes=["scrD"])
            pr.do(lambda e2: e2.memset(scrC[:, 0:TOK], 3.0e38),
                  writes=["scrC"])
        pr.section("DVE", amax_init)
        for c in range(NIC):
            sl = c % 2
            def zl(e, c=c, sl=sl):
                if _P6SUB & 1:
                    pr.dma(zt[sl][0:64, :], zv3[2 * c], writes=[f"zt{sl}"],
                           q=f"qz{sl}")
                    pr.dma(zt[sl][64:128, :], zv3[2 * c + 1],
                           writes=[f"zt{sl}"], q=f"qz{sl}")
            pr.section("SP", zl)
            if not (_P6SUB & 1):
                pr.section("DVE", lambda e, sl=sl: pr.do(
                    lambda e2: e2.memset(zt[sl][:, :], 0.0),
                    writes=[f"zt{sl}"]))
            if not (_P6SUB & 2):
                continue
            def mmy(e, c=c, sl=sl):
                for th in range(2):
                    pr.do_group([
                        lambda e2, th=th: e2.matmul(
                            ppd[f"pp_y{th}"][:, 0:512], s_h64b[:, :],
                            zt[sl][:, th * 512:th * 512 + 512],
                            start=True, stop=True)],
                        reads=[f"zt{sl}", "h64b"], writes=[f"pp_y{th}"])
            pr.section("PE", mmy)
            def evy(e, sl=sl):
                for th in range(2):
                    pr.do(lambda e2, th=th: e2.activation(
                        ys[sl][:, th * 512:th * 512 + 512],
                        ppd[f"pp_y{th}"][:, 0:512], AF.Copy),
                        reads=[f"pp_y{th}"], writes=[f"ys{sl}"])
            pr.section("ACT", evy)
            def mx(e, sl=sl):
                for th in range(2):
                    pr.do(lambda e2, th=th: e2.tensor_tensor(
                        scrD[:, th * 512:th * 512 + 512],
                        scrD[:, th * 512:th * 512 + 512],
                        ppd[f"pp_y{th}"][:, 0:512], ALU.max),
                        reads=[f"pp_y{th}", "scrD", f"ys{sl}"],
                        writes=["scrD"])
                    pr.do(lambda e2, th=th: e2.tensor_tensor(
                        scrC[:, th * 512:th * 512 + 512],
                        scrC[:, th * 512:th * 512 + 512],
                        ppd[f"pp_y{th}"][:, 0:512], ALU.min),
                        reads=[f"pp_y{th}", "scrC"], writes=["scrC"])
            pr.section("DVE", mx)
            pr.section("POOL", lambda e, c=c, sl=sl: pr.dma(
                yv_dram[c], ys[sl][:, :], reads=[f"ys{sl}"],
                writes=[f"yv{c}"], q=f"qsy{sl}"))
        if not (_P6SUB & 4):
            raise _StopBuild
        def ymax_comb(e):
            pr.do(lambda e2: e2.tensor_scalar(
                scrC[:, 0:TOK], scrC[:, 0:TOK], -1.0, None, ALU.mult),
                reads=["scrC"], writes=["scrC"])
            pr.do(lambda e2: e2.tensor_tensor(
                scrD[:, 0:TOK], scrD[:, 0:TOK], scrC[:, 0:TOK], ALU.max),
                reads=["scrC", "scrD"], writes=["scrD"])
        pr.section("DVE", ymax_comb)
        tokmax("scrD", 10)
        qmB = sbin("p67", "qmB", [128, TOK], F32)
        bdB = sbin("p67", "bdB", [128, TOK], F32)
        def s2_rows(e):
            pr.do(lambda e2: e2.tensor_scalar(rowv(10, TOK), rowv(10, TOK),
                                              FWHT_NORM, 1e-5, ALU.mult,
                                              ALU.max),
                  reads=["rows"], writes=["rows"])
            pr.do(lambda e2: e2.reciprocal(rowv(11, TOK), rowv(10, TOK)),
                  reads=["rows"], writes=["rows"])
            pr.do(lambda e2: e2.tensor_scalar(rowv(11, TOK), rowv(11, TOK),
                                              128.0 * FWHT_NORM, None,
                                              ALU.mult),
                  reads=["rows"], writes=["rows"])
            pr.do(lambda e2: e2.tensor_scalar(rowv(12, TOK), rowv(10, TOK),
                                              mrow_ap(2), 1.0 / 128.0,
                                              ALU.mult, ALU.mult),
                  reads=["rows", "rows"], writes=["rows"])
        pr.section("DVE", s2_rows)
        for ri, dst, dkey in ((11, qmB, "qmB"), (12, bdB, "bdB")):
            for g in range(2):
                bcast_row(ri, g * 512, dst[:, g * 512:g * 512 + 512],
                          dkey, 512)
        pr.barrier()
        if _PHASES <= 6:
            raise _StopBuild

        ppd = psum_phase([(f"pp_d{i}", [128, 512], F32) for i in range(8)])
        d_out = bigB
        aqs = [bigA[:, 0:1024], bigA[:, 1024:2048]]
        tds = [bigA[:, 2048:3072], bigA[:, 3072:4096]]
        zt_p7 = [bigA[:, 4096:5120], bigA[:, 5120:6144]]

        for oh in range(2):
            for th in range(2):
                first = (oh == 0 and th == 0)
                for c in range(NIC):
                    sl = c % 2
                    def spl(e, c=c, sl=sl, oh=oh, th=th, first=first):
                        pr.dma(tds[sl][:, :],
                               trd_dram[c][:, oh * 1024:oh * 1024 + 1024],
                               writes=[f"tds{sl}"], q=f"qt{sl}")
                        if first:
                            pr.dma(zt_p7[sl][:, :], yv_dram[c],
                                   writes=[f"ybp{sl}"], q=f"qy{sl}")
                        else:
                            pr.dma(aqs[sl][:, 0:512],
                                   aq2_dram[c][:, th * 512:th * 512 + 512],
                                   reads=[f"aqd{c}"], writes=[f"aqs{sl}"],
                                   q=f"qa{sl}")
                    pr.section("SP", spl)
                    if first:
                        pr.section("DVE", lambda e, sl=sl: pr.do(
                            lambda e2: e2.tensor_tensor(
                                scrB[:, 0:TOK], zt_p7[sl][:, :], qmB[:, :],
                                ALU.mult),
                            reads=[f"ybp{sl}", "qmB"], writes=["scrB"]))
                        if c == 0:
                            tap("qv_c0", scrB[:, 0:TOK], "scrB")
                            tap("yb_c0", zt_p7[sl][:, :], f"ybp{sl}")
                            tap("qmB", qmB[:, :], "qmB")
                        round_exact(aqs[sl][:, 0:TOK], f"aqs{sl}",
                                    scrB[:, 0:TOK], "scrB", TOK,
                                    dbg=("r7" if (_DEBUG and c == 0)
                                         else None))
                        pr.section("POOL", lambda e, c=c, sl=sl: pr.dma(
                            aq2_dram[c], aqs[sl][:, :], reads=[f"aqs{sl}"],
                            writes=[f"aqd{c}"], q=f"qsa{sl}"))
                    def mmd(e, c=c, sl=sl, th=th, first=first):
                        rhs = aqs[sl][:, th * 512:th * 512 + 512] if first \
                            else aqs[sl][:, 0:512]
                        builds = []
                        for j in range(8):
                            builds.append(
                                (lambda e2, j=j, sl=sl, rhs=rhs:
                                 e2.matmul(ppd[f"pp_d{j}"][:, 0:512],
                                           tds[sl][:, j * 128:(j + 1) * 128],
                                           rhs, start=(c == 0),
                                           stop=(c == NIC - 1))))
                        pr.do_group(builds, reads=[f"tds{sl}", f"aqs{sl}"],
                                    writes=["ppd"])
                    pr.section("PE", mmd)
                def evd(e, oh=oh, th=th):
                    for j in range(8):
                        oc = oh * 8 + j
                        pr.do(lambda e2, j=j, oc=oc, th=th: e2.tensor_tensor(
                            d_out[:, oc * TOK + th * 512:
                                  oc * TOK + th * 512 + 512],
                            ppd[f"pp_d{j}"][:, 0:512],
                            bdB[:, th * 512:th * 512 + 512], ALU.mult),
                            reads=["ppd", "bdB"], writes=["d_out"])
                pr.section("DVE", evd)
        close_scope("p67")
        pr.barrier()
        if _PHASES <= 7:
            raise _StopBuild

        ppd = psum_phase([("pp_stat", [128, 512], F32),
                          ("pp_b", [128, 512], F32)])
        for g in range(2):
            stats2048("ln4g%d" % g,
                      lambda c, g=g: d_out[:, c * TOK + g * 512:
                                           c * TOK + g * 512 + 512],
                      g * 512, 512, "d_out", bf_src=True)
        for c in range(NCO):
            sl = c % 2
            pr.section("SP", lambda e, c=c, sl=sl: pr.dma(
                xst[sl][:, :], x1_dram[c], reads=[f"x1d{c}"],
                writes=[f"xst{sl}"], q=f"qx{sl}"))
            def ln4_apply(e, c=c, sl=sl):
                pr.do(lambda e2: e2.tensor_tensor(
                    scrB[:, 0:TOK], d_out[:, c * TOK:(c + 1) * TOK],
                    b1b[:, 0:TOK], ALU.mult),
                    reads=["d_out", "b1b"], writes=["scrB"])
                pr.do(lambda e2: e2.tensor_tensor(
                    scrB[:, 0:TOK], scrB[:, 0:TOK], b2b[:, 0:TOK],
                    ALU.subtract),
                    reads=["scrB", "b2b"], writes=["scrB"])
                pr.do(lambda e2: e2.tensor_tensor(
                    xst[sl][:, :], xst[sl][:, :], scrB[:, 0:TOK], ALU.add),
                    reads=["scrB", f"xst{sl}"], writes=[f"xst{sl}"])
            pr.section("DVE", ln4_apply)
            pr.section("POOL", lambda e, c=c, sl=sl: pr.dma(
                y_h[c * 128:(c + 1) * 128, :], xst[sl][:, :],
                reads=[f"xst{sl}"], q=f"qso{sl}"))
        t0, t1 = pr.cnt.get("qso0", 0), pr.cnt.get("qso1", 0)
        def final_wait(e):
            pr._wait("qso0", t0)
            pr._wait("qso1", t1)
        pr.section("POOL", final_wait)
    except _StopBuild:
        pass
    return nc


def _stage(inputs):
    x = np.asarray(inputs["L0_attn_x"], dtype=np.float32)
    wqkv = _stage_stripes(np.asarray(inputs["L0_attn_w_qkv"], np.float32), 24)
    wo = np.ascontiguousarray(
        np.asarray(inputs["L0_attn_w_o"], np.float32).T).reshape(16, 128, HID)
    wg = _stage_stripes(np.asarray(inputs["L0_ffn_w_gate"], np.float32), NIC)
    wu = _stage_stripes(np.asarray(inputs["L0_ffn_w_up"], np.float32), NIC)
    wdT = np.ascontiguousarray(np.asarray(inputs["L0_ffn_w_down"],
                                          np.float32).T)
    perm = np.empty(IFF, dtype=np.int64)
    for c in range(NIC):
        for p in range(128):
            perm[c * 128 + p] = (p % 64) * 128 + 2 * c + p // 64
    wd = np.ascontiguousarray(wdT[perm].reshape(NIC, 128, HID))
    cos, sin = _rope_tables()
    sinf = sin.copy()
    sinf[:64, :] = -sinf[:64, :]
    qn_g = np.asarray(inputs["L0_attn_qn_g"], np.float32).reshape(128, 1)
    qn_b = np.asarray(inputs["L0_attn_qn_b"], np.float32).reshape(128, 1)
    kn_g = np.asarray(inputs["L0_attn_kn_g"], np.float32).reshape(128, 1)
    kn_b = np.asarray(inputs["L0_attn_kn_b"], np.float32).reshape(128, 1)

    in_maps = []
    meta = []
    for c in range(8):
        bc = c // 2
        blocks = (0, 3) if c % 2 == 0 else (1, 2)
        own = np.concatenate([np.arange(b * 512, (b + 1) * 512)
                              for b in blocks])
        xk = np.ascontiguousarray(x[bc].T)
        xq = np.ascontiguousarray(x[bc].T[:, own])
        cosq = np.ascontiguousarray(cos[:, own])
        sinq = np.ascontiguousarray(sinf[:, own])
        mask = np.zeros((16, 128, 512), dtype=ml_dtypes.bfloat16)
        for (qb, kt) in MASKED:
            i = MASK_IDX[(qb, kt)]
            q0 = blocks[qb] * 512
            kk = kt * 128 + np.arange(128)[:, None]
            qq = q0 + np.arange(512)[None, :]
            mask[i] = (kk <= qq).astype(np.float32).astype(ml_dtypes.bfloat16)
        in_maps.append(dict(
            xq=xq, xk=xk, cosq=cosq, sinq=sinq,
            cosk=np.ascontiguousarray(cos), sink=np.ascontiguousarray(sinf),
            mask01=mask, qn_g=qn_g, qn_b=qn_b, kn_g=kn_g, kn_b=kn_b,
            wqkv=wqkv, wo=wo, wg=wg, wu=wu, wd=wd))
        meta.append((bc, blocks))
    return in_maps, meta


def kernel(**inputs):
    in_maps, meta = _stage(inputs)
    nc = build_program()
    res = run_bass_kernel_spmd(nc, in_maps, core_ids=list(range(8)))
    out = np.empty((B, S, HID), dtype=np.float32)
    for c in range(8):
        bc, blocks = meta[c]
        y = np.asarray(res.results[c]["y"])
        for qi, b in enumerate(blocks):
            out[bc, b * 512:(b + 1) * 512, :] = y[:, qi * 512:(qi + 1) * 512].T
    return out


# version 24 (final)
# speedup vs baseline: 1.0227x; 1.0144x over previous
import contextlib
import numpy as np
import ml_dtypes
import concourse.bass as bass
import concourse.mybir as mybir
from concourse.bass_utils import run_bass_kernel_spmd

F32 = mybir.dt.float32
BF16 = mybir.dt.bfloat16
I32 = mybir.dt.int32
AF = mybir.ActivationFunctionType
ALU = mybir.AluOpType
AX = mybir.AxisListType

B, S, HID = 4, 2048, 2048
H, HKV, D = 16, 4, 128
IFF = 8192
NCO = 16
NIC = 64
TOK = 1024
EPS = 1e-5
INV_SQRT_D = 1.0 / float(np.float32(np.sqrt(np.float32(128.0))))
FWHT_NORM = float(np.float32(1.0 / np.sqrt(8192.0)))
MAGIC = 12582912.0
MAGIC64 = 6755399441055744.0
INV_N_GU = 1.0 / float(IFF * HID)

MASKED = [(0, kt) for kt in range(8)] + [(1, kt) for kt in range(8, 16)]
MASK_IDX = {qk: i for i, qk in enumerate(MASKED)}
KT_OF_QB = {0: 8, 1: 16}
_DEBUG = False
_PHASES = 99
_P6SUB = 7


class _StopBuild(Exception):
    pass


def _rope_tables():
    import jax
    import jax.numpy as jnp
    cpu = jax.devices("cpu")[0]
    with jax.default_device(cpu):
        inv = 1.0 / (1000000.0 ** (jnp.arange(0, D, 2, dtype=jnp.float32) / D))
        t = jnp.arange(S, dtype=jnp.float32)
        fr = jnp.outer(t, inv)
        cos = np.asarray(jnp.concatenate([jnp.cos(fr), jnp.cos(fr)], axis=-1))
        sin = np.asarray(jnp.concatenate([jnp.sin(fr), jnp.sin(fr)], axis=-1))
    return cos.T.copy(), sin.T.copy()


def _stage_stripes(w, n_oc):
    out_dim, in_dim = w.shape
    nhc = in_dim // 128
    st = np.empty((n_oc, 128, in_dim), dtype=np.float32)
    for oc in range(n_oc):
        s = np.ascontiguousarray(w[oc * 128:(oc + 1) * 128, :].T)
        st[oc] = s.reshape(nhc, 128, 128).transpose(1, 0, 2).reshape(128,
                                                                     in_dim)
    return st


class Pr:
    ENG = {"PE": "tensor", "ACT": "scalar", "DVE": "vector", "POOL": "gpsimd",
           "SP": "sync"}
    QUEUES = {"SP": "QSP", "POOL": "QPOOL"}

    def __init__(self, nc, es):
        self.nc = nc
        self.es = es
        self.block = es.enter_context(nc.Block())
        self.sem = {}
        self.cnt = {}
        for name in ["PE", "ACT", "DVE", "POOL"]:
            self.sem[name] = es.enter_context(nc.semaphore(name.lower()))
            self.cnt[name] = 0
        self.waited = {}
        self.lastw = {}
        self.readers = {}
        self.cur = None

    def qsem(self, name):
        if name not in self.sem:
            self.sem[name] = self.es.enter_context(
                self.nc.semaphore(name.lower()))
            self.cnt[name] = 0
        return name

    def section(self, eng_name, fn):
        def body(e):
            self.cur = (eng_name, e)
            fn(e)
            self.cur = None
        getattr(self.block, self.ENG[eng_name])(body)

    def _wait(self, sem_name, val):
        if val <= 0:
            return
        key = (self.cur[0], sem_name)
        if self.waited.get(key, 0) >= val:
            return
        self.cur[1].wait_ge(self.sem[sem_name], val)
        self.waited[key] = val

    def _need(self, reads, writes):
        for k in reads:
            w = self.lastw.get(k)
            if w:
                self._wait(*w)
        for k in writes:
            w = self.lastw.get(k)
            if w:
                self._wait(*w)
            for r in self.readers.get(k, ()):
                self._wait(*r)

    def _did(self, sem_name, reads, writes):
        c = self.cnt[sem_name]
        for k in writes:
            self.lastw[k] = (sem_name, c)
            self.readers[k] = []
        for k in reads:
            self.readers.setdefault(k, []).append((sem_name, c))

    def do(self, build, reads=(), writes=()):
        eng_name, e = self.cur
        self._need(reads, writes)
        inst = build(e)
        self.cnt[eng_name] += 1
        inst.then_inc(self.sem[eng_name], 1)
        self._did(eng_name, reads, writes)
        return inst

    def do_group(self, builds, reads=(), writes=()):
        eng_name, e = self.cur
        self._need(reads, writes)
        inst = None
        for b in builds:
            inst = b(e)
        self.cnt[eng_name] += 1
        inst.then_inc(self.sem[eng_name], 1)
        self._did(eng_name, reads, writes)
        return inst

    def dma(self, out, in_, reads=(), writes=(), q="q_dflt"):
        eng_name, e = self.cur
        self.qsem(q)
        self._need(reads, writes)
        inst = e.dma_start(out=out, in_=in_)
        self.cnt[q] += 16
        inst.then_inc(self.sem[q], 16)
        c = self.cnt[q]
        for k in writes:
            self.lastw[k] = (q, c)
            self.readers[k] = []
        for k in reads:
            self.readers.setdefault(k, []).append((q, c))
        return inst

    def barrier(self):
        snap = dict(self.cnt)
        for en in ["PE", "ACT", "DVE", "POOL", "SP"]:
            def fn(e, en=en):
                for sn, v in snap.items():
                    if sn != en and v > 0:
                        self._wait(sn, v)
            self.section(en, fn)


def build_program():
    nc = bass.Bass("TRN2", target_bir_lowering=False)

    def din(name, shape, dtype=F32):
        return nc.dram_tensor(name, shape, dtype, kind="ExternalInput").ap()

    xq_h = din("xq", [HID, TOK])
    xk_h = din("xk", [HID, S])
    cosq_h = din("cosq", [D, TOK]); sinq_h = din("sinq", [D, TOK])
    cosk_h = din("cosk", [D, S]); sink_h = din("sink", [D, S])
    mask_h = din("mask01", [16, 128, 512], BF16)
    qng_h = din("qn_g", [D, 1]); qnb_h = din("qn_b", [D, 1])
    kng_h = din("kn_g", [D, 1]); knb_h = din("kn_b", [D, 1])
    wqkv_h = din("wqkv", [24, 128, HID])
    wo_h = din("wo", [16, 128, HID])
    wg_h = din("wg", [NIC, 128, HID])
    wu_h = din("wu", [NIC, 128, HID])
    wd_h = din("wd", [NIC, 128, HID])
    y_h = nc.dram_tensor("y", [HID, TOK], F32, kind="ExternalOutput").ap()

    z_dram = nc.dram_tensor("z_scr", [NIC, 128, TOK], BF16, kind="Internal").ap()
    yv_dram = nc.dram_tensor("yv_scr", [NIC, 128, TOK], BF16,
                             kind="Internal").ap()
    aq2_dram = nc.dram_tensor("aq2_scr", [NIC, 128, TOK], BF16,
                              kind="Internal").ap()
    trd_dram = nc.dram_tensor("trd_scr", [NIC, 128, HID], BF16,
                              kind="Internal").ap()
    x1_dram = nc.dram_tensor("x1_scr", [NCO, 128, TOK], F32,
                             kind="Internal").ap()

    ident_bf = nc.inline_tensor(np.eye(128, dtype=ml_dtypes.bfloat16), "id_bf")
    ident_f = nc.inline_tensor(np.eye(128, dtype=np.float32), "id_f")
    ones_col_t = nc.inline_tensor(np.ones((128, 1), dtype=np.float32), "ones_c")
    ones_row_t = nc.inline_tensor(np.ones((1, 128), dtype=np.float32), "ones_r")
    ones_all_t = nc.inline_tensor(np.ones((128, 128), dtype=np.float32),
                                  "ones_a")
    eps_col_t = nc.inline_tensor(np.full((128, 1), EPS, dtype=np.float32),
                                 "eps_c")
    srt_t = nc.inline_tensor(np.full((1, 1), np.sqrt(np.float32(128.0)),
                                     dtype=np.float32), "srt_c")
    hmat = np.ones((1, 1), dtype=np.float64)
    while hmat.shape[0] < 128:
        hmat = np.block([[hmat, hmat], [hmat, -hmat]])
    h64 = hmat[:64, :64]
    h64blk = np.zeros((128, 128), dtype=np.float64)
    h64blk[:64, :64] = h64; h64blk[64:, 64:] = h64
    h128_t = nc.inline_tensor(hmat.astype(ml_dtypes.bfloat16), "h128")
    h64b_t = nc.inline_tensor(h64blk.astype(ml_dtypes.bfloat16), "h64b")

    es = contextlib.ExitStack()
    try:
      with es:
        pr = Pr(nc, es)
        scopes = {}
        tapn = [0]

        def tap(name, ap, key):
            if not _DEBUG:
                return
            t = nc.dram_tensor("tap_" + name, list(ap.shape), ap.dtype,
                               kind="Internal").ap()
            tapn[0] += 1
            pr.section("POOL", lambda e: pr.dma(t, ap, reads=[key],
                                                q="qtap%d" % tapn[0]))

        def sbin(scope, name, shape, dt=F32):
            st = scopes.setdefault(scope, contextlib.ExitStack())
            return st.enter_context(nc.sbuf_tensor(name, shape, dt))

        def close_scope(scope):
            if scope in scopes:
                scopes.pop(scope).close()

        sb = lambda n, s, d=F32: sbin("all", n, s, d)

        qng = sb("qng", [128, 1]); qnb = sb("qnb", [128, 1])
        kng = sb("kng", [128, 1]); knb = sb("knb", [128, 1])
        s_id_bf = sb("s_id_bf", [128, 128], BF16)
        s_id_f = sb("s_id_f", [128, 128], F32)
        s_ones_c = sb("s_ones_c", [128, 1], F32)
        s_ones_r = sb("s_ones_r", [1, 128], F32)
        s_eps = sb("s_eps", [128, 1], F32)
        s_h128 = sb("s_h128", [128, 128], BF16)
        s_h64b = sb("s_h64b", [128, 128], BF16)
        onescol_bf = sb("onescol_bf", [128, 1], BF16)
        s_ones_all = sb("s_ones_all", [128, 128], F32)
        rows = sb("rows", [128, 4096], F32)
        _RP = {0: 32, 1: 0, 2: 0, 3: 32, 4: 64, 5: 64, 6: 0, 7: 32,
               8: 0, 9: 0, 10: 32, 11: 0, 12: 32, 13: 64, 14: 32, 15: 64}
        _RC = {0: 0, 1: 0, 2: 512, 3: 512, 4: 0, 5: 1024, 6: 1024,
               7: 1024, 8: 2048, 9: 2560, 10: 2048, 11: 3072, 12: 3072,
               13: 3072, 14: 512, 15: 512}
        cols = sb("cols", [128, 32], F32)
        mean_acc = sb("mean_acc", [128, 4], F32)
        scrA = sb("scrA", [128, 1024], F32)
        scrB = sb("scrB", [128, 1024], F32)
        scrC = sb("scrC", [128, 1024], F32)
        scrD = sb("scrD", [128, 1024], F32)
        b1b = sb("b1b", [128, 1024], F32)
        b2b = sb("b2b", [128, 1024], F32)
        wstr = [sb(f"wstr{i}", [128, HID], F32) for i in range(2)]
        wstr_bf = [sb(f"wstrbf{i}", [128, HID], BF16) for i in range(2)]
        bigA = sb("bigA", [128, NCO * TOK], BF16)
        bigB = sb("bigB", [128, NCO * TOK], BF16)

        def rowv(i, n=1024):
            p = _RP[i]
            return rows[p:p + 1, _RC[i]:_RC[i] + n]

        def colv(i):
            return cols[:, i:i + 1]

        def p0_sp(e):
            pr.dma(qng[:, :], qng_h, writes=["qng"], q="qc_qng")
            pr.dma(qnb[:, :], qnb_h, writes=["qnb"], q="qc_qnb")
            pr.dma(kng[:, :], kng_h, writes=["kng"], q="qc_kng")
            pr.dma(knb[:, :], knb_h, writes=["knb"], q="qc_knb")
            pr.dma(s_id_bf[:, :], ident_bf.ap(), writes=["idbf"], q="qc_idbf")
            pr.dma(s_id_f[:, :], ident_f.ap(), writes=["idf"], q="qc_idf")
            pr.dma(s_ones_c[:, :], ones_col_t.ap(), writes=["onesc"], q="qc_onesc")
            pr.dma(s_ones_r[:, :], ones_row_t.ap(), writes=["onesr"], q="qc_onesr")
            pr.dma(s_ones_all[:, :], ones_all_t.ap(), writes=["onesall"],
                   q="qc_onesall")
            pr.dma(s_eps[:, :], eps_col_t.ap(), writes=["s_eps"], q="qc_s_eps")
            pr.dma(s_h128[:, :], h128_t.ap(), writes=["h128"], q="qc_h128")
            pr.dma(s_h64b[:, :], h64b_t.ap(), writes=["h64b"], q="qc_h64b")
        pr.section("SP", p0_sp)

        def p0_dve(e):
            pr.do(lambda e2: e2.memset(mean_acc[:, :], 0.0),
                  writes=["mean_acc"])
            pr.do(lambda e2: e2.memset(cols[:, :], 0.0),
                  writes=["cols", "mcol0", "mcol1"])
            pr.do(lambda e2: e2.tensor_copy(onescol_bf[:, :], s_ones_c[:, :]),
                  reads=["onesc"], writes=["onescbf"])
        pr.section("DVE", p0_dve)

        mslot = [None, None]
        mean_jobs = []
        for mi, wh in enumerate((wg_h, wu_h, wd_h)):
            for oc in range(NIC):
                for hf in range(2):
                    mean_jobs.append((wh[oc][:, hf * 1024:hf * 1024 + 1024],
                                      mi))
        mean_state = {"dma": 0, "cmp": 0}

        def mean_emit(n):
            j0 = mean_state["cmp"]
            j1 = min(j0 + n, len(mean_jobs))
            if j0 >= j1:
                return
            def emit_dmas(bound):
                while mean_state["dma"] <= min(bound, len(mean_jobs) - 1):
                    jd = mean_state["dma"]
                    srcd, _m = mean_jobs[jd]
                    sld = jd % 2
                    pr.dma(mslot[sld][:, :], srcd, writes=[f"mslot{sld}"],
                           q=f"qm{sld}")
                    mean_state["dma"] += 1
            for j in range(j0, j1):
                src_, mi = mean_jobs[j]
                sl = j % 2
                ck = f"mcol{sl}"
                def act_fn(e, j=j, sl=sl, ck=ck):
                    emit_dmas(j + 1)
                    pr.do(lambda e2: e2.activation(
                        mslot[sl][:, :], mslot[sl][:, :], AF.Abs,
                        accum_out=colv(8 + sl)),
                        reads=[f"mslot{sl}"], writes=[f"mslot{sl}", ck])
                    emit_dmas(j + 2)
                pr.section("ACT", act_fn)
                pr.section("DVE", lambda e, mi=mi, sl=sl, ck=ck: pr.do(
                    lambda e2: e2.tensor_tensor(
                        mean_acc[:, mi:mi + 1], mean_acc[:, mi:mi + 1],
                        colv(8 + sl), ALU.add),
                    reads=[ck, "mean_acc"], writes=["mean_acc"]))
            mean_state["cmp"] = j1

        psum_es = [None, 0]

        def psum_phase(spec):
            if psum_es[0] is not None:
                psum_es[0].close()
            psum_es[0] = contextlib.ExitStack()
            psum_es[1] += 1
            out = {}
            for name, shape, dt in spec:
                out[name] = psum_es[0].enter_context(
                    nc.psum_tensor(f"{name}_{psum_es[1]}", shape, dt))
            return out

        ppd = {}

        def stats2048(tag, chunk_ap, T0, T, src_key, bf_src=False,
                      chunk_load=None):
            kS, kS2 = f"{tag}S", f"{tag}S2"
            ones_ap = onescol_bf[:, :] if bf_src else s_ones_c[:, :]
            ones_key = "onescbf" if bf_src else "onesc"
            for c in range(NCO):
                if chunk_load is not None:
                    chunk_load(c)
                skey = src_key(c) if callable(src_key) else src_key
                pr.section("PE", lambda e, c=c, skey=skey: pr.do_group(
                    [lambda e2: e2.matmul(
                        ppd["pp_stat"][0:1, 0:T], ones_ap, chunk_ap(c),
                        start=(c == 0), stop=(c == NCO - 1))],
                    reads=[skey, ones_key], writes=[kS, "ppstat"]))
                sl = c % 2
                scr = scrA[:, sl * 512:sl * 512 + 512][:, 0:T]
                pr.section("DVE", lambda e, c=c, scr=scr, skey=skey: pr.do(
                    lambda e2: e2.tensor_tensor(scr, chunk_ap(c), chunk_ap(c),
                                                ALU.mult),
                    reads=[skey], writes=["scrA"]))
                pr.section("PE", lambda e, c=c, scr=scr: pr.do_group(
                    [lambda e2: e2.matmul(ppd["pp_stat"][64:65, 0:T],
                                          s_ones_c[:, :], scr,
                                          start=(c == 0),
                                          stop=(c == NCO - 1))],
                    reads=["scrA", "onesc"], writes=[kS2, "ppstat"]))
            def dve_rows(e):
                pr.do(lambda e2: e2.tensor_scalar(
                    rowv(0, T), ppd["pp_stat"][0:1, 0:T], 1.0 / 2048.0, None,
                    ALU.mult), reads=[kS], writes=["hr0"])
                pr.do(lambda e2: e2.tensor_scalar(
                    rowv(1, T), ppd["pp_stat"][64:65, 0:T], 1.0 / 2048.0,
                    None, ALU.mult), reads=[kS2], writes=["hr0"])
                pr.do(lambda e2: e2.tensor_tensor(
                    rowv(2, T), rowv(0, T), rowv(0, T), ALU.mult),
                    reads=["hr0"], writes=["hr0"])
                pr.do(lambda e2: e2.tensor_tensor(
                    rowv(1, T), rowv(1, T), rowv(2, T), ALU.subtract),
                    reads=["hr0"], writes=["hr0"])
            pr.section("DVE", dve_rows)
            pr.section("ACT", lambda e: pr.do(
                lambda e2: e2.activation(rowv(2, T), rowv(1, T), AF.Sqrt,
                                         bias=s_eps[0:1, 0:1]),
                reads=["hr0", "s_eps"], writes=["hr0"]))
            def dve_rows2(e):
                pr.do(lambda e2: e2.reciprocal(rowv(3, T), rowv(2, T)),
                      reads=["hr0"], writes=["hr0"])
                pr.do(lambda e2: e2.tensor_tensor(
                    rowv(4, T), rowv(0, T), rowv(3, T), ALU.mult),
                    reads=["hr0"], writes=["hr0"])
            pr.section("DVE", dve_rows2)
            bcast_row(3, 0, b1b[:, T0:T0 + T], "b1b", T, row_key="hr0")
            bcast_row(4, 0, b2b[:, T0:T0 + T], "b2b", T, row_key="hr0")

        def bcast_row(ri, col0, dst_ap, dst_key, T, row_key="rows"):
            p = _RP[ri]
            src = rows[p:p + 1, _RC[ri] + col0:_RC[ri] + col0 + T]
            pr.section("PE", lambda e: pr.do(
                lambda e2: e2.matmul(ppd["pp_b"][:, 0:T],
                                     s_ones_all[p:p + 1, :], src,
                                     start=True, stop=True),
                reads=[row_key, "onesall"], writes=["ppb"]))
            pr.section("DVE", lambda e: pr.do(
                lambda e2: e2.tensor_copy(dst_ap, ppd["pp_b"][:, 0:T]),
                reads=["ppb"], writes=[dst_key]))

        def round_exact(dst_ap, dst_key, v_ap, v_key, T, dbg=None):
            kf = scrKI[:, 0:T].bitcast(F32)
            def dve1(e):
                pr.do(lambda e2: e2.tensor_scalar(
                    scrC[:, 0:T], v_ap, MAGIC, MAGIC, ALU.add, ALU.subtract),
                    reads=[v_key], writes=["scrC"])
                pr.do(lambda e2: e2.tensor_copy(kf, scrC[:, 0:T]),
                      reads=["scrC"], writes=["scrKI"])
                pr.do(lambda e2: e2.scalar_tensor_tensor(
                    scrC[:, 0:T], kf, -1.0, v_ap, ALU.mult, ALU.add),
                    reads=["scrKI", v_key], writes=["scrC"])
            pr.section("DVE", dve1)
            if dbg:
                tap(dbg + "k0", kf, "scrKI")
                tap(dbg + "d", scrC[:, 0:T], "scrC")
            def dve3(e):
                pr.do(lambda e2: e2.tensor_scalar(
                    scrC[:, 0:T], scrC[:, 0:T], MAGIC64, MAGIC64,
                    ALU.add, ALU.subtract),
                    reads=["scrC"], writes=["scrC"])
                pr.do(lambda e2: e2.scalar_tensor_tensor(
                    scrC[:, 0:T], scrC[:, 0:T], 1.0, kf, ALU.mult, ALU.add),
                    reads=["scrC", "scrKI"], writes=["scrC"])
                pr.do(lambda e2: e2.tensor_scalar(
                    dst_ap, scrC[:, 0:T], -128.0, 127.0, ALU.max, ALU.min),
                    reads=["scrC"], writes=[dst_key])
            pr.section("DVE", dve3)

        def load_stripe(sl, src_ap, n=HID):
            pr.section("SP", lambda e: pr.dma(wstr[sl][:, 0:n], src_ap,
                                              writes=[f"wsf{sl}"],
                                              q=f"qw{sl}"))
            pr.section("DVE", lambda e: pr.do(
                lambda e2: e2.tensor_copy(wstr_bf[sl][:, 0:n],
                                          wstr[sl][:, 0:n]),
                reads=[f"wsf{sl}"], writes=[f"wsb{sl}"]))

        def ln_from_dram(tag, dram_ap, col0, dst, xg):
            for g in range(4):
                mean_emit(6)
                T0g = g * 256
                kx = "xgrp"
                def sp(e, g=g, kx=kx):
                    for c in range(NCO):
                        pr.dma(xg[c][:, :],
                               dram_ap[c * 128:(c + 1) * 128,
                                       col0 + g * 256: col0 + g * 256 + 256],
                               writes=[kx], q="qxg")
                pr.section("SP", sp)
                stats2048(tag + str(g), lambda c: xg[c][:, :],
                          0, 256, kx)
                def dve_apply(e, g=g, kx=kx):
                    for c in range(NCO):
                        pr.do(lambda e2, c=c: e2.tensor_tensor(
                            xg[c][:, :], xg[c][:, :], b1b[:, 0:256],
                            ALU.mult),
                            reads=["b1b", kx], writes=[kx])
                        pr.do(lambda e2, c=c, g=g: e2.tensor_tensor(
                            dst[:, c * TOK + g * 256: c * TOK + g * 256 + 256],
                            xg[c][:, :], b2b[:, 0:256], ALU.subtract),
                            reads=["b2b", kx], writes=["lnb"])
                pr.section("DVE", dve_apply)

        HLR = ((0, 1, 2, 3, 4), (10, 11, 6, 12, 13))

        def head_ln_rope(tag, src, src_key, T, gcol, gbkeys, bcol, cosT, sinT,
                         ckey, dst_ap, dst_key, par=0):
            R = HLR[par]
            rk = f"hr{par}"
            pst = "ppstat" if par == 0 else "ppstat1"
            psn = "pp_stat" if par == 0 else "pp_st2"
            pb = 0
            for g in range(T // 512):
                sl512 = g * 512
                sq = sqbuf[:, par * 512:par * 512 + 512]
                sqk = f"sq{par}"
                pr.section("DVE", lambda e, sl512=sl512, sq=sq, sqk=sqk:
                           pr.do(lambda e2: e2.tensor_tensor(
                               sq, src[:, sl512:sl512 + 512],
                               src[:, sl512:sl512 + 512], ALU.mult),
                               reads=[src_key], writes=[sqk]))
                pr.section("PE", lambda e, sl512=sl512, sq=sq, sqk=sqk: (
                    pr.do(lambda e2: e2.matmul(
                        ppd[psn][pb:pb + 1, 0:512], s_ones_c[:, :],
                        src[:, sl512:sl512 + 512], start=True, stop=True),
                        reads=[src_key, "onesc"], writes=[pst]),
                    pr.do(lambda e2: e2.matmul(
                        ppd[psn][pb + 64:pb + 65, 0:512],
                        s_ones_c[:, :], sq, start=True, stop=True),
                        reads=[sqk, "onesc"], writes=[pst])))
                def rows_fn(e):
                    pr.do(lambda e2: e2.tensor_scalar(
                        rowv(R[0], 512), ppd[psn][pb:pb + 1, 0:512],
                        1.0 / 128.0, None, ALU.mult),
                        reads=[pst], writes=[rk])
                    pr.do(lambda e2: e2.tensor_scalar(
                        rowv(R[1], 512), ppd[psn][pb + 64:pb + 65, 0:512],
                        1.0 / 128.0, None, ALU.mult),
                        reads=[pst], writes=[rk])
                    pr.do(lambda e2: e2.tensor_tensor(
                        rowv(R[2], 512), rowv(R[0], 512), rowv(R[0], 512),
                        ALU.mult), reads=[rk], writes=[rk])
                    pr.do(lambda e2: e2.tensor_tensor(
                        rowv(R[1], 512), rowv(R[1], 512), rowv(R[2], 512),
                        ALU.subtract), reads=[rk], writes=[rk])
                pr.section("DVE", rows_fn)
                pr.section("ACT", lambda e: pr.do(
                    lambda e2: e2.activation(rowv(R[2], 512), rowv(R[1], 512),
                                             AF.Sqrt, bias=s_eps[0:1, 0:1]),
                    reads=[rk, "s_eps"], writes=[rk]))
                def rows2_fn(e):
                    pr.do(lambda e2: e2.reciprocal(rowv(R[3], 512),
                                                   rowv(R[2], 512)),
                          reads=[rk], writes=[rk])
                    pr.do(lambda e2: e2.tensor_tensor(
                        rowv(R[4], 512), rowv(R[0], 512), rowv(R[3], 512),
                        ALU.mult), reads=[rk], writes=[rk])
                pr.section("DVE", rows2_fn)
                bcast_row(R[3], 0, b1b[:, sl512:sl512 + 512], "b1b", 512,
                          row_key=rk)
                bcast_row(R[4], 0, b2b[:, sl512:sl512 + 512], "b2b", 512,
                          row_key=rk)
            def apply_fn(e):
                pr.do(lambda e2: e2.tensor_tensor(
                    scrB[:, 0:T], src[:, 0:T], b1b[:, 0:T], ALU.mult),
                    reads=[src_key, "b1b"], writes=["scrB"])
                pr.do(lambda e2: e2.tensor_tensor(
                    scrB[:, 0:T], scrB[:, 0:T], b2b[:, 0:T], ALU.subtract),
                    reads=["scrB", "b2b"], writes=["scrB"])
            pr.section("DVE", apply_fn)
            pr.section("DVE", lambda e: pr.do(
                lambda e2: e2.tensor_scalar(scrB[:, 0:T], scrB[:, 0:T],
                                            gcol[:, 0:1], bcol[:, 0:1],
                                            ALU.mult, ALU.add),
                reads=["scrB"] + list(gbkeys), writes=["scrB"]))
            def rope_fn(e):
                pr.do(lambda e2: e2.tensor_tensor(
                    scrC[:, 0:T], scrB[:, 0:T], cosT, ALU.mult),
                    reads=["scrB", ckey], writes=["scrC"])
                pr.do(lambda e2: e2.tensor_copy(
                    scrD[0:64, 0:T], scrB[64:128, 0:T]),
                    reads=["scrB"], writes=["scrD"])
                pr.do(lambda e2: e2.tensor_copy(
                    scrD[64:128, 0:T], scrB[0:64, 0:T]),
                    reads=["scrB"], writes=["scrD"])
                pr.do(lambda e2: e2.tensor_tensor(
                    scrD[:, 0:T], scrD[:, 0:T], sinT, ALU.mult),
                    reads=["scrD", ckey], writes=["scrD"])
                pr.do(lambda e2: e2.tensor_tensor(
                    dst_ap, scrC[:, 0:T], scrD[:, 0:T], ALU.add),
                    reads=["scrC", "scrD"], writes=[dst_key])
            pr.section("DVE", rope_fn)

        ppd = psum_phase([("pp_stat", [128, 512], F32),
                          ("pp_b", [128, 512], F32),
                          ("pp_a0", [128, 512], F32),
                          ("pp_a1", [128, 512], F32),
                          ("pp_a2", [128, 512], F32),
                          ("pp_a3", [128, 512], F32),
                          ("pp_st2", [128, 512], F32),
                          ("pp_tr", [128, 128], BF16)])
        kpr = sbin("p12", "kpr", [128, HKV * S], BF16)
        vT = sbin("p12", "vT", [128, HKV * 16 * 128], BF16)
        mslot[0] = sbin("p02", "mean_t0", [128, 1024], F32)
        mslot[1] = sbin("p02", "mean_t1", [128, 1024], F32)
        qpr = bigB
        lnb = bigA
        xg = [sbin("p1", f"xg{i}", [128, 256], F32) for i in range(NCO)]
        vbf = sbin("p1", "vbf", [128, 1024], BF16)
        scrA2 = sbin("p1", "scrA2", [128, 1024], F32)
        sqbuf = sbin("p1", "sqbuf", [128, 1024], F32)
        scrAp = [scrA, scrA2]
        scrApk = ["scrA", "scrA2"]

        def qkv_mm(oc, sl, rhs, rhs_key):
            pq = (oc % 2) * 2
            def mm(e):
                for th in range(2):
                    pk = f"pp_a{pq + th}"
                    pr.do_group(
                        [(lambda e2, hc=hc, th=th, pk=pk:
                          e2.matmul(ppd[pk][:, 0:512],
                                    wstr_bf[sl][:, hc * 128:(hc + 1) * 128],
                                    rhs[:, hc * TOK + th * 512:
                                        hc * TOK + th * 512 + 512],
                                    start=(hc == 0), stop=(hc == NCO - 1)))
                         for hc in range(NCO)],
                        reads=[f"wsb{sl}", rhs_key], writes=[pk])
            pr.section("PE", mm)

        def evac_psum(dst, dst_key, oc):
            pq = (oc % 2) * 2
            def ev(e):
                for th in range(2):
                    pr.do(lambda e2, th=th: e2.tensor_copy(
                        dst[:, th * 512:th * 512 + 512],
                        ppd[f"pp_a{pq + th}"][:, 0:512]),
                        reads=[f"pp_a{pq + th}"], writes=[dst_key])
            pr.section("DVE", ev)

        cosq = sbin("p1q", "s_cosq", [128, TOK], F32)
        sinq = sbin("p1q", "s_sinq", [128, TOK], F32)
        pr.section("SP", lambda e: (
            pr.dma(cosq[:, :], cosq_h, writes=["cosq"], q="qcos"),
            pr.dma(sinq[:, :], sinq_h, writes=["cosq"], q="qcos")))
        ln_from_dram("l1q", xq_h, 0, lnb, xg)
        tap("ln1q", lnb[:, :], "lnb")
        load_stripe(0, wqkv_h[0])
        qkv_mm(0, 0, lnb, "lnb")
        load_stripe(1, wqkv_h[1])
        qkv_mm(1, 1, lnb, "lnb")
        for oc in range(16):
            par = oc % 2
            evac_psum(scrAp[par][:, 0:1024], scrApk[par], oc)
            if oc == 0:
                tap("qraw0", scrAp[par][:, 0:1024], scrApk[par])
            if oc + 2 < 16:
                load_stripe(par, wqkv_h[oc + 2])
                qkv_mm(oc + 2, par, lnb, "lnb")
            head_ln_rope(f"q{oc}", scrAp[par][:, 0:1024], scrApk[par], TOK,
                         qng, ("qng", "qnb"), qnb, cosq[:, 0:TOK],
                         sinq[:, 0:TOK], "cosq",
                         qpr[:, oc * TOK:(oc + 1) * TOK], "qpr", par=par)
            mean_emit(14)
        close_scope("p1q")
        pr.barrier()

        cosk = sbin("p1k", "s_cosk", [128, TOK], F32)
        sink = sbin("p1k", "s_sink", [128, TOK], F32)
        for half in range(2):
            def ckload(e, half=half):
                pr.dma(cosk[:, :], cosk_h[:, half * 1024:half * 1024 + 1024],
                       writes=["cosk"], q="qcos")
                pr.dma(sink[:, :], sink_h[:, half * 1024:half * 1024 + 1024],
                       writes=["cosk"], q="qcos")
            pr.section("SP", ckload)
            ln_from_dram(f"l1k{half}", xk_h, half * 1024, lnb, xg)
            load_stripe(0, wqkv_h[16])
            qkv_mm(16, 0, lnb, "lnb")
            load_stripe(1, wqkv_h[17])
            qkv_mm(17, 1, lnb, "lnb")
            for oc in range(16, 24):
                kv = oc - 16 if oc < 20 else oc - 20
                is_k = oc < 20
                par = oc % 2
                evac_psum(scrAp[par][:, 0:1024] if is_k else vbf,
                          scrApk[par] if is_k else "vbf", oc)
                if oc + 2 < 24:
                    load_stripe(par, wqkv_h[oc + 2])
                    qkv_mm(oc + 2, par, lnb, "lnb")
                if is_k:
                    head_ln_rope(f"k{kv}h{half}", scrAp[par][:, 0:1024],
                                 scrApk[par], 1024, kng, ("kng", "knb"), knb,
                                 cosk[:, 0:1024], sink[:, 0:1024],
                                 "cosk",
                                 kpr[:, kv * S + half * 1024:
                                     kv * S + half * 1024 + 1024], "kpr",
                                 par=par)
                else:
                    for tb in range(8):
                        kt = half * 8 + tb
                        pr.section("PE", lambda e, tb=tb: pr.do(
                            lambda e2: e2.transpose(
                                ppd["pp_tr"][:, 0:128],
                                vbf[:, tb * 128:(tb + 1) * 128],
                                s_id_bf[:, :]),
                            reads=["vbf", "idbf"], writes=["pptr"]))
                        pr.section("DVE", lambda e, kv=kv, kt=kt: pr.do(
                            lambda e2: e2.tensor_copy(
                                vT[:, (kv * 16 + kt) * 128:
                                   (kv * 16 + kt) * 128 + 128],
                                ppd["pp_tr"][:, 0:128]),
                            reads=["pptr"], writes=["vT"]))
                mean_emit(10)
        mean_emit(len(mean_jobs))
        tap("qpr", qpr[:, :], "qpr")
        tap("kpr", kpr[:, :], "kpr")
        tap("vT", vT[:, :], "vT")
        close_scope("p1k")
        close_scope("p1")
        close_scope("p02")
        pr.barrier()
        if _PHASES <= 1:
            raise _StopBuild

        ppd = psum_phase([("pp_s0", [128, 512], F32),
                          ("pp_s1", [128, 512], F32),
                          ("pp_s2", [128, 512], F32),
                          ("pp_s3", [128, 512], F32),
                          ("pp_v0", [128, 512], F32),
                          ("pp_v1", [128, 512], F32),
                          ("pp_sum", [128, 512], F32),
                          ("pp_b", [128, 512], F32)])
        attn = bigA
        maskt = sbin("p2", "maskt", [128, 16 * 512], BF16)
        cbSs = [sbin("p2", f"cbS{i}", [128, 512], F32) for i in range(2)]
        rbSs = [sbin("p2", f"rbS{i}", [128, 512], F32) for i in range(2)]
        scfp = [sbin("p2", f"scfp{i}", [128, 1024], F32) for i in range(2)]
        ptp2 = [sbin("p2", f"ptp{i}", [128, 1024], BF16) for i in range(2)]
        def mload(e):
            for i in range(16):
                pr.dma(maskt[:, i * 512:(i + 1) * 512], mask_h[i],
                       writes=["maskt"], q="qmask")
        pr.section("SP", mload)
        srtS = sbin("p2", "srtS", [1, 1], F32)
        invd = sbin("p2", "invd", [128, 1], F32)
        pr.section("SP", lambda e: pr.dma(srtS[:, :], srt_t.ap(),
                                          writes=["srtS"], q="qc_srt"))
        pr.section("DVE", lambda e: pr.do(
            lambda e2: e2.reciprocal(rowv(8, 1), srtS[:, :]),
            reads=["srtS"], writes=["rows"]))
        bcast_row(8, 0, invd[:, 0:1], "invd", 1)

        for kv in range(HKV):
            for gp in range(2):
                pr.section("DVE", lambda e, kv=kv, gp=gp: pr.do(
                    lambda e2: e2.tensor_tensor(
                        scrA[:, 0:1024],
                        kpr[:, kv * S + gp * 1024:kv * S + gp * 1024 + 1024],
                        kpr[:, kv * S + gp * 1024:kv * S + gp * 1024 + 1024],
                        ALU.mult),
                    reads=["kpr"], writes=["scrA"]))
                pr.section("PE", lambda e, gp=gp: pr.do_group(
                    [(lambda e2, g=g: e2.matmul(
                        ppd["pp_sum"][g * 32:g * 32 + 1, 0:512],
                        s_ones_c[:, :],
                        scrA[:, g * 512:g * 512 + 512],
                        start=True, stop=True)) for g in range(2)],
                    reads=["scrA", "onesc"], writes=["ppsum"]))
                def nk_rows(e, gp=gp):
                    for g in range(2):
                        pr.do(lambda e2, g=g, gp=gp: e2.tensor_copy(
                            rowv(5, 2048)[0:1, (gp * 2 + g) * 512:
                                          (gp * 2 + g) * 512 + 512],
                            ppd["pp_sum"][g * 32:g * 32 + 1, 0:512]),
                            reads=["ppsum"], writes=["rows"])
                pr.section("DVE", nk_rows)
            pr.section("ACT", lambda e: pr.do(
                lambda e2: e2.activation(rowv(5, 2048), rowv(5, 2048),
                                         AF.Sqrt),
                reads=["rows"], writes=["rows"]))
            pr.section("DVE", lambda e, kv=kv: pr.do(
                lambda e2: e2.reduce_max(
                    rowv(14, 1024)[0:1, kv:kv + 1],
                    rowv(5, 2048), axis=AX.X),
                reads=["rows"], writes=["rows"]))

        for hh in range(H):
            kv = hh // 4
            pr.section("DVE", lambda e, hh=hh: pr.do(
                lambda e2: e2.tensor_tensor(
                    scrA[:, 0:TOK], qpr[:, hh * TOK:(hh + 1) * TOK],
                    qpr[:, hh * TOK:(hh + 1) * TOK], ALU.mult),
                reads=["qpr"], writes=["scrA"]))
            pr.section("PE", lambda e: pr.do_group(
                [(lambda e2, g=g: e2.matmul(
                    ppd["pp_sum"][g * 32:g * 32 + 1, 0:512], s_ones_c[:, :],
                    scrA[:, g * 512:g * 512 + 512], start=True, stop=True))
                 for g in range(2)],
                reads=["scrA", "onesc"], writes=["ppsum"]))
            def nq_rows(e):
                for g in range(2):
                    pr.do(lambda e2, g=g: e2.tensor_copy(
                        rowv(6, TOK)[0:1, g * 512:g * 512 + 512],
                        ppd["pp_sum"][g * 32:g * 32 + 1, 0:512]),
                        reads=["ppsum"], writes=["rows"])
            pr.section("DVE", nq_rows)
            pr.section("ACT", lambda e: pr.do(
                lambda e2: e2.activation(rowv(6, TOK), rowv(6, TOK), AF.Sqrt),
                reads=["rows"], writes=["rows"]))
            pr.section("DVE", lambda e, kv=kv: pr.do(
                lambda e2: e2.tensor_scalar(
                    rowv(7, TOK), rowv(6, TOK),
                    rowv(14, 1024)[0:1, kv:kv + 1],
                    INV_SQRT_D, ALU.mult, ALU.mult),
                reads=["rows", "rows"], writes=["rows"]))
            for qb in range(2):
                cbi = (hh * 2 + qb) % 2
                cb = cbSs[cbi]
                cbk = f"cbS{cbi}"
                rb = rbSs[cbi]
                rbk = f"rbS{cbi}"
                bcast_row(7, qb * 512, cb[:, :], cbk, 512)
                nkt = KT_OF_QB[qb]
                pvk = f"pp_v{qb}"
                def sc_mm(kt, hh=hh, kv=kv, qb=qb):
                    sps = f"pp_s{kt % 4}"
                    pr.section("PE", lambda e, kt=kt, sps=sps: pr.do(
                        lambda e2: e2.matmul(
                            ppd[sps][:, 0:512],
                            kpr[:, kv * S + kt * 128:kv * S + kt * 128 + 128],
                            qpr[:, hh * TOK + qb * 512:
                                hh * TOK + qb * 512 + 512],
                            start=True, stop=True),
                        reads=["kpr", "qpr"], writes=[sps]))
                sc_mm(0)
                sc_mm(1)
                for kp in range(nkt // 2):
                    k0 = 2 * kp
                    pb = kp % 2
                    sfp = scfp[pb]
                    sfpk = f"scfp{pb}"
                    ptp = ptp2[pb]
                    ptpk = f"ptp{pb}"
                    if k0 + 2 < nkt:
                        sc_mm(k0 + 2)
                        sc_mm(k0 + 3)
                    def stt2(e, sfp=sfp, sfpk=sfpk, cb=cb, cbk=cbk, k0=k0):
                        for j in range(2):
                            sps = f"pp_s{(k0 + j) % 4}"
                            pr.do(lambda e2, j=j, sps=sps:
                                  e2.scalar_tensor_tensor(
                                      sfp[:, j * 512:j * 512 + 512],
                                      ppd[sps][:, 0:512], invd[:, 0:1],
                                      cb[:, :], ALU.mult, ALU.subtract),
                                  reads=[sps, cbk, "invd"], writes=[sfpk])
                    pr.section("DVE", stt2)
                    pr.section("ACT", lambda e, sfp=sfp, sfpk=sfpk, ptp=ptp,
                               ptpk=ptpk: pr.do(
                        lambda e2: e2.activation(ptp[:, :], sfp[:, :],
                                                 AF.Exp),
                        reads=[sfpk], writes=[ptpk]))
                    if (qb, k0) in MASK_IDX:
                        mi0 = MASK_IDX[(qb, k0)]
                        pr.section("DVE", lambda e, ptp=ptp, ptpk=ptpk,
                                   mi0=mi0: pr.do(
                            lambda e2: e2.tensor_tensor(
                                ptp[:, :], ptp[:, :],
                                maskt[:, mi0 * 512:mi0 * 512 + 1024],
                                ALU.mult),
                            reads=["maskt", ptpk], writes=[ptpk]))
                    def pe_pv2(e, k0=k0, kv=kv, ptp=ptp, ptpk=ptpk, pvk=pvk,
                               nkt=nkt, qb=qb):
                        for j in range(2):
                            kt = k0 + j
                            pr.do(lambda e2, j=j, kt=kt: e2.matmul(
                                ppd["pp_sum"][qb * 64:qb * 64 + 1, 0:512],
                                onescol_bf[:, :],
                                ptp[:, j * 512:j * 512 + 512],
                                start=(kt == 0), stop=(kt == nkt - 1)),
                                reads=[ptpk, "onescbf"], writes=["ppsum"])
                            pr.do(lambda e2, j=j, kt=kt: e2.matmul(
                                ppd[pvk][:, 0:512],
                                vT[:, (kv * 16 + kt) * 128:
                                   (kv * 16 + kt) * 128 + 128],
                                ptp[:, j * 512:j * 512 + 512],
                                start=(kt == 0), stop=(kt == nkt - 1)),
                                reads=[ptpk, "vT"], writes=[pvk])
                    pr.section("PE", pe_pv2)
                def fin(e, qb=qb):
                    pr.do(lambda e2: e2.tensor_copy(
                        rowv(8, 512),
                        ppd["pp_sum"][qb * 64:qb * 64 + 1, 0:512]),
                        reads=["ppsum"], writes=["rows"])
                    pr.do(lambda e2: e2.reciprocal(rowv(9, 512), rowv(8, 512)),
                          reads=["rows"], writes=["rows"])
                pr.section("DVE", fin)
                bcast_row(9, 0, rb[:, :], rbk, 512)
                pr.section("DVE", lambda e, hh=hh, qb=qb, pvk=pvk, rb=rb,
                           rbk=rbk: pr.do(
                    lambda e2: e2.tensor_tensor(
                        attn[:, hh * TOK + qb * 512:
                             hh * TOK + qb * 512 + 512],
                        ppd[pvk][:, 0:512], rb[:, :], ALU.mult),
                    reads=[pvk, rbk], writes=["attn"]))
        tap("attn", attn[:, :], "attn")
        close_scope("p2")
        close_scope("p12")
        pr.barrier()
        if _PHASES <= 2:
            raise _StopBuild

        ppd = psum_phase([(f"pp_w{i}", [128, 512], F32) for i in range(6)]
                         + [("pp_stat", [128, 512], F32),
                            ("pp_b", [128, 512], F32)])
        scrKI = sbin("rest", "scrKI", [128, 1024], I32)
        xst = [sbin("rest", f"xst{i}", [128, TOK], F32) for i in range(2)]
        wo_out = bigB

        for og in range(6):
            ocs = list(range(og * 3, min(og * 3 + 3, NCO)))
            if not ocs:
                continue
            for dc in range(NCO):
                sl = dc % 2
                def spf(e, dc=dc, sl=sl, ocs=ocs):
                    pr.dma(wstr[sl][:, 0:len(ocs) * 128],
                           wo_h[dc][:, ocs[0] * 128:(ocs[-1] + 1) * 128],
                           writes=[f"wsf{sl}"], q=f"qw{sl}")
                pr.section("SP", spf)
                pr.section("DVE", lambda e, sl=sl, n=len(ocs): pr.do(
                    lambda e2: e2.tensor_copy(wstr_bf[sl][:, 0:n * 128],
                                              wstr[sl][:, 0:n * 128]),
                    reads=[f"wsf{sl}"], writes=[f"wsb{sl}"]))
                def mmw(e, dc=dc, sl=sl, ocs=ocs):
                    builds = []
                    for j in range(len(ocs)):
                        for th in range(2):
                            builds.append(
                                (lambda e2, j=j, th=th, dc=dc, sl=sl:
                                 e2.matmul(
                                     ppd[f"pp_w{j * 2 + th}"][:, 0:512],
                                     wstr_bf[sl][:, j * 128:(j + 1) * 128],
                                     attn[:, dc * TOK + th * 512:
                                          dc * TOK + th * 512 + 512],
                                     start=(dc == 0), stop=(dc == NCO - 1))))
                    pr.do_group(builds, reads=[f"wsb{sl}", "attn"],
                                writes=["ppw"])
                pr.section("PE", mmw)
            def evacw(e, ocs=ocs):
                for j, oc in enumerate(ocs):
                    for th in range(2):
                        pr.do(lambda e2, j=j, oc=oc, th=th: e2.tensor_copy(
                            wo_out[:, oc * TOK + th * 512:
                                   oc * TOK + th * 512 + 512],
                            ppd[f"pp_w{j * 2 + th}"][:, 0:512]),
                            reads=["ppw"], writes=["wo_out"])
            pr.section("DVE", evacw)

        tap("wo_out", wo_out[:, :], "wo_out")
        for g in range(2):
            stats2048("ln2g%d" % g,
                      lambda c, g=g: wo_out[:, c * TOK + g * 512:
                                            c * TOK + g * 512 + 512],
                      g * 512, 512, "wo_out", bf_src=True)
        for c in range(NCO):
            sl = c % 2
            pr.section("SP", lambda e, c=c, sl=sl: pr.dma(
                xst[sl][:, :], xq_h[c * 128:(c + 1) * 128, :],
                writes=[f"xst{sl}"], q=f"qx{sl}"))
            def ln2_apply(e, c=c, sl=sl):
                pr.do(lambda e2: e2.tensor_tensor(
                    scrB[:, 0:TOK], wo_out[:, c * TOK:(c + 1) * TOK],
                    b1b[:, 0:TOK], ALU.mult),
                    reads=["wo_out", "b1b"], writes=["scrB"])
                pr.do(lambda e2: e2.tensor_tensor(
                    scrB[:, 0:TOK], scrB[:, 0:TOK], b2b[:, 0:TOK],
                    ALU.subtract),
                    reads=["scrB", "b2b"], writes=["scrB"])
                pr.do(lambda e2: e2.tensor_tensor(
                    xst[sl][:, :], xst[sl][:, :], scrB[:, 0:TOK], ALU.add),
                    reads=["scrB", f"xst{sl}"], writes=[f"xst{sl}"])
            pr.section("DVE", ln2_apply)
            pr.section("POOL", lambda e, c=c, sl=sl: pr.dma(
                x1_dram[c], xst[sl][:, :], reads=[f"xst{sl}"],
                writes=[f"x1d{c}"], q=f"qsx{sl}"))
        pr.barrier()
        if _PHASES <= 3:
            raise _StopBuild

        ppd = psum_phase([("pp_stat", [128, 512], F32),
                          ("pp_b", [128, 512], F32),
                          ("pp_t", [128, 512], F32)])
        xqi = bigB
        bgB = sbin("p45", "bgB", [128, TOK], F32)
        buB = sbin("p45", "buB", [128, TOK], F32)
        s1B = sbin("p4s", "s1B", [128, TOK], F32)

        def mrow_ap(mi):
            return rowv(15, 1024)[0:1, mi:mi + 1]

        pr.section("PE", lambda e: pr.do_group(
            [(lambda e2, mi=mi: e2.matmul(
                ppd["pp_stat"][mi * 32:mi * 32 + 1, 0:1], s_ones_c[:, :],
                mean_acc[:, mi:mi + 1], start=True, stop=True))
             for mi in range(3)],
            reads=["mean_acc", "onesc"], writes=["ppstat"]))
        def mrows(e):
            for mi in range(3):
                pr.do(lambda e2, mi=mi: e2.tensor_scalar(
                    mrow_ap(mi), ppd["pp_stat"][mi * 32:mi * 32 + 1, 0:1],
                    INV_N_GU, 1e-5, ALU.mult, ALU.max),
                    reads=["ppstat"], writes=["rows"])
            for mi in range(3):
                pr.do(lambda e2, mi=mi: e2.tensor_scalar(
                    rowv(15, 1024)[0:1, 4 + mi:5 + mi],
                    mrow_ap(mi), 0.5, None, ALU.mult),
                    reads=["rows"], writes=["rows"])
        pr.section("DVE", mrows)
        for mi in range(3):
            bcast_row(15, 4 + mi, colv(mi), "cols", 1)
        pr.section("DVE", lambda e: pr.do(
            lambda e2: e2.tensor_scalar(cols[:, 4:7], cols[:, 0:3], -1.0,
                                        None, ALU.mult),
            reads=["cols"], writes=["cols"]))
        tap("cols", cols[:, 0:8], "cols")
        tap("mean_acc", mean_acc[:, :], "mean_acc")

        for g in range(2):
            def l3load(c, g=g):
                sl = c % 2
                pr.section("SP", lambda e, c=c, sl=sl, g=g: pr.dma(
                    xst[sl][:, 0:512],
                    x1_dram[c][:, g * 512:g * 512 + 512],
                    reads=[f"x1d{c}"], writes=[f"xst{sl}"], q=f"qx{sl}"))
            stats2048("ln3g%d" % g,
                      lambda c: xst[c % 2][:, 0:512],
                      g * 512, 512, lambda c: f"xst{c % 2}",
                      chunk_load=l3load)
        def amax_init(e):
            pr.do(lambda e2: e2.memset(scrD[:, 0:TOK], -3.0e38),
                  writes=["scrD"])
            pr.do(lambda e2: e2.memset(scrC[:, 0:TOK], 3.0e38),
                  writes=["scrC"])
        pr.section("DVE", amax_init)
        for c in range(NCO):
            sl = c % 2
            pr.section("SP", lambda e, c=c, sl=sl: pr.dma(
                xst[sl][:, :], x1_dram[c], reads=[f"x1d{c}"],
                writes=[f"xst{sl}"], q=f"qx{sl}"))
            def ln3_chunks(e, c=c, sl=sl):
                pr.do(lambda e2: e2.tensor_tensor(
                    scrB[:, 0:TOK], xst[sl][:, :],
                    b1b[:, 0:TOK], ALU.mult),
                    reads=[f"xst{sl}", "b1b"], writes=["scrB"])
                pr.do(lambda e2: e2.tensor_tensor(
                    scrB[:, 0:TOK], scrB[:, 0:TOK],
                    b2b[:, 0:TOK], ALU.subtract),
                    reads=["scrB", "b2b"], writes=["scrB"])
                pr.do(lambda e2: e2.tensor_tensor(
                    scrD[:, 0:TOK], scrD[:, 0:TOK],
                    scrB[:, 0:TOK], ALU.max),
                    reads=["scrB", "scrD"], writes=["scrD"])
                pr.do(lambda e2: e2.tensor_tensor(
                    scrC[:, 0:TOK], scrC[:, 0:TOK],
                    scrB[:, 0:TOK], ALU.min),
                    reads=["scrB", "scrC"], writes=["scrC"])
            pr.section("DVE", ln3_chunks)

        def tokmax(src_key, dst_row_i):
            for tb in range(8):
                pr.section("PE", lambda e, tb=tb: pr.do(
                    lambda e2: e2.transpose(
                        ppd["pp_t"][:, 0:128],
                        scrD[:, tb * 128:(tb + 1) * 128], s_id_f[:, :]),
                    reads=[src_key, "idf"], writes=["ppt"]))
                pr.section("DVE", lambda e: pr.do(
                    lambda e2: e2.reduce_max(colv(10), ppd["pp_t"][:, 0:128],
                                             axis=AX.X),
                    reads=["ppt"], writes=["cols"]))
                pr.section("PE", lambda e: pr.do(
                    lambda e2: e2.transpose(ppd["pp_t"][0:1, 128:256],
                                            colv(10), s_id_f[:, :]),
                    reads=["cols", "idf"], writes=["ppt"]))
                pr.section("DVE", lambda e, tb=tb, dst_row_i=dst_row_i: pr.do(
                    lambda e2: e2.tensor_copy(
                        rowv(dst_row_i, TOK)[0:1, tb * 128:(tb + 1) * 128],
                        ppd["pp_t"][0:1, 128:256]),
                    reads=["ppt"], writes=["rows"]))
        def amax_comb(e):
            pr.do(lambda e2: e2.tensor_scalar(
                scrC[:, 0:TOK], scrC[:, 0:TOK], -1.0, None, ALU.mult),
                reads=["scrC"], writes=["scrC"])
            pr.do(lambda e2: e2.tensor_tensor(
                scrD[:, 0:TOK], scrD[:, 0:TOK], scrC[:, 0:TOK], ALU.max),
                reads=["scrC", "scrD"], writes=["scrD"])
        pr.section("DVE", amax_comb)
        tokmax("scrD", 10)
        def s1_rows(e):
            pr.do(lambda e2: e2.tensor_scalar(rowv(10, TOK), rowv(10, TOK),
                                              1e-5, None, ALU.max),
                  reads=["rows"], writes=["rows"])
            pr.do(lambda e2: e2.reciprocal(rowv(11, TOK), rowv(10, TOK)),
                  reads=["rows"], writes=["rows"])
            pr.do(lambda e2: e2.tensor_scalar(rowv(11, TOK), rowv(11, TOK),
                                              128.0, None, ALU.mult),
                  reads=["rows"], writes=["rows"])
            pr.do(lambda e2: e2.tensor_scalar(rowv(12, TOK), rowv(10, TOK),
                                              mrow_ap(0), 1.0 / 128.0,
                                              ALU.mult, ALU.mult),
                  reads=["rows", "rows"], writes=["rows"])
            pr.do(lambda e2: e2.tensor_scalar(rowv(13, TOK), rowv(10, TOK),
                                              mrow_ap(1), 1.0 / 128.0,
                                              ALU.mult, ALU.mult),
                  reads=["rows", "rows"], writes=["rows"])
        pr.section("DVE", s1_rows)
        for ri, dst, dkey in ((11, s1B, "s1B"), (12, bgB, "bgB"),
                              (13, buB, "buB")):
            for g in range(2):
                bcast_row(ri, g * 512, dst[:, g * 512:g * 512 + 512],
                          dkey, 512)
        for c in range(NCO):
            sl = c % 2
            pr.section("SP", lambda e, c=c, sl=sl: pr.dma(
                xst[sl][:, :], x1_dram[c], reads=[f"x1d{c}"],
                writes=[f"xst{sl}"], q=f"qx{sl}"))
            def qv_fn(e, c=c, sl=sl):
                pr.do(lambda e2: e2.tensor_tensor(
                    scrB[:, 0:TOK], xst[sl][:, :],
                    b1b[:, 0:TOK], ALU.mult),
                    reads=[f"xst{sl}", "b1b"], writes=["scrB"])
                pr.do(lambda e2: e2.tensor_tensor(
                    scrB[:, 0:TOK], scrB[:, 0:TOK], b2b[:, 0:TOK],
                    ALU.subtract),
                    reads=["scrB", "b2b"], writes=["scrB"])
                pr.do(lambda e2: e2.tensor_tensor(
                    scrB[:, 0:TOK], scrB[:, 0:TOK], s1B[:, :], ALU.mult),
                    reads=["scrB", "s1B"], writes=["scrB"])
            pr.section("DVE", qv_fn)
            round_exact(xqi[:, c * TOK:(c + 1) * TOK], "xqi",
                        scrB[:, 0:TOK], "scrB", TOK)
        tap("xqi", xqi[:, :], "xqi")
        tap("s1B", s1B[:, :], "s1B")
        tap("bgB", bgB[:, :], "bgB")
        close_scope("p4s")
        pr.barrier()
        if _PHASES <= 4:
            raise _StopBuild

        ppd = psum_phase([("pp_g0", [128, 512], F32),
                          ("pp_g1", [128, 512], F32),
                          ("pp_u0", [128, 512], F32),
                          ("pp_u1", [128, 512], F32),
                          ("pp_z0", [128, 512], F32),
                          ("pp_z1", [128, 512], F32)])
        utb = [bigA[:, 0:2048], bigA[:, 2048:4096]]
        gtb = [bigA[:, 4096:6144], bigA[:, 6144:8192]]
        islot = bigA[:, 8192:9216]
        zslot = [bigA[:, 9216:10240], bigA[:, 10240:11264]]
        dtb = bigA[:, 11264:13312]
        wdf = sbin("p45", "wdf", [128, HID], F32)

        def ternarize(src, src_key, dst, dst_key, mi):
            def fn(e):
                pr.do(lambda e2: e2.tensor_scalar(
                    dst[:, :], src[:, :], colv(4 + mi), None, ALU.is_lt),
                    reads=[src_key, "cols"], writes=[dst_key])
                pr.do(lambda e2: e2.scalar_tensor_tensor(
                    dst[:, :], src[:, :], colv(mi), dst[:, :], ALU.is_gt,
                    ALU.subtract),
                    reads=[src_key, "cols", dst_key], writes=[dst_key])
            pr.section("DVE", fn)

        def ldgu(oc):
            def sp(e, oc=oc):
                pr.dma(wstr[0][:, :], wg_h[oc], writes=["wsf0"], q="qw0")
                pr.dma(wstr[1][:, :], wu_h[oc], writes=["wsf1"], q="qw1")
            pr.section("SP", sp)

        ldgu(0)
        for oc in range(NIC):
            sl = oc % 2
            pr.section("POOL", lambda e, oc=oc: pr.dma(
                wdf[:, :], wd_h[oc], writes=["wdf"], q="qwd"))
            ternarize(wstr[0], "wsf0", gtb[sl], f"gtb{sl}", 0)
            ternarize(wstr[1], "wsf1", utb[sl], f"utb{sl}", 1)
            if oc + 1 < NIC:
                ldgu(oc + 1)
            def mmgu(e, oc=oc, sl=sl):
                for th in range(2):
                    for mat, tb in (("g", gtb[sl]), ("u", utb[sl])):
                        pk = f"pp_{mat}{th}"
                        pr.do_group(
                            [(lambda e2, hc=hc, th=th, tb=tb, pk=pk:
                              e2.matmul(ppd[pk][:, 0:512],
                                        tb[:, hc * 128:(hc + 1) * 128],
                                        xqi[:, hc * TOK + th * 512:
                                            hc * TOK + th * 512 + 512],
                                        start=(hc == 0),
                                        stop=(hc == NCO - 1)))
                             for hc in range(NCO)],
                            reads=[("gtb%d" if mat == "g" else "utb%d") % sl,
                                   "xqi"], writes=[pk])
            pr.section("PE", mmgu)
            def evgu(e):
                for th in range(2):
                    pr.do(lambda e2, th=th: e2.tensor_tensor(
                        scrB[:, th * 512:th * 512 + 512],
                        ppd[f"pp_g{th}"][:, 0:512],
                        bgB[:, th * 512:th * 512 + 512], ALU.mult),
                        reads=[f"pp_g{th}", "bgB"], writes=["scrB"])
                    pr.do(lambda e2, th=th: e2.tensor_tensor(
                        scrC[:, th * 512:th * 512 + 512],
                        ppd[f"pp_u{th}"][:, 0:512],
                        buB[:, th * 512:th * 512 + 512], ALU.mult),
                        reads=[f"pp_u{th}", "buB"], writes=["scrC"])
            pr.section("DVE", evgu)
            pr.section("ACT", lambda e: pr.do(
                lambda e2: e2.activation(scrD[:, 0:TOK], scrB[:, 0:TOK],
                                         AF.Silu),
                reads=["scrB"], writes=["scrD"]))
            pr.section("DVE", lambda e: pr.do(
                lambda e2: e2.tensor_tensor(islot[:, 0:TOK], scrD[:, 0:TOK],
                                            scrC[:, 0:TOK], ALU.mult),
                reads=["scrD", "scrC"], writes=["islot"]))
            def mmz(e):
                for th in range(2):
                    pr.do_group([
                        lambda e2, th=th: e2.matmul(
                            ppd[f"pp_z{th}"][:, 0:512], s_h128[:, :],
                            islot[:, th * 512:th * 512 + 512],
                            start=True, stop=True)],
                        reads=["islot", "h128"], writes=[f"pp_z{th}"])
            pr.section("PE", mmz)
            zs = zslot[oc % 2]
            zk = f"zs{oc % 2}"
            def evz(e, zs=zs, zk=zk):
                for th in range(2):
                    pr.do(lambda e2, th=th: e2.activation(
                        zs[:, th * 512:th * 512 + 512],
                        ppd[f"pp_z{th}"][:, 0:512], AF.Copy),
                        reads=[f"pp_z{th}"], writes=[zk])
            pr.section("ACT", evz)
            pr.section("POOL", lambda e, oc=oc, zs=zs, zk=zk: pr.dma(
                z_dram[oc], zs[:, :], reads=[zk], writes=[f"zd{oc}"],
                q=f"qsz{oc % 2}"))
            ternarize(wdf, "wdf", dtb, "dtb", 2)
            pr.section("POOL", lambda e, oc=oc: pr.dma(
                trd_dram[oc], dtb[:, :], reads=["dtb"], writes=[f"td{oc}"],
                q="qstd"))
        close_scope("p45")
        pr.barrier()
        if _PHASES <= 5:
            raise _StopBuild

        ppd = psum_phase([("pp_y0", [128, 512], F32),
                          ("pp_y1", [128, 512], F32),
                          ("pp_t", [128, 512], F32),
                          ("pp_b", [128, 512], F32)])
        zt = [bigA[:, 0:1024], bigA[:, 1024:2048]]
        ys = [bigA[:, 2048:3072], bigA[:, 3072:4096]]
        zv3 = z_dram.rearrange("a b t -> b a t")
        def amax_init(e):
            pr.do(lambda e2: e2.memset(scrD[:, 0:TOK], -3.0e38),
                  writes=["scrD"])
            pr.do(lambda e2: e2.memset(scrC[:, 0:TOK], 3.0e38),
                  writes=["scrC"])
        pr.section("DVE", amax_init)
        for c in range(NIC):
            sl = c % 2
            def zl(e, c=c, sl=sl):
                if _P6SUB & 1:
                    pr.dma(zt[sl][0:64, :], zv3[2 * c], writes=[f"zt{sl}"],
                           q=f"qz{sl}")
                    pr.dma(zt[sl][64:128, :], zv3[2 * c + 1],
                           writes=[f"zt{sl}"], q=f"qz{sl}")
            pr.section("SP", zl)
            if not (_P6SUB & 1):
                pr.section("DVE", lambda e, sl=sl: pr.do(
                    lambda e2: e2.memset(zt[sl][:, :], 0.0),
                    writes=[f"zt{sl}"]))
            if not (_P6SUB & 2):
                continue
            def mmy(e, c=c, sl=sl):
                for th in range(2):
                    pr.do_group([
                        lambda e2, th=th: e2.matmul(
                            ppd[f"pp_y{th}"][:, 0:512], s_h64b[:, :],
                            zt[sl][:, th * 512:th * 512 + 512],
                            start=True, stop=True)],
                        reads=[f"zt{sl}", "h64b"], writes=[f"pp_y{th}"])
            pr.section("PE", mmy)
            def evy(e, sl=sl):
                for th in range(2):
                    pr.do(lambda e2, th=th: e2.activation(
                        ys[sl][:, th * 512:th * 512 + 512],
                        ppd[f"pp_y{th}"][:, 0:512], AF.Copy),
                        reads=[f"pp_y{th}"], writes=[f"ys{sl}"])
            pr.section("ACT", evy)
            def mx(e, sl=sl):
                for th in range(2):
                    pr.do(lambda e2, th=th: e2.tensor_tensor(
                        scrD[:, th * 512:th * 512 + 512],
                        scrD[:, th * 512:th * 512 + 512],
                        ppd[f"pp_y{th}"][:, 0:512], ALU.max),
                        reads=[f"pp_y{th}", "scrD", f"ys{sl}"],
                        writes=["scrD"])
                    pr.do(lambda e2, th=th: e2.tensor_tensor(
                        scrC[:, th * 512:th * 512 + 512],
                        scrC[:, th * 512:th * 512 + 512],
                        ppd[f"pp_y{th}"][:, 0:512], ALU.min),
                        reads=[f"pp_y{th}", "scrC"], writes=["scrC"])
            pr.section("DVE", mx)
            pr.section("POOL", lambda e, c=c, sl=sl: pr.dma(
                yv_dram[c], ys[sl][:, :], reads=[f"ys{sl}"],
                writes=[f"yv{c}"], q=f"qsy{sl}"))
        if not (_P6SUB & 4):
            raise _StopBuild
        def ymax_comb(e):
            pr.do(lambda e2: e2.tensor_scalar(
                scrC[:, 0:TOK], scrC[:, 0:TOK], -1.0, None, ALU.mult),
                reads=["scrC"], writes=["scrC"])
            pr.do(lambda e2: e2.tensor_tensor(
                scrD[:, 0:TOK], scrD[:, 0:TOK], scrC[:, 0:TOK], ALU.max),
                reads=["scrC", "scrD"], writes=["scrD"])
        pr.section("DVE", ymax_comb)
        tokmax("scrD", 10)
        qmB = sbin("p67", "qmB", [128, TOK], F32)
        bdB = sbin("p67", "bdB", [128, TOK], F32)
        def s2_rows(e):
            pr.do(lambda e2: e2.tensor_scalar(rowv(10, TOK), rowv(10, TOK),
                                              FWHT_NORM, 1e-5, ALU.mult,
                                              ALU.max),
                  reads=["rows"], writes=["rows"])
            pr.do(lambda e2: e2.reciprocal(rowv(11, TOK), rowv(10, TOK)),
                  reads=["rows"], writes=["rows"])
            pr.do(lambda e2: e2.tensor_scalar(rowv(11, TOK), rowv(11, TOK),
                                              128.0 * FWHT_NORM, None,
                                              ALU.mult),
                  reads=["rows"], writes=["rows"])
            pr.do(lambda e2: e2.tensor_scalar(rowv(12, TOK), rowv(10, TOK),
                                              mrow_ap(2), 1.0 / 128.0,
                                              ALU.mult, ALU.mult),
                  reads=["rows", "rows"], writes=["rows"])
        pr.section("DVE", s2_rows)
        for ri, dst, dkey in ((11, qmB, "qmB"), (12, bdB, "bdB")):
            for g in range(2):
                bcast_row(ri, g * 512, dst[:, g * 512:g * 512 + 512],
                          dkey, 512)
        pr.barrier()
        if _PHASES <= 6:
            raise _StopBuild

        ppd = psum_phase([(f"pp_d{i}", [128, 512], F32) for i in range(8)])
        d_out = bigB
        aqs = [bigA[:, 0:1024], bigA[:, 1024:2048]]
        tds = [bigA[:, 2048:3072], bigA[:, 3072:4096]]
        zt_p7 = [bigA[:, 4096:5120], bigA[:, 5120:6144]]

        for oh in range(2):
            for th in range(2):
                first = (oh == 0 and th == 0)
                for c in range(NIC):
                    sl = c % 2
                    pr.section("ACT", lambda e, c=c, sl=sl, oh=oh: pr.dma(
                        tds[sl][:, :],
                        trd_dram[c][:, oh * 1024:oh * 1024 + 1024],
                        writes=[f"tds{sl}"], q=f"qt{sl}"))
                    def spl(e, c=c, sl=sl, th=th, first=first):
                        if first:
                            pr.dma(zt_p7[sl][:, :], yv_dram[c],
                                   writes=[f"ybp{sl}"], q=f"qy{sl}")
                        else:
                            pr.dma(aqs[sl][:, 0:512],
                                   aq2_dram[c][:, th * 512:th * 512 + 512],
                                   reads=[f"aqd{c}"], writes=[f"aqs{sl}"],
                                   q=f"qa{sl}")
                    pr.section("SP", spl)
                    if first:
                        pr.section("DVE", lambda e, sl=sl: pr.do(
                            lambda e2: e2.tensor_tensor(
                                scrB[:, 0:TOK], zt_p7[sl][:, :], qmB[:, :],
                                ALU.mult),
                            reads=[f"ybp{sl}", "qmB"], writes=["scrB"]))
                        if c == 0:
                            tap("qv_c0", scrB[:, 0:TOK], "scrB")
                            tap("yb_c0", zt_p7[sl][:, :], f"ybp{sl}")
                            tap("qmB", qmB[:, :], "qmB")
                        round_exact(aqs[sl][:, 0:TOK], f"aqs{sl}",
                                    scrB[:, 0:TOK], "scrB", TOK,
                                    dbg=("r7" if (_DEBUG and c == 0)
                                         else None))
                        pr.section("POOL", lambda e, c=c, sl=sl: pr.dma(
                            aq2_dram[c], aqs[sl][:, :], reads=[f"aqs{sl}"],
                            writes=[f"aqd{c}"], q=f"qsa{sl}"))
                    def mmd(e, c=c, sl=sl, th=th, first=first):
                        rhs = aqs[sl][:, th * 512:th * 512 + 512] if first \
                            else aqs[sl][:, 0:512]
                        builds = []
                        for j in range(8):
                            builds.append(
                                (lambda e2, j=j, sl=sl, rhs=rhs:
                                 e2.matmul(ppd[f"pp_d{j}"][:, 0:512],
                                           tds[sl][:, j * 128:(j + 1) * 128],
                                           rhs, start=(c == 0),
                                           stop=(c == NIC - 1))))
                        pr.do_group(builds, reads=[f"tds{sl}", f"aqs{sl}"],
                                    writes=["ppd"])
                    pr.section("PE", mmd)
                def evd(e, oh=oh, th=th):
                    for j in range(8):
                        oc = oh * 8 + j
                        pr.do(lambda e2, j=j, oc=oc, th=th: e2.tensor_tensor(
                            d_out[:, oc * TOK + th * 512:
                                  oc * TOK + th * 512 + 512],
                            ppd[f"pp_d{j}"][:, 0:512],
                            bdB[:, th * 512:th * 512 + 512], ALU.mult),
                            reads=["ppd", "bdB"], writes=["d_out"])
                pr.section("DVE", evd)
        close_scope("p67")
        pr.barrier()
        if _PHASES <= 7:
            raise _StopBuild

        ppd = psum_phase([("pp_stat", [128, 512], F32),
                          ("pp_b", [128, 512], F32)])
        for g in range(2):
            stats2048("ln4g%d" % g,
                      lambda c, g=g: d_out[:, c * TOK + g * 512:
                                           c * TOK + g * 512 + 512],
                      g * 512, 512, "d_out", bf_src=True)
        for c in range(NCO):
            sl = c % 2
            pr.section("SP", lambda e, c=c, sl=sl: pr.dma(
                xst[sl][:, :], x1_dram[c], reads=[f"x1d{c}"],
                writes=[f"xst{sl}"], q=f"qx{sl}"))
            def ln4_apply(e, c=c, sl=sl):
                pr.do(lambda e2: e2.tensor_tensor(
                    scrB[:, 0:TOK], d_out[:, c * TOK:(c + 1) * TOK],
                    b1b[:, 0:TOK], ALU.mult),
                    reads=["d_out", "b1b"], writes=["scrB"])
                pr.do(lambda e2: e2.tensor_tensor(
                    scrB[:, 0:TOK], scrB[:, 0:TOK], b2b[:, 0:TOK],
                    ALU.subtract),
                    reads=["scrB", "b2b"], writes=["scrB"])
                pr.do(lambda e2: e2.tensor_tensor(
                    xst[sl][:, :], xst[sl][:, :], scrB[:, 0:TOK], ALU.add),
                    reads=["scrB", f"xst{sl}"], writes=[f"xst{sl}"])
            pr.section("DVE", ln4_apply)
            pr.section("POOL", lambda e, c=c, sl=sl: pr.dma(
                y_h[c * 128:(c + 1) * 128, :], xst[sl][:, :],
                reads=[f"xst{sl}"], q=f"qso{sl}"))
        t0, t1 = pr.cnt.get("qso0", 0), pr.cnt.get("qso1", 0)
        def final_wait(e):
            pr._wait("qso0", t0)
            pr._wait("qso1", t1)
        pr.section("POOL", final_wait)
    except _StopBuild:
        pass
    return nc


def _stage(inputs):
    x = np.asarray(inputs["L0_attn_x"], dtype=np.float32)
    wqkv = _stage_stripes(np.asarray(inputs["L0_attn_w_qkv"], np.float32), 24)
    wo = np.ascontiguousarray(
        np.asarray(inputs["L0_attn_w_o"], np.float32).T).reshape(16, 128, HID)
    wg = _stage_stripes(np.asarray(inputs["L0_ffn_w_gate"], np.float32), NIC)
    wu = _stage_stripes(np.asarray(inputs["L0_ffn_w_up"], np.float32), NIC)
    wdT = np.ascontiguousarray(np.asarray(inputs["L0_ffn_w_down"],
                                          np.float32).T)
    perm = np.empty(IFF, dtype=np.int64)
    for c in range(NIC):
        for p in range(128):
            perm[c * 128 + p] = (p % 64) * 128 + 2 * c + p // 64
    wd = np.ascontiguousarray(wdT[perm].reshape(NIC, 128, HID))
    cos, sin = _rope_tables()
    sinf = sin.copy()
    sinf[:64, :] = -sinf[:64, :]
    qn_g = np.asarray(inputs["L0_attn_qn_g"], np.float32).reshape(128, 1)
    qn_b = np.asarray(inputs["L0_attn_qn_b"], np.float32).reshape(128, 1)
    kn_g = np.asarray(inputs["L0_attn_kn_g"], np.float32).reshape(128, 1)
    kn_b = np.asarray(inputs["L0_attn_kn_b"], np.float32).reshape(128, 1)

    in_maps = []
    meta = []
    for c in range(8):
        bc = c // 2
        blocks = (0, 3) if c % 2 == 0 else (1, 2)
        own = np.concatenate([np.arange(b * 512, (b + 1) * 512)
                              for b in blocks])
        xk = np.ascontiguousarray(x[bc].T)
        xq = np.ascontiguousarray(x[bc].T[:, own])
        cosq = np.ascontiguousarray(cos[:, own])
        sinq = np.ascontiguousarray(sinf[:, own])
        mask = np.zeros((16, 128, 512), dtype=ml_dtypes.bfloat16)
        for (qb, kt) in MASKED:
            i = MASK_IDX[(qb, kt)]
            q0 = blocks[qb] * 512
            kk = kt * 128 + np.arange(128)[:, None]
            qq = q0 + np.arange(512)[None, :]
            mask[i] = (kk <= qq).astype(np.float32).astype(ml_dtypes.bfloat16)
        in_maps.append(dict(
            xq=xq, xk=xk, cosq=cosq, sinq=sinq,
            cosk=np.ascontiguousarray(cos), sink=np.ascontiguousarray(sinf),
            mask01=mask, qn_g=qn_g, qn_b=qn_b, kn_g=kn_g, kn_b=kn_b,
            wqkv=wqkv, wo=wo, wg=wg, wu=wu, wd=wd))
        meta.append((bc, blocks))
    return in_maps, meta


def kernel(**inputs):
    in_maps, meta = _stage(inputs)
    nc = build_program()
    res = run_bass_kernel_spmd(nc, in_maps, core_ids=list(range(8)))
    out = np.empty((B, S, HID), dtype=np.float32)
    for c in range(8):
        bc, blocks = meta[c]
        y = np.asarray(res.results[c]["y"])
        for qi, b in enumerate(blocks):
            out[bc, b * 512:(b + 1) * 512, :] = y[:, qi * 512:(qi + 1) * 512].T
    return out
